# version 06
# baseline (speedup 1.0000x reference)
from contextlib import ExitStack

import numpy as np
import concourse.bass as bass
import concourse.mybir as mybir
from concourse.bass_utils import run_bass_kernel_spmd

F32 = mybir.dt.float32
I32 = mybir.dt.int32
AF = mybir.ActivationFunctionType
ALU = mybir.AluOpType

B, L, V, D, FF, DEPTH = 2, 8192, 4096, 512, 2048, 4
NCORES, QPER = 8, 4
TC = L // QPER
TT = 512
NTOK = TC // TT
KV = V // 128
GD = D // 128
KF = FF // 128
PI = float(np.pi)
TWO_PI = float(2 * np.pi)
INV_2PI = float(1.0 / (2 * np.pi))
PI_LO = float(np.float32(3.1415925))
LN_EPS = 1e-5

ENGINES = ("tensor", "vector", "scalar", "gpsimd", "sync")

DEBUG_TAPS = ()
STAGE = 4
PHASE_MARKS = []


class Prog:

    SELF = {"vector": "V", "scalar": "S", "gpsimd": "G"}

    def __init__(self, nc, stack, sem_names):
        self.nc = nc
        self.q = {e: [] for e in ENGINES}
        self.sem = {n: stack.enter_context(nc.semaphore(n)) for n in sem_names}
        self.cnt = {n: 0 for n in sem_names}

    def run(self, eng, fn, inc=None, n=1):
        selfsem = self.SELF.get(eng)
        if selfsem is not None and inc is None:
            inc, n = selfsem, 1
        if inc is not None:
            if selfsem is not None and inc == selfsem:
                prev = self.cnt[inc]
                if prev:
                    self.q[eng].append(("w", self.sem[inc], prev))
            self.cnt[inc] += n
            sem = self.sem[inc]
            self.q[eng].append(lambda fn=fn, sem=sem, n=n: fn().then_inc(sem, n))
            return self.cnt[inc]
        self.q[eng].append(fn)
        return None

    def wait(self, eng, name, val=None):
        v = self.cnt[name] if val is None else val
        if v <= 0:
            return
        self.q[eng].append(("w", self.sem[name], v))

    def emit(self):
        with self.nc.Block() as blk:
            for e in ENGINES:
                def body(engobj, e=e):
                    for item in self.q[e]:
                        if isinstance(item, tuple):
                            engobj.wait_ge(item[1], item[2])
                        else:
                            item()
                getattr(blk, e)(body)


def stage_inputs(inp):
    f = np.ascontiguousarray
    x = np.asarray(inp["L0_attn_x"], dtype=np.float32)

    def ch(v):
        return f(np.asarray(v, np.float32).reshape(GD, 128).T)

    def chl(v):
        return f(np.asarray(v, np.float32).reshape(DEPTH, GD, 128).transpose(2, 0, 1))

    def vw(v, g):
        return f(np.asarray(v, np.float32).reshape(g, 128).T)

    common = {
        "winT": f(np.asarray(inp["L0_attn_Win"], np.float32).T),
        "w1T": f(np.asarray(inp["L0_attn_W1"], np.float32).transpose(0, 2, 1)),
        "w2T": f(np.asarray(inp["L0_attn_W2"], np.float32).transpose(0, 2, 1)),
        "woutT": f(np.asarray(inp["L0_attn_Wout"], np.float32).T),
        "aT": f(np.asarray(inp["L0_ffn_A"], np.float32).T),
        "bin": ch(inp["L0_attn_bin"]),
        "b1": f(np.asarray(inp["L0_attn_b1"], np.float32).reshape(DEPTH, KF, 128).transpose(2, 0, 1)),
        "b2": chl(inp["L0_attn_b2"]),
        "ln1w": chl(inp["L0_attn_ln1_w"]), "ln1b": chl(inp["L0_attn_ln1_b"]),
        "ln2w": chl(inp["L0_attn_ln2_w"]), "ln2b": chl(inp["L0_attn_ln2_b"]),
        "phzr": chl(inp["L0_attn_phz_re"]), "phzi": chl(inp["L0_attn_phz_im"]),
        "lcr": chl(inp["L0_attn_lc_re"]), "lci": chl(inp["L0_attn_lc_im"]),
        "scl": chl(inp["L0_attn_sc_lin"]),
        "lnlw": ch(inp["L0_attn_lnl_w"]), "lnlb": ch(inp["L0_attn_lnl_b"]),
        "bout": vw(inp["L0_attn_bout"], KV),
        "bA": vw(inp["L0_ffn_b"], KV),
        "iota": f(np.broadcast_to(np.arange(TC, dtype=np.float32), (128, TC)).copy()),
    }
    in_maps = []
    for c in range(NCORES):
        b, q = c // QPER, c % QPER
        xT = f(x[b, q * TC:(q + 1) * TC, :].T)
        sel = np.zeros((128, 7), np.float32)
        for j in range(7):
            if QPER * b <= j < QPER * b + q:
                sel[:, j] = 1.0
        in_maps.append(dict(common, xT=xT, sel=sel))
    return in_maps


def build_program(taps=(), stage=4):
    PHASE_MARKS.clear()
    nc = bass.Bass("TRN2", target_bir_lowering=False)
    taps = set(taps)
    nv, ns, ntn, ng = nc.vector, nc.scalar, nc.tensor, nc.gpsimd

    dd = {}
    for name, shape in [
        ("xT", [V, TC]), ("winT", [V, D]), ("w1T", [DEPTH, D, FF]), ("w2T", [DEPTH, FF, D]),
        ("woutT", [D, V]), ("aT", [V, V]), ("bin", [128, GD]),
        ("b1", [128, DEPTH, KF]), ("b2", [128, DEPTH, GD]),
        ("ln1w", [128, DEPTH, GD]), ("ln1b", [128, DEPTH, GD]),
        ("ln2w", [128, DEPTH, GD]), ("ln2b", [128, DEPTH, GD]),
        ("phzr", [128, DEPTH, GD]), ("phzi", [128, DEPTH, GD]),
        ("lcr", [128, DEPTH, GD]), ("lci", [128, DEPTH, GD]),
        ("scl", [128, DEPTH, GD]), ("lnlw", [128, GD]), ("lnlb", [128, GD]),
        ("bout", [128, KV]), ("bA", [128, KV]), ("iota", [128, TC]), ("sel", [128, 7]),
    ]:
        dd[name] = nc.dram_tensor(name, shape, F32, kind="ExternalInput").ap()

    outT = nc.dram_tensor("outT", [V, TC], F32, kind="ExternalOutput").ap() if stage >= 4 else None
    d_tap = {}
    for t in taps:
        shp = {"h0": [128, GD * TC], "y0": [128, GD * TC], "conv0": [128, GD * TC],
               "h1": [128, GD * TC], "hfin": [128, GD * TC], "hln": [128, GD * TC],
               "x1": [V, TC]}[t]
        d_tap[t] = nc.dram_tensor("tap_" + t, shp, F32, kind="ExternalOutput").ap()

    ag_in = nc.dram_tensor("ag_in", [1, 1024], F32)
    ag_out = nc.dram_tensor("ag_out", [NCORES, 1024], F32)
    bounce = nc.dram_tensor("bounce", [1, V], F32)

    st = ExitStack()
    with st:
        def sbt(name, shape, dt=F32):
            return st.enter_context(nc.sbuf_tensor(name, shape, dt))

        h = sbt("h", [128, GD, TC])
        y = sbt("y", [128, GD, TC])
        tabs = sbt("tabs", [128, 2, TC])
        u = sbt("u", [128, 2, TC])
        bcst = sbt("bcst", [128, 2, TC])
        y1 = sbt("y1", [128, 2, TT])
        w1t = sbt("w1t", [128, GD, FF])
        w2b = sbt("w2b", [128, 3, TT])
        xb = sbt("xb", [128, 3, TT])
        wtb = sbt("wtb", [128, 3, TT])
        ab4 = sbt("ab4", [128, 2, 4, TT])
        xeb = sbt("xeb", [128, 3, TT])
        rowb = sbt("rowb", [1, 2, TT])
        iot = sbt("iot", [128, TC])
        onec = sbt("onec", [128, 1])
        zerc = sbt("zerc", [128, 1])
        nonec = sbt("nonec", [128, 1])
        oner = sbt("oner", [1, 128])
        binS = sbt("binS", [128, GD])
        b1S = sbt("b1S", [128, DEPTH, KF])
        b2S = sbt("b2S", [128, DEPTH, GD])
        prm = {k: sbt(k + "S", [128, DEPTH, GD]) for k in
               ["ln1w", "ln1b", "ln2w", "ln2b", "phzr", "phzi", "lcr", "lci", "scl"]}
        lnlwS = sbt("lnlwS", [128, GD])
        lnlbS = sbt("lnlbS", [128, GD])
        boutS = sbt("boutS", [128, KV])
        bAS = sbt("bAS", [128, KV])
        selS = sbt("selS", [128, 7])
        rsS = sbt("rsS", [128, GD])
        rsAS = sbt("rsAS", [128, KV])
        pp = {k: sbt("pp_" + k, [128, DEPTH, GD]) for k in
              ["rho", "lnrho", "absth", "sgn", "eR", "eI", "p2r", "p2i",
               "ln2047", "th2047"]}
        t16a = sbt("t16a", [128, DEPTH * GD])
        t16b = sbt("t16b", [128, DEPTH * GD])
        t16c = sbt("t16c", [128, DEPTH * GD])
        t16d = sbt("t16d", [128, DEPTH * GD])
        fcell = sbt("fcell", [1, 8])
        ES = sbt("ES", [128, 8])
        EA = sbt("EA", [128, 7, 8])
        CCs = sbt("CCs", [128, 8])
        WIs = sbt("WIs", [128, 8])
        c8a = sbt("c8a", [128, 8])
        c8b = sbt("c8b", [128, 8])

        ps_a = st.enter_context(nc.psum_tensor("ps_a", [128, 4, TT], F32))
        ps_b = st.enter_context(nc.psum_tensor("ps_b", [128, 2, TT], F32))
        ps_st = st.enter_context(nc.psum_tensor("ps_st", [1, 2, TT], F32))

        p = Prog(nc, st, ["T", "V", "S", "G", "GDm",
                          "L0", "L1", "L2", "L3", "L4", "L5",
                          "E0", "E1", "E2", "O0", "O1"])

        def Vr(fn, inc="V", n=1):
            return p.run("vector", fn, None, 1)

        def Sc(fn, inc="S", n=1):
            return p.run("scalar", fn, None, 1)

        def Tr(fn, inc=None, n=1):
            return p.run("tensor", fn, inc, n)

        def Gp(fn, inc=None, n=1):
            if inc is None:
                return p.run("gpsimd", fn, None, 1)
            return p.run("gpsimd", fn, inc, n)

        def Sy(fn, inc=None, n=1):
            return p.run("sync", fn, inc, n)

        def mark(name):
            PHASE_MARKS.append((name, dict(p.cnt)))

        class Ring:

            def __init__(self, eng, names):
                self.eng, self.names = eng, names
                self.i = 0

            def load_group(self, pairs):
                nm = self.names[self.i % len(self.names)]
                self.i += 1
                p.wait(self.eng, nm, p.cnt[nm])
                for dst_ap, src_ap in pairs:
                    if self.eng == "sync":
                        p.run("sync", lambda dst_ap=dst_ap, src_ap=src_ap:
                              nc.sync.dma_start(out=dst_ap, in_=src_ap), nm, 16)
                    else:
                        p.run("gpsimd", lambda dst_ap=dst_ap, src_ap=src_ap:
                              ng.dma_start(out=dst_ap, in_=src_ap), nm, 16)
                return (nm, p.cnt[nm])

            def load(self, dst_ap, src_ap, slow=False):
                nm = self.names[self.i % len(self.names)]
                self.i += 1
                p.wait(self.eng, nm, p.cnt[nm])
                if self.eng == "sync":
                    p.run("sync", lambda dst_ap=dst_ap, src_ap=src_ap:
                          nc.sync.dma_start(out=dst_ap, in_=src_ap), nm, 16)
                else:
                    p.run("gpsimd", lambda dst_ap=dst_ap, src_ap=src_ap, slow=slow:
                          ng.dma_start(out=dst_ap, in_=src_ap,
                                       allow_slow_non_contiguous=slow), nm, 16)
                return (nm, p.cnt[nm])

            def wait(self, eng, mark):
                p.wait(eng, mark[0], mark[1])

        sring = Ring("sync", ["L0", "L1", "L2", "L3", "L4", "L5"])
        ering = Ring("gpsimd", ["E0", "E1", "E2"])

        def gp_load(dst_ap, src_ap, slow=False):
            p.wait("gpsimd", "GDm", p.cnt["GDm"])
            return Gp(lambda dst_ap=dst_ap, src_ap=src_ap, slow=slow:
                      ng.dma_start(out=dst_ap, in_=src_ap,
                                   allow_slow_non_contiguous=slow), inc="GDm", n=16)

        def tap_out(name, src):
            m = gp_load(d_tap[name], src)
            p.wait("vector", "GDm", m)
            p.wait("scalar", "GDm", m)
            p.wait("tensor", "GDm", m)

        def mm_fence(bank_ap):
            Tr(lambda bank_ap=bank_ap: ntn.matmul(
                bank_ap, zerc[:, 0:1], onec[:, 0:1],
                start=False, stop=True, skip_group_check=True), inc=None)
            return Tr(lambda bank_ap=bank_ap: ntn.matmul(
                bank_ap, zerc[:, 0:1], onec[:, 0:1],
                start=False, stop=True, skip_group_check=True), inc="T")

        def range_reduce(dst, src, iscr):
            ii = iscr.bitcast(I32)
            Vr(lambda: nv.tensor_single_scalar(dst, src, INV_2PI, ALU.mult), inc=None)
            Vr(lambda: nv.tensor_copy(ii, dst), inc=None)
            Vr(lambda: nv.tensor_copy(iscr, ii), inc=None)
            Vr(lambda: nv.scalar_tensor_tensor(dst, iscr, -TWO_PI, src, ALU.mult, ALU.add), inc=None)
            Vr(lambda: nv.tensor_single_scalar(iscr, dst, PI, ALU.is_gt), inc=None)
            Vr(lambda: nv.scalar_tensor_tensor(dst, iscr, -TWO_PI, dst, ALU.mult, ALU.add), inc=None)
            Vr(lambda: nv.tensor_single_scalar(iscr, dst, -PI, ALU.is_lt), inc=None)
            Vr(lambda: nv.scalar_tensor_tensor(dst, iscr, TWO_PI, dst, ALU.mult, ALU.add), inc=None)
            Vr(lambda: nv.tensor_single_scalar(dst, dst, PI_LO, ALU.min), inc=None)
            Vr(lambda: nv.tensor_single_scalar(dst, dst, -PI_LO, ALU.max), inc=None)

        def cos_from_red(dst, red, mscr):
            Vr(lambda: nv.tensor_scalar(mscr, red, -1.0, PI / 2, ALU.mult, ALU.add), inc=None)
            Vr(lambda: nv.tensor_single_scalar(dst, mscr, PI, ALU.is_gt), inc=None)
            Vr(lambda: nv.scalar_tensor_tensor(mscr, dst, -TWO_PI, mscr, ALU.mult, ALU.add), inc=None)
            Vr(lambda: nv.tensor_single_scalar(mscr, mscr, PI_LO, ALU.min), inc=None)
            Vr(lambda: nv.tensor_single_scalar(mscr, mscr, -PI_LO, ALU.max))
            p.wait("scalar", "V")
            Sc(lambda: ns.activation(dst, mscr, AF.Sin))

        def build_tables(lay, g, with_rho):
            ath = pp["absth"][:, lay, g:g + 1].broadcast_to([128, TC])
            agl = tabs[:, 0, :]
            Vr(lambda agl=agl, ath=ath: nv.tensor_tensor(agl, iot[:], ath, ALU.mult), inc=None)
            if with_rho:
                th47 = pp["th2047"][:, lay, g:g + 1].broadcast_to([128, TC])
                Vr(lambda agl=agl, th47=th47: nv.scalar_tensor_tensor(
                    agl, agl, -1.0, th47, ALU.mult, ALU.add), inc=None)
            range_reduce(u[:, 0, :], agl, u[:, 1, :])
            if with_rho:
                lnr = pp["lnrho"][:, lay, g:g + 1].broadcast_to([128, TC])
                l47 = pp["ln2047"][:, lay, g:g + 1].broadcast_to([128, TC])
                Vr(lambda lnr=lnr: nv.tensor_tensor(tabs[:, 1, :], iot[:], lnr, ALU.mult), inc=None)
                Vr(lambda l47=l47: nv.scalar_tensor_tensor(
                    tabs[:, 1, :], tabs[:, 1, :], -1.0, l47, ALU.mult, ALU.add))
                p.wait("scalar", "V")
                Sc(lambda: ns.activation(tabs[:, 1, :], tabs[:, 1, :], AF.Exp))
                p.wait("vector", "S")
            cos_from_red(tabs[:, 0, :], u[:, 0, :], u[:, 1, :])
            p.wait("vector", "S")
            if with_rho:
                Vr(lambda: nv.tensor_tensor(tabs[:, 0, :], tabs[:, 0, :], tabs[:, 1, :], ALU.mult), inc=None)
                Vr(lambda: nv.tensor_copy(u[:, 1, :], u[:, 0, :]))
                p.wait("scalar", "V")
                Sc(lambda: ns.activation(u[:, 1, :], u[:, 1, :], AF.Sin))
                p.wait("vector", "S")
                Vr(lambda: nv.tensor_tensor(tabs[:, 1, :], u[:, 1, :], tabs[:, 1, :], ALU.mult), inc=None)
            else:
                Vr(lambda: nv.tensor_copy(u[:, 1, :], u[:, 0, :]))
                p.wait("scalar", "V")
                Sc(lambda: ns.activation(tabs[:, 1, :], u[:, 1, :], AF.Sin))
                p.wait("vector", "S")

        def ln_stats_and_bcast(tok, denom, extra_wait_v=None):
            mu, rr = rowb[:, 0, :], rowb[:, 1, :]
            p.wait("vector", "T")
            Vr(lambda: nv.tensor_single_scalar(mu, ps_st[:, 0, :], 1.0 / denom, ALU.mult), inc=None)
            Vr(lambda: nv.tensor_tensor(rr, mu, mu, ALU.mult), inc=None)
            Vr(lambda: nv.scalar_tensor_tensor(rr, ps_st[:, 1, :], 1.0 / denom, rr,
                                               ALU.mult, ALU.subtract))
            st_free = p.cnt["V"]
            Vr(lambda: nv.tensor_single_scalar(rr, rr, LN_EPS, ALU.add))
            p.wait("scalar", "V")
            Sc(lambda: ns.activation(rr, rr, AF.Sqrt))
            p.wait("vector", "S")
            Vr(lambda: nv.reciprocal(rr, rr), inc=None)
            Vr(lambda: nv.tensor_tensor(mu, rr, mu, ALU.mult))
            rows_done = p.cnt["V"]
            p.wait("tensor", "V", rows_done)
            if extra_wait_v is not None:
                p.wait("tensor", "V", extra_wait_v)
            p.wait("tensor", "S")
            Tr(lambda: ntn.matmul(ps_b[:, 0, :], oner[:], rr, start=True, stop=True), inc=None)
            Tr(lambda: ntn.matmul(ps_b[:, 1, :], oner[:], mu, start=True, stop=True), inc=None)
            mm_fence(ps_b[0:1, 0, 0:1])
            p.wait("vector", "T")
            sl = slice(tok * TT, (tok + 1) * TT)
            Vr(lambda sl=sl: nv.tensor_copy(bcst[:, 0, sl], ps_b[:, 0, :]), inc=None)
            Vr(lambda sl=sl: nv.tensor_copy(bcst[:, 1, sl], ps_b[:, 1, :]))
            return p.cnt["V"], st_free

        Vr(lambda: nv.memset(onec[:], 1.0), inc=None)
        Vr(lambda: nv.memset(zerc[:], 0.0), inc=None)
        Vr(lambda: nv.memset(fcell[:], 1.0), inc=None)
        Vr(lambda: nv.memset(nonec[:], -1.0), inc=None)
        Vr(lambda: nv.memset(oner[:], 1.0))
        const_v = p.cnt["V"]
        p.wait("tensor", "V", const_v)
        p.wait("scalar", "V", const_v)
        p.wait("gpsimd", "V", const_v)
        for dst, src in [(iot, dd["iota"]), (binS, dd["bin"]), (b1S, dd["b1"]),
                         (b2S, dd["b2"]), (lnlwS, dd["lnlw"]), (lnlbS, dd["lnlb"]),
                         (boutS, dd["bout"]), (bAS, dd["bA"]), (selS, dd["sel"])] + \
                        [(prm[k], dd[k]) for k in prm]:
            gp_load(dst[:], src)
        ld_params = p.cnt["GDm"]

        phr, phi = prm["phzr"][:, :, :], prm["phzi"][:, :, :]
        rho16, lnrho16 = pp["rho"][:, :, :], pp["lnrho"][:, :, :]
        absth16, sgn16 = pp["absth"][:, :, :], pp["sgn"][:, :, :]
        eR16, eI16 = pp["eR"][:, :, :], pp["eI"][:, :, :]
        p2r16, p2i16 = pp["p2r"][:, :, :], pp["p2i"][:, :, :]
        ln2047_16, th2047_16 = pp["ln2047"][:, :, :], pp["th2047"][:, :, :]
        A16, B16, C16, D16 = t16a[:], t16b[:], t16c[:], t16d[:]

        p.wait("vector", "GDm", ld_params)
        Vr(lambda: nv.tensor_tensor(A16, phr, phr, ALU.mult), inc=None)
        Vr(lambda: nv.tensor_tensor(B16, phi, phi, ALU.mult), inc=None)
        Vr(lambda: nv.tensor_tensor(A16, A16, B16, ALU.add), inc=None)
        Vr(lambda: nv.tensor_single_scalar(A16, A16, 1e-30, ALU.max))
        p.wait("scalar", "V")
        Sc(lambda: ns.activation(B16, A16, AF.Sqrt))
        p.wait("vector", "S")
        Vr(lambda: nv.reciprocal(C16, B16), inc=None)
        Vr(lambda: nv.tensor_tensor(eR16, phr, C16, ALU.mult), inc=None)
        Vr(lambda: nv.tensor_tensor(eI16, phi, C16, ALU.mult))
        p.wait("scalar", "V")
        Sc(lambda: ns.activation(rho16, B16, AF.Tanh), inc=None)
        Sc(lambda: ns.activation(lnrho16, rho16, AF.Ln), inc=None)
        Sc(lambda: ns.activation(A16, eR16, AF.Abs), inc=None)
        Sc(lambda: ns.activation(B16, eI16, AF.Abs))
        p.wait("vector", "S")
        Vr(lambda: nv.tensor_single_scalar(ln2047_16, lnrho16, 2047.0, ALU.mult), inc=None)
        Vr(lambda: nv.tensor_tensor(C16, A16, B16, ALU.min), inc=None)
        Vr(lambda: nv.tensor_tensor(D16, A16, B16, ALU.max), inc=None)
        Vr(lambda: nv.reciprocal(D16, D16), inc=None)
        Vr(lambda: nv.tensor_tensor(C16, C16, D16, ALU.mult))
        p.wait("scalar", "V")
        Sc(lambda: ns.activation(D16, C16, AF.Arctan))
        p.wait("scalar", "S")
        Sc(lambda: ns.activation(A16, D16, AF.Sin))
        p.wait("vector", "S")
        Vr(lambda: nv.tensor_tensor(B16, A16, A16, ALU.mult), inc=None)
        Vr(lambda: nv.tensor_scalar(B16, B16, -1.0, 1.0, ALU.mult, ALU.add))
        p.wait("scalar", "V")
        Sc(lambda: ns.activation(B16, B16, AF.Sqrt))
        p.wait("vector", "S")
        Vr(lambda: nv.tensor_tensor(C16, C16, B16, ALU.mult), inc=None)
        Vr(lambda: nv.tensor_tensor(C16, C16, A16, ALU.subtract), inc=None)
        Vr(lambda: nv.tensor_tensor(C16, C16, B16, ALU.mult), inc=None)
        Vr(lambda: nv.tensor_tensor(D16, D16, C16, ALU.add), inc=None)
        Vr(lambda: nv.tensor_tensor(A16, eI16, eI16, ALU.mult), inc=None)
        Vr(lambda: nv.tensor_tensor(B16, eR16, eR16, ALU.mult), inc=None)
        Vr(lambda: nv.tensor_tensor(A16, A16, B16, ALU.is_gt), inc=None)
        Vr(lambda: nv.tensor_scalar(B16, D16, -2.0, PI / 2, ALU.mult, ALU.add), inc=None)
        Vr(lambda: nv.tensor_tensor(A16, A16, B16, ALU.mult), inc=None)
        Vr(lambda: nv.tensor_tensor(D16, D16, A16, ALU.add), inc=None)
        Vr(lambda: nv.tensor_single_scalar(A16, eR16, 0.0, ALU.is_lt), inc=None)
        Vr(lambda: nv.tensor_scalar(B16, D16, -2.0, PI, ALU.mult, ALU.add), inc=None)
        Vr(lambda: nv.tensor_tensor(A16, A16, B16, ALU.mult), inc=None)
        Vr(lambda: nv.tensor_tensor(absth16, D16, A16, ALU.add), inc=None)
        Vr(lambda: nv.tensor_single_scalar(A16, eI16, 0.0, ALU.is_ge), inc=None)
        Vr(lambda: nv.tensor_scalar(sgn16, A16, 2.0, -1.0, ALU.mult, ALU.add), inc=None)
        Vr(lambda: nv.tensor_single_scalar(th2047_16, absth16, 2047.0, ALU.mult), inc=None)
        Vr(lambda: nv.tensor_single_scalar(A16, lnrho16, 2048.0, ALU.mult))
        p.wait("scalar", "V")
        Sc(lambda: ns.activation(A16, A16, AF.Exp))
        p.wait("vector", "S")
        Vr(lambda: nv.tensor_single_scalar(B16, absth16, 2048.0, ALU.mult), inc=None)
        range_reduce(C16, B16, D16)
        Vr(lambda: nv.tensor_copy(D16, C16))
        p.wait("scalar", "V")
        Sc(lambda: ns.activation(B16, C16, AF.Sin))
        p.wait("vector", "S")
        Vr(lambda: nv.tensor_tensor(p2i16, A16, B16, ALU.mult), inc=None)
        Vr(lambda: nv.tensor_tensor(p2i16, p2i16, sgn16, ALU.mult), inc=None)
        cos_from_red(B16, D16, C16)
        p.wait("vector", "S")
        Vr(lambda: nv.tensor_tensor(p2r16, A16, B16, ALU.mult))
        mark("P1")

        tmark_last = 0
        epi_done_v = 0
        stfree_v = 0
        tmarks = [0, 0]
        smarks = [0, 0]
        for tok in range(NTOK):
            for k in range(KV):
                sl = k % 3
                p.wait("sync", "T", tmarks[0])
                p.wait("sync", "S", smarks[0])
                mk = sring.load_group([
                    (xb[:, sl, :], dd["xT"][k * 128:(k + 1) * 128, tok * TT:(tok + 1) * TT]),
                    (wtb[:, sl, :], dd["winT"][k * 128:(k + 1) * 128, :])])
                sring.wait("scalar", mk)
                p.wait("scalar", "T", tmarks[0])
                s_sq = Sc(lambda sl=sl: ns.activation(xeb[:, sl, :], xb[:, sl, :], AF.Square))
                smarks = [smarks[1], s_sq]
                sring.wait("tensor", mk)
                if k == 0:
                    p.wait("tensor", "V", epi_done_v)
                    p.wait("tensor", "V", stfree_v)
                for dm in range(GD):
                    Tr(lambda k=k, sl=sl, dm=dm: ntn.matmul(
                        ps_a[:, dm, :], wtb[:, sl, dm * 128:(dm + 1) * 128], xb[:, sl, :],
                        start=(k == 0), stop=(k == KV - 1)), inc=None)
                Tr(lambda k=k, sl=sl: ntn.matmul(
                    ps_st[:, 0, :], onec[:], xb[:, sl, :],
                    start=(k == 0), stop=(k == KV - 1)), inc=None)
                p.wait("tensor", "S", s_sq)
                if tok == 0:
                    Tr(lambda k=k, sl=sl: ntn.matmul(
                        ps_b[0:1, 0, 0:D], nonec[:], wtb[:, sl, :],
                        start=(k == 0), stop=(k == KV - 1)), inc=None)
                tm = Tr(lambda k=k, sl=sl: ntn.matmul(
                    ps_st[:, 1, :], onec[:], xeb[:, sl, :],
                    start=(k == 0), stop=(k == KV - 1)), inc="T")
                tmarks = [tmarks[1], tm]
            mm_fence(ps_st[0:1, 0, 0:1])
            if tok == 0:
                p.wait("vector", "T")
                rs_v = Vr(lambda: nv.tensor_copy(rowb[:, 1, 0:TT], ps_b[0:1, 0, 0:D]))
                p.wait("gpsimd", "V", rs_v)
                g1 = gp_load(bounce.ap()[0:1, 0:D], rowb[0:1, 1, 0:D])
                p.wait("gpsimd", "GDm", g1)
                g2 = gp_load(rsS[:], bounce.ap()[0, 0:D].rearrange("(g q) -> q g", q=128), slow=True)
                p.wait("vector", "GDm", g1)
            bc_v, stfree_v = ln_stats_and_bcast(tok, float(V), extra_wait_v=(rs_v if tok == 0 else None))
            if tok == 0:
                p.wait("vector", "GDm", g2)
            sl_t = slice(tok * TT, (tok + 1) * TT)
            for dm in range(GD):
                hsl = h[:, dm, sl_t]
                Vr(lambda dm=dm, hsl=hsl, sl_t=sl_t: nv.tensor_tensor(
                    hsl, ps_a[:, dm, :], bcst[:, 0, sl_t], ALU.mult), inc=None)
                Vr(lambda dm=dm, hsl=hsl, sl_t=sl_t: nv.scalar_tensor_tensor(
                    hsl, bcst[:, 1, sl_t], rsS[:, dm:dm + 1], hsl, ALU.mult, ALU.add), inc=None)
                Vr(lambda dm=dm, hsl=hsl: nv.tensor_scalar_add(hsl, hsl, binS[:, dm:dm + 1]))
            epi_done_v = p.cnt["V"]

        mark("P2_win")
        if "h0" in taps:
            p.wait("gpsimd", "V")
            tap_out("h0", h[:, :, :])

        hf = h[:, :, :]
        yf = y[:, :, :]

        def inner_ln(lay, wkey, bkey, act, y_free_t=0):
            nonlocal stfree_v, epi_done_v
            p.wait("scalar", "V")
            p.wait("scalar", "T", y_free_t)
            sq_s = Sc(lambda: ns.activation(yf, hf, AF.Square))
            for tok in range(NTOK):
                p.wait("tensor", "S", sq_s)
                p.wait("tensor", "V", stfree_v)
                for kt in range(GD):
                    Tr(lambda kt=kt, tok=tok: ntn.matmul(
                        ps_st[:, 0, :], onec[:], h[:, kt, tok * TT:(tok + 1) * TT],
                        start=(kt == 0), stop=(kt == GD - 1)), inc=None)
                for kt in range(GD):
                    Tr(lambda kt=kt, tok=tok: ntn.matmul(
                        ps_st[:, 1, :], onec[:], y[:, kt, tok * TT:(tok + 1) * TT],
                        start=(kt == 0), stop=(kt == GD - 1)), inc=None)
                mm_fence(ps_st[0:1, 0, 0:1])
                bc_v, stfree_v = ln_stats_and_bcast(tok, float(D))
            rb = bcst[:, 0:1, :].broadcast_to([128, GD, TC])
            mb = bcst[:, 1:2, :].broadcast_to([128, GD, TC])
            Vr(lambda: nv.tensor_tensor(yf, hf, rb, ALU.mult), inc=None)
            Vr(lambda: nv.tensor_tensor(yf, yf, mb, ALU.subtract))
            if act == AF.Copy:
                for g in range(GD):
                    wcol = prm[wkey][:, lay, g:g + 1]
                    bb = prm[bkey][:, lay, g:g + 1].broadcast_to([128, TC])
                    Vr(lambda g=g, wcol=wcol, bb=bb: nv.scalar_tensor_tensor(
                        y[:, g, :], y[:, g, :], wcol, bb, ALU.mult, ALU.add))
                return ("V", p.cnt["V"])
            p.wait("scalar", "V")
            for g in range(GD):
                Sc(lambda g=g: ns.activation(
                    y[:, g, :], y[:, g, :], act,
                    bias=prm[bkey][:, lay, g:g + 1], scale=prm[wkey][:, lay, g:g + 1]))
            return ("S", p.cnt["S"])

        prev_mm1_t = 0
        w2marks = [0, 0]
        w2i = 0
        for lay in range(DEPTH if stage >= 2 else 0):
            p.wait("sync", "T", prev_mm1_t)
            w1_ld = sring.load(w1t[:, :, :], dd["w1T"][lay].rearrange("(kt q) f -> q kt f", q=128))

            s_ln1 = inner_ln(lay, "ln1w", "ln1b", AF.Silu, y_free_t=prev_mm1_t)
            mark(f"L{lay}_ln1")
            if lay == 0 and "y0" in taps:
                p.wait("gpsimd", s_ln1[0], s_ln1[1])
                tap_out("y0", yf)

            p.wait("vector", s_ln1[0], s_ln1[1])
            for g in range(GD):
                build_tables(lay, g, with_rho=True)
                Vr(lambda g=g: nv.tensor_tensor(u[:, 0, :], tabs[:, 0, :], y[:, g, :], ALU.mult), inc=None)
                Vr(lambda g=g: nv.reduce_sum(ES[:, g:g + 1], u[:, 0, :], axis=mybir.AxisListType.X), inc=None)
                Vr(lambda g=g: nv.tensor_tensor(u[:, 1, :], tabs[:, 1, :], y[:, g, :], ALU.mult), inc=None)
                Vr(lambda g=g: nv.reduce_sum(ES[:, 4 + g:5 + g], u[:, 1, :], axis=mybir.AxisListType.X), inc=None)
            Vr(lambda lay=lay: nv.tensor_tensor(ES[:, 4:8], ES[:, 4:8], pp["sgn"][:, lay, :], ALU.mult))
            e_v = p.cnt["V"]

            p.wait("gpsimd", "V", e_v)
            gp1 = gp_load(ag_in.ap().rearrange("o (q k) -> (o q) k", q=128), ES[:])
            p.wait("gpsimd", "GDm", gp1)
            cc_m = Gp(lambda: ng.collective_compute(
                "AllGather", ALU.bypass, replica_groups=[list(range(NCORES))],
                ins=[ag_in.ap().opt()], outs=[ag_out.ap().opt()]))
            p.wait("gpsimd", "G", cc_m)
            gp2 = gp_load(EA[:, :, :], ag_out.ap()[0:7, :].rearrange("j (q k) -> q j k", q=128))

            p.wait("vector", "GDm", gp2)
            Vr(lambda lay=lay: nv.tensor_copy(CCs[:, 0:4], prm["lcr"][:, lay, :]), inc=None)
            Vr(lambda lay=lay: nv.tensor_copy(CCs[:, 4:8], prm["lci"][:, lay, :]), inc=None)
            for j in range(7):
                Cr, Ci = CCs[:, 0:4], CCs[:, 4:8]
                p2r_l, p2i_l = pp["p2r"][:, lay, :], pp["p2i"][:, lay, :]
                Vr(lambda Cr=Cr, p2r_l=p2r_l: nv.tensor_tensor(c8a[:, 0:4], Cr, p2r_l, ALU.mult), inc=None)
                Vr(lambda Ci=Ci, p2i_l=p2i_l: nv.tensor_tensor(c8b[:, 0:4], Ci, p2i_l, ALU.mult), inc=None)
                Vr(lambda: nv.tensor_tensor(c8a[:, 0:4], c8a[:, 0:4], c8b[:, 0:4], ALU.subtract), inc=None)
                Vr(lambda j=j: nv.tensor_tensor(c8a[:, 0:4], c8a[:, 0:4], EA[:, j, 0:4], ALU.add), inc=None)
                Vr(lambda Cr=Cr, p2i_l=p2i_l: nv.tensor_tensor(c8b[:, 0:4], Cr, p2i_l, ALU.mult), inc=None)
                Vr(lambda Ci=Ci, p2r_l=p2r_l: nv.tensor_tensor(c8b[:, 4:8], Ci, p2r_l, ALU.mult), inc=None)
                Vr(lambda: nv.tensor_tensor(c8a[:, 4:8], c8b[:, 0:4], c8b[:, 4:8], ALU.add), inc=None)
                Vr(lambda j=j: nv.tensor_tensor(c8a[:, 4:8], c8a[:, 4:8], EA[:, j, 4:8], ALU.add), inc=None)
                Vr(lambda: nv.tensor_tensor(c8b[:, 0:8], c8a[:, 0:8], CCs[:, 0:8], ALU.subtract), inc=None)
                Vr(lambda j=j: nv.scalar_tensor_tensor(
                    CCs[:, 0:8], c8b[:, 0:8], selS[:, j:j + 1], CCs[:, 0:8], ALU.mult, ALU.add), inc=None)
            eRl, eIl = pp["eR"][:, lay, :], pp["eI"][:, lay, :]
            Vr(lambda eRl=eRl: nv.tensor_tensor(WIs[:, 0:4], CCs[:, 0:4], eRl, ALU.mult), inc=None)
            Vr(lambda eIl=eIl: nv.tensor_tensor(c8a[:, 0:4], CCs[:, 4:8], eIl, ALU.mult), inc=None)
            Vr(lambda: nv.tensor_tensor(WIs[:, 0:4], WIs[:, 0:4], c8a[:, 0:4], ALU.subtract), inc=None)
            Vr(lambda eRl=eRl: nv.tensor_tensor(WIs[:, 4:8], CCs[:, 4:8], eRl, ALU.mult), inc=None)
            Vr(lambda eIl=eIl: nv.tensor_tensor(c8a[:, 4:8], CCs[:, 0:4], eIl, ALU.mult), inc=None)
            Vr(lambda: nv.tensor_tensor(WIs[:, 4:8], WIs[:, 4:8], c8a[:, 4:8], ALU.add), inc=None)
            Vr(lambda lay=lay: nv.tensor_tensor(WIs[:, 4:8], WIs[:, 4:8], pp["sgn"][:, lay, :], ALU.mult))

            for g in range(GD):
                build_tables(lay, g, with_rho=False)
                rho_b = pp["rho"][:, lay, g:g + 1].broadcast_to([128, TC])
                Vr(lambda g=g: nv.tensor_tensor(u[:, 0, :], tabs[:, 0, :], y[:, g, :], ALU.mult), inc=None)
                Vr(lambda g=g: nv.tensor_tensor(u[:, 1, :], tabs[:, 1, :], y[:, g, :], ALU.mult), inc=None)
                Vr(lambda g=g, rho_b=rho_b: nv.tensor_tensor_scan(
                    u[:, 0, :], rho_b, u[:, 0, :], WIs[:, g:g + 1], ALU.mult, ALU.add), inc=None)
                Vr(lambda g=g, rho_b=rho_b: nv.tensor_tensor_scan(
                    u[:, 1, :], rho_b, u[:, 1, :], WIs[:, 4 + g:5 + g], ALU.mult, ALU.subtract), inc=None)
                Vr(lambda g=g: nv.tensor_tensor(y[:, g, :], tabs[:, 0, :], u[:, 0, :], ALU.mult), inc=None)
                Vr(lambda g=g: nv.tensor_tensor(u[:, 0, :], tabs[:, 1, :], u[:, 1, :], ALU.mult), inc=None)
                Vr(lambda g=g: nv.tensor_tensor(y[:, g, :], y[:, g, :], u[:, 0, :], ALU.subtract), inc=None)
            Vr(lambda: nv.tensor_copy(c8a[:, 0:1], CCs[:, 0:1]))
            if lay == 0 and "conv0" in taps:
                p.wait("gpsimd", "V")
                tap_out("conv0", yf)

            for g in range(GD):
                Vr(lambda g=g, lay=lay: nv.scalar_tensor_tensor(
                    h[:, g, :], y[:, g, :], prm["scl"][:, lay, g:g + 1], h[:, g, :],
                    ALU.mult, ALU.add), inc=None)
            Vr(lambda: nv.tensor_copy(c8a[:, 0:1], CCs[:, 0:1]))

            mark(f"L{lay}_scan")
            s_ln2 = inner_ln(lay, "ln2w", "ln2b", AF.Copy)
            mark(f"L{lay}_ln2")

            evac_s = [0, 0, 0]
            mm2_t = 0
            hupd_v = p.cnt["V"]
            sring.wait("tensor", w1_ld)
            p.wait("tensor", s_ln2[0], s_ln2[1])
            for tok in range(NTOK):
                p.wait("tensor", "V", hupd_v)
                for fk in range(KF):
                    sl = w2i % 3
                    w2i += 1
                    p.wait("sync", "T", w2marks[0])
                    wl = sring.load(w2b[:, sl, :],
                                    dd["w2T"][lay][fk * 128:(fk + 1) * 128, :])
                    pb = fk % 2
                    p.wait("tensor", "S", evac_s[pb])
                    for kt in range(GD):
                        Tr(lambda lay=lay, fk=fk, kt=kt, pb=pb, tok=tok: ntn.matmul(
                            ps_b[:, pb, :], w1t[:, kt, fk * 128:(fk + 1) * 128],
                            y[:, kt, tok * TT:(tok + 1) * TT],
                            start=(kt == 0), stop=(kt == GD - 1)), inc=None)
                    mm1_t = mm_fence(ps_b[0:1, pb, 0:1])
                    p.wait("scalar", "T", mm1_t)
                    evac_s[pb] = Sc(lambda lay=lay, fk=fk, pb=pb: ns.activation(
                        y1[:, pb, :], ps_b[:, pb, :], AF.Silu,
                        bias=b1S[:, lay, fk:fk + 1], scale=1.0))
                    sring.wait("tensor", wl)
                    p.wait("tensor", "S", evac_s[pb])
                    for dm in range(GD):
                        Tr(lambda fk=fk, sl=sl, dm=dm, pb=pb: ntn.matmul(
                            ps_a[:, dm, :], w2b[:, sl, dm * 128:(dm + 1) * 128],
                            y1[:, pb, :],
                            start=(fk == 0), stop=(fk == KF - 1)),
                           inc=("T" if dm == GD - 1 else None))
                    w2marks = [w2marks[1], p.cnt["T"]]
                mm2_t = mm_fence(ps_a[0:1, 0, 0:1])
                p.wait("vector", "T", mm2_t)
                for dm in range(GD):
                    hsl = h[:, dm, tok * TT:(tok + 1) * TT]
                    Vr(lambda dm=dm, lay=lay, hsl=hsl: nv.scalar_tensor_tensor(
                        hsl, ps_a[:, dm, :], b2S[:, lay, dm:dm + 1], hsl,
                        ALU.add, ALU.add), inc=None)
                Vr(lambda: nv.tensor_copy(c8a[:, 0:1], CCs[:, 0:1]))
                hupd_v = p.cnt["V"]
            prev_mm1_t = p.cnt["T"]
            mark(f"L{lay}_ffn")

            if lay == 0 and "h1" in taps:
                p.wait("gpsimd", "V", hupd_v)
                tap_out("h1", hf)

        if stage >= 2 and "hfin" in taps:
            p.wait("gpsimd", "V")
            tap_out("hfin", hf)

        if stage >= 3:
            def final_ln():
                nonlocal stfree_v
                p.wait("scalar", "V")
                p.wait("scalar", "T", prev_mm1_t if stage >= 2 else 0)
                sq_s = Sc(lambda: ns.activation(yf, hf, AF.Square))
                for tok in range(NTOK):
                    p.wait("tensor", "S", sq_s)
                    p.wait("tensor", "V", stfree_v)
                    for kt in range(GD):
                        Tr(lambda kt=kt, tok=tok: ntn.matmul(
                            ps_st[:, 0, :], onec[:], h[:, kt, tok * TT:(tok + 1) * TT],
                            start=(kt == 0), stop=(kt == GD - 1)), inc=None)
                    for kt in range(GD):
                        Tr(lambda kt=kt, tok=tok: ntn.matmul(
                            ps_st[:, 1, :], onec[:], y[:, kt, tok * TT:(tok + 1) * TT],
                            start=(kt == 0), stop=(kt == GD - 1)),
                           inc=("T" if kt == GD - 1 else None))
                    bc_v, stfree_v = ln_stats_and_bcast(tok, float(D))
                rb = bcst[:, 0:1, :].broadcast_to([128, GD, TC])
                mbb = bcst[:, 1:2, :].broadcast_to([128, GD, TC])
                Vr(lambda: nv.tensor_tensor(yf, hf, rb, ALU.mult), inc=None)
                Vr(lambda: nv.tensor_tensor(yf, yf, mbb, ALU.subtract))
                for g in range(GD):
                    bb = lnlbS[:, g:g + 1].broadcast_to([128, TC])
                    Vr(lambda g=g, bb=bb: nv.scalar_tensor_tensor(
                        y[:, g, :], y[:, g, :], lnlwS[:, g:g + 1], bb,
                        ALU.mult, ALU.add))

            final_ln()
            mark("P4_ln")
            if "hln" in taps:
                p.wait("gpsimd", "V")
                tap_out("hln", yf)

            def x1sb(k):
                buf = h if k < 16 else w1t
                kk = k % 16
                return buf[:, kk // 4, (kk % 4) * TT:(kk % 4 + 1) * TT]

            hln_v = p.cnt["V"]
            p.wait("sync", "T", p.cnt["T"])
            wout_mark = [hln_v] * 4
            wt_marks = [0, 0]
            xe_mark = [0, 0, 0]
            sq2_marks = [0, 0, 0]
            stfree2 = 0
            ab_marks = [0, 0]
            rs_evac = 0
            a_epi_v = 0
            rsA_gd = 0
            a_out_mark = [None, None]
            ai = 0
            for tok in range(NTOK):
                sl_t = slice(tok * TT, (tok + 1) * TT)
                p.wait("tensor", "V", stfree_v)
                for vm in range(KV):
                    sl = vm % 3
                    pa = vm % 4
                    p.wait("sync", "T", wt_marks[0])
                    wl = sring.load(wtb[:, sl, :],
                                    dd["woutT"][:, vm * 128:(vm + 1) * 128].rearrange(
                                        "(kt q) m -> q kt m", q=128))
                    p.wait("gpsimd", "V", xe_mark[sl])
                    xl = ering.load(xeb[:, sl, :],
                                    dd["xT"][vm * 128:(vm + 1) * 128, sl_t])
                    sring.wait("tensor", wl)
                    p.wait("tensor", "V", wout_mark[pa])
                    for kt in range(GD):
                        Tr(lambda vm=vm, sl=sl, kt=kt, pa=pa, sl_t=sl_t: ntn.matmul(
                            ps_a[:, pa, :], wtb[:, sl, kt * 128:(kt + 1) * 128],
                            y[:, kt, sl_t],
                            start=(kt == 0), stop=(kt == GD - 1)), inc=None)
                    wt_marks = [wt_marks[1], mm_fence(ps_a[0:1, pa, 0:1])]
                    x1t = x1sb(vm)
                    p.wait("vector", "T")
                    ering.wait("vector", xl)
                    Vr(lambda vm=vm, sl=sl, pa=pa, x1t=x1t: nv.scalar_tensor_tensor(
                        x1t, ps_a[:, pa, :], boutS[:, vm:vm + 1], xeb[:, sl, :],
                        ALU.add, ALU.add))
                    epi_v = p.cnt["V"]
                    wout_mark[pa] = epi_v
                    xe_mark[sl] = epi_v
                    su = vm % 3
                    p.wait("scalar", "V", epi_v)
                    p.wait("scalar", "T", sq2_marks[su])
                    sqs = Sc(lambda su=su, x1t=x1t: ns.activation(
                        u[:, 0, su * TT:(su + 1) * TT], x1t, AF.Square))
                    p.wait("tensor", "S", sqs)
                    if vm == 0:
                        p.wait("tensor", "V", stfree2)
                    Tr(lambda vm=vm, x1t=x1t: ntn.matmul(
                        ps_st[:, 0, :], onec[:], x1t,
                        start=(vm == 0), stop=(vm == KV - 1)), inc=None)
                    sq2_marks[su] = Tr(lambda vm=vm, su=su: ntn.matmul(
                        ps_st[:, 1, :], onec[:], u[:, 0, su * TT:(su + 1) * TT],
                        start=(vm == 0), stop=(vm == KV - 1)), inc="T")
                mm_fence(ps_st[0:1, 0, 0:1])
                bc_v, stfree2 = ln_stats_and_bcast(tok, float(V))

                if "x1" in taps:
                    p.wait("gpsimd", "V", bc_v)
                    for vm in range(KV):
                        gp_load(d_tap["x1"][vm * 128:(vm + 1) * 128, sl_t], x1sb(vm))
                    p.wait("vector", "GDm")

                if stage < 4:
                    continue
                for mb in range(8):
                    p.wait("tensor", "V", bc_v if mb == 0 else a_epi_v)
                    for kb in range(KV // 4):
                        j = ai % 2
                        ai += 1
                        p.wait("sync", "T", ab_marks[0])
                        al = sring.load(
                            ab4[:, j],
                            dd["aT"][kb * 512:(kb + 1) * 512,
                                     mb * TT:(mb + 1) * TT].rearrange(
                                         "(c q) m -> q c m", q=128))
                        sring.wait("tensor", al)
                        if tok == 0 and kb == 0:
                            p.wait("tensor", "V", rs_evac)
                        for c in range(4):
                            k = kb * 4 + c
                            for mt in range(4):
                                Tr(lambda j=j, c=c, mt=mt, k=k: ntn.matmul(
                                    ps_a[:, mt, :], ab4[:, j, c, mt * 128:(mt + 1) * 128],
                                    x1sb(k),
                                    start=(k == 0), stop=(k == KV - 1)),
                                   inc=(None if (mt < 3 or tok == 0 or c < 3) else "T"))
                            if tok == 0:
                                Tr(lambda j=j, c=c, k=k: ntn.matmul(
                                    ps_b[0:1, 0, :], nonec[:], ab4[:, j, c, :],
                                    start=(k == 0), stop=(k == KV - 1)),
                                   inc=("T" if c == 3 else None))
                        ab_marks = ab_marks[1:] + [p.cnt["T"]]
                    mm_t = mm_fence(ps_a[0:1, 0, 0:1])
                    ab_marks = ab_marks[1:] + [mm_t]
                    if tok == 0:
                        p.wait("vector", "T", mm_t)
                        p.wait("vector", "GDm")
                        rs_evac = Vr(lambda: nv.tensor_copy(rowb[:, 1, :], ps_b[0:1, 0, :]))
                        p.wait("gpsimd", "V", rs_evac)
                        gb1 = gp_load(bounce.ap()[0:1, mb * TT:(mb + 1) * TT], rowb[0:1, 1, :])
                        p.wait("gpsimd", "GDm", gb1)
                        rsA_gd = gp_load(
                            rsAS[:, mb * 4:(mb + 1) * 4],
                            bounce.ap()[0, mb * TT:(mb + 1) * TT].rearrange("(g q) -> q g", q=128),
                            slow=True)
                    p.wait("vector", "T", mm_t)
                    if tok == 0:
                        p.wait("vector", "GDm", rsA_gd)
                    for mt in range(4):
                        vm = mb * 4 + mt
                        sl2 = mt % 2
                        esl = u[:, 1, sl2 * TT:(sl2 + 1) * TT]
                        if a_out_mark[sl2] is not None:
                            p.wait("vector", a_out_mark[sl2][0], a_out_mark[sl2][1])
                        Vr(lambda mt=mt, esl=esl, sl_t=sl_t: nv.tensor_tensor(
                            esl, ps_a[:, mt, :], bcst[:, 0, sl_t], ALU.mult), inc=None)
                        Vr(lambda vm=vm, esl=esl, sl_t=sl_t: nv.scalar_tensor_tensor(
                            esl, bcst[:, 1, sl_t], rsAS[:, vm:vm + 1], esl,
                            ALU.mult, ALU.add), inc=None)
                        Vr(lambda vm=vm, esl=esl: nv.scalar_tensor_tensor(
                            esl, esl, bAS[:, vm:vm + 1], x1sb(vm),
                            ALU.add, ALU.add))
                        ep_v = p.cnt["V"]
                        p.wait("scalar", "V", ep_v)
                        onm = "O0" if sl2 % 2 == 0 else "O1"
                        p.wait("scalar", onm, p.cnt[onm])
                        p.run("scalar", lambda vm=vm, esl=esl, sl_t=sl_t:
                              nc.scalar.dma_start(out=outT[vm * 128:(vm + 1) * 128, sl_t], in_=esl),
                              onm, 16)
                        a_out_mark[sl2] = (onm, p.cnt[onm])
                    a_epi_v = p.cnt["V"]
                wout_mark = [a_epi_v] * 4
            mark("P4_wout")

        if stage >= 4:
            mark("P5_A")
        p.wait("sync", "O0", p.cnt["O0"])
        p.wait("sync", "O1", p.cnt["O1"])
        p.wait("sync", "GDm", p.cnt["GDm"])
        p.emit()
    return nc, d_tap


def kernel(**inputs):
    in_maps = stage_inputs(inputs)
    nc, _ = build_program(taps=DEBUG_TAPS, stage=STAGE)
    res = run_bass_kernel_spmd(nc, in_maps, core_ids=list(range(NCORES)))
    out = np.empty((B, L, V), dtype=np.float32)
    for c in range(NCORES):
        b, q = c // QPER, c % QPER
        out[b, q * TC:(q + 1) * TC, :] = res.results[c]["outT"].T
    return out


# version 07
# speedup vs baseline: 1.0194x; 1.0194x over previous
from contextlib import ExitStack

import numpy as np
import concourse.bass as bass
import concourse.mybir as mybir
from concourse.bass_utils import run_bass_kernel_spmd

F32 = mybir.dt.float32
I32 = mybir.dt.int32
AF = mybir.ActivationFunctionType
ALU = mybir.AluOpType

B, L, V, D, FF, DEPTH = 2, 8192, 4096, 512, 2048, 4
NCORES, QPER = 8, 4
TC = L // QPER
TT = 512
NTOK = TC // TT
KV = V // 128
GD = D // 128
KF = FF // 128
PI = float(np.pi)
TWO_PI = float(2 * np.pi)
INV_2PI = float(1.0 / (2 * np.pi))
PI_LO = float(np.float32(3.1415925))
LN_EPS = 1e-5

ENGINES = ("tensor", "vector", "scalar", "gpsimd", "sync")

DEBUG_TAPS = ()
STAGE = 4
PHASE_MARKS = []


class Prog:

    SELF = {"vector": "V", "scalar": "S", "gpsimd": "G"}

    def __init__(self, nc, stack, sem_names):
        self.nc = nc
        self.q = {e: [] for e in ENGINES}
        self.sem = {n: stack.enter_context(nc.semaphore(n)) for n in sem_names}
        self.cnt = {n: 0 for n in sem_names}

    def run(self, eng, fn, inc=None, n=1):
        selfsem = self.SELF.get(eng)
        if selfsem is not None and inc is None:
            inc, n = selfsem, 1
        if inc is not None:
            if selfsem is not None and inc == selfsem:
                prev = self.cnt[inc]
                if prev:
                    self.q[eng].append(("w", self.sem[inc], prev))
            self.cnt[inc] += n
            sem = self.sem[inc]
            self.q[eng].append(lambda fn=fn, sem=sem, n=n: fn().then_inc(sem, n))
            return self.cnt[inc]
        self.q[eng].append(fn)
        return None

    def wait(self, eng, name, val=None):
        v = self.cnt[name] if val is None else val
        if v <= 0:
            return
        self.q[eng].append(("w", self.sem[name], v))

    def emit(self):
        with self.nc.Block() as blk:
            for e in ENGINES:
                def body(engobj, e=e):
                    for item in self.q[e]:
                        if isinstance(item, tuple):
                            engobj.wait_ge(item[1], item[2])
                        else:
                            item()
                getattr(blk, e)(body)


def stage_inputs(inp):
    f = np.ascontiguousarray
    x = np.asarray(inp["L0_attn_x"], dtype=np.float32)

    def ch(v):
        return f(np.asarray(v, np.float32).reshape(GD, 128).T)

    def chl(v):
        return f(np.asarray(v, np.float32).reshape(DEPTH, GD, 128).transpose(2, 0, 1))

    def vw(v, g):
        return f(np.asarray(v, np.float32).reshape(g, 128).T)

    common = {
        "winT": f(np.asarray(inp["L0_attn_Win"], np.float32).T),
        "w1T": f(np.asarray(inp["L0_attn_W1"], np.float32).transpose(0, 2, 1)),
        "w2T": f(np.asarray(inp["L0_attn_W2"], np.float32).transpose(0, 2, 1)),
        "woutT": f(np.asarray(inp["L0_attn_Wout"], np.float32).T),
        "aT": f(np.asarray(inp["L0_ffn_A"], np.float32).T),
        "bin": ch(inp["L0_attn_bin"]),
        "b1": f(np.asarray(inp["L0_attn_b1"], np.float32).reshape(DEPTH, KF, 128).transpose(2, 0, 1)),
        "b2": chl(inp["L0_attn_b2"]),
        "ln1w": chl(inp["L0_attn_ln1_w"]), "ln1b": chl(inp["L0_attn_ln1_b"]),
        "ln2w": chl(inp["L0_attn_ln2_w"]), "ln2b": chl(inp["L0_attn_ln2_b"]),
        "phzr": chl(inp["L0_attn_phz_re"]), "phzi": chl(inp["L0_attn_phz_im"]),
        "lcr": chl(inp["L0_attn_lc_re"]), "lci": chl(inp["L0_attn_lc_im"]),
        "scl": chl(inp["L0_attn_sc_lin"]),
        "lnlw": ch(inp["L0_attn_lnl_w"]), "lnlb": ch(inp["L0_attn_lnl_b"]),
        "bout": vw(inp["L0_attn_bout"], KV),
        "bA": vw(inp["L0_ffn_b"], KV),
        "iota": f(np.broadcast_to(np.arange(TC, dtype=np.float32), (128, TC)).copy()),
    }
    in_maps = []
    for c in range(NCORES):
        b, q = c // QPER, c % QPER
        xT = f(x[b, q * TC:(q + 1) * TC, :].T)
        sel = np.zeros((128, 7), np.float32)
        for j in range(7):
            if QPER * b <= j < QPER * b + q:
                sel[:, j] = 1.0
        in_maps.append(dict(common, xT=xT, sel=sel))
    return in_maps


def build_program(taps=(), stage=4):
    PHASE_MARKS.clear()
    nc = bass.Bass("TRN2", target_bir_lowering=False)
    taps = set(taps)
    nv, ns, ntn, ng = nc.vector, nc.scalar, nc.tensor, nc.gpsimd

    dd = {}
    for name, shape in [
        ("xT", [V, TC]), ("winT", [V, D]), ("w1T", [DEPTH, D, FF]), ("w2T", [DEPTH, FF, D]),
        ("woutT", [D, V]), ("aT", [V, V]), ("bin", [128, GD]),
        ("b1", [128, DEPTH, KF]), ("b2", [128, DEPTH, GD]),
        ("ln1w", [128, DEPTH, GD]), ("ln1b", [128, DEPTH, GD]),
        ("ln2w", [128, DEPTH, GD]), ("ln2b", [128, DEPTH, GD]),
        ("phzr", [128, DEPTH, GD]), ("phzi", [128, DEPTH, GD]),
        ("lcr", [128, DEPTH, GD]), ("lci", [128, DEPTH, GD]),
        ("scl", [128, DEPTH, GD]), ("lnlw", [128, GD]), ("lnlb", [128, GD]),
        ("bout", [128, KV]), ("bA", [128, KV]), ("iota", [128, TC]), ("sel", [128, 7]),
    ]:
        dd[name] = nc.dram_tensor(name, shape, F32, kind="ExternalInput").ap()

    outT = nc.dram_tensor("outT", [V, TC], F32, kind="ExternalOutput").ap() if stage >= 4 else None
    d_tap = {}
    for t in taps:
        shp = {"h0": [128, GD * TC], "y0": [128, GD * TC], "conv0": [128, GD * TC],
               "h1": [128, GD * TC], "hfin": [128, GD * TC], "hln": [128, GD * TC],
               "x1": [V, TC]}[t]
        d_tap[t] = nc.dram_tensor("tap_" + t, shp, F32, kind="ExternalOutput").ap()

    ag_in = nc.dram_tensor("ag_in", [1, 1024], F32)
    ag_out = nc.dram_tensor("ag_out", [NCORES, 1024], F32)
    bounce = nc.dram_tensor("bounce", [1, V], F32)

    st = ExitStack()
    with st:
        def sbt(name, shape, dt=F32):
            return st.enter_context(nc.sbuf_tensor(name, shape, dt))

        h = sbt("h", [128, GD, TC])
        y = sbt("y", [128, GD, TC])
        tabs = sbt("tabs", [128, 2, TC])
        u = sbt("u", [128, 2, TC])
        bcst = sbt("bcst", [128, 2, TC])
        y1 = sbt("y1", [128, 2, TT])
        w1t = sbt("w1t", [128, GD, FF])
        w2b = sbt("w2b", [128, 3, TT])
        xb = sbt("xb", [128, 3, TT])
        wtb = sbt("wtb", [128, 3, TT])
        ab4 = sbt("ab4", [128, 2, 4, TT])
        xeb = sbt("xeb", [128, 3, TT])
        rowb = sbt("rowb", [1, 2, TT])
        iot = sbt("iot", [128, TC])
        onec = sbt("onec", [128, 1])
        zerc = sbt("zerc", [128, 1])
        nonec = sbt("nonec", [128, 1])
        oner = sbt("oner", [1, 128])
        binS = sbt("binS", [128, GD])
        b1S = sbt("b1S", [128, DEPTH, KF])
        b2S = sbt("b2S", [128, DEPTH, GD])
        prm = {k: sbt(k + "S", [128, DEPTH, GD]) for k in
               ["ln1w", "ln1b", "ln2w", "ln2b", "phzr", "phzi", "lcr", "lci", "scl"]}
        lnlwS = sbt("lnlwS", [128, GD])
        lnlbS = sbt("lnlbS", [128, GD])
        boutS = sbt("boutS", [128, KV])
        bAS = sbt("bAS", [128, KV])
        selS = sbt("selS", [128, 7])
        rsS = sbt("rsS", [128, GD])
        rsAS = sbt("rsAS", [128, KV])
        pp = {k: sbt("pp_" + k, [128, DEPTH, GD]) for k in
              ["rho", "lnrho", "absth", "sgn", "eR", "eI", "p2r", "p2i",
               "ln2047", "th2047"]}
        t16a = sbt("t16a", [128, DEPTH * GD])
        t16b = sbt("t16b", [128, DEPTH * GD])
        t16c = sbt("t16c", [128, DEPTH * GD])
        t16d = sbt("t16d", [128, DEPTH * GD])
        fcell = sbt("fcell", [1, 8])
        ES = sbt("ES", [128, 8])
        EA = sbt("EA", [128, 7, 8])
        CCs = sbt("CCs", [128, 8])
        WIs = sbt("WIs", [128, 8])
        c8a = sbt("c8a", [128, 8])
        c8b = sbt("c8b", [128, 8])

        ps_a = st.enter_context(nc.psum_tensor("ps_a", [128, 4, TT], F32))
        ps_b = st.enter_context(nc.psum_tensor("ps_b", [128, 2, TT], F32))
        ps_st = st.enter_context(nc.psum_tensor("ps_st", [1, 2, TT], F32))

        p = Prog(nc, st, ["T", "V", "S", "G", "GDm",
                          "L0", "L1", "L2", "L3", "L4", "L5",
                          "E0", "E1", "E2", "O0", "O1"])

        def Vr(fn, inc="V", n=1):
            return p.run("vector", fn, None, 1)

        def Sc(fn, inc="S", n=1):
            return p.run("scalar", fn, None, 1)

        def Tr(fn, inc=None, n=1):
            return p.run("tensor", fn, inc, n)

        def Gp(fn, inc=None, n=1):
            if inc is None:
                return p.run("gpsimd", fn, None, 1)
            return p.run("gpsimd", fn, inc, n)

        def Sy(fn, inc=None, n=1):
            return p.run("sync", fn, inc, n)

        def mark(name):
            PHASE_MARKS.append((name, dict(p.cnt)))

        class Ring:

            def __init__(self, eng, names):
                self.eng, self.names = eng, names
                self.i = 0

            def load_group(self, pairs):
                nm = self.names[self.i % len(self.names)]
                self.i += 1
                p.wait(self.eng, nm, p.cnt[nm])
                for dst_ap, src_ap in pairs:
                    if self.eng == "sync":
                        p.run("sync", lambda dst_ap=dst_ap, src_ap=src_ap:
                              nc.sync.dma_start(out=dst_ap, in_=src_ap), nm, 16)
                    else:
                        p.run("gpsimd", lambda dst_ap=dst_ap, src_ap=src_ap:
                              ng.dma_start(out=dst_ap, in_=src_ap), nm, 16)
                return (nm, p.cnt[nm])

            def load(self, dst_ap, src_ap, slow=False):
                nm = self.names[self.i % len(self.names)]
                self.i += 1
                p.wait(self.eng, nm, p.cnt[nm])
                if self.eng == "sync":
                    p.run("sync", lambda dst_ap=dst_ap, src_ap=src_ap:
                          nc.sync.dma_start(out=dst_ap, in_=src_ap), nm, 16)
                else:
                    p.run("gpsimd", lambda dst_ap=dst_ap, src_ap=src_ap, slow=slow:
                          ng.dma_start(out=dst_ap, in_=src_ap,
                                       allow_slow_non_contiguous=slow), nm, 16)
                return (nm, p.cnt[nm])

            def wait(self, eng, mark):
                p.wait(eng, mark[0], mark[1])

        sring = Ring("sync", ["L0", "L1", "L2", "L3", "L4", "L5"])
        ering = Ring("gpsimd", ["E0", "E1", "E2"])

        def gp_load(dst_ap, src_ap, slow=False):
            p.wait("gpsimd", "GDm", p.cnt["GDm"])
            return Gp(lambda dst_ap=dst_ap, src_ap=src_ap, slow=slow:
                      ng.dma_start(out=dst_ap, in_=src_ap,
                                   allow_slow_non_contiguous=slow), inc="GDm", n=16)

        def tap_out(name, src):
            m = gp_load(d_tap[name], src)
            p.wait("vector", "GDm", m)
            p.wait("scalar", "GDm", m)
            p.wait("tensor", "GDm", m)

        def mm_fence(bank_ap):
            Tr(lambda bank_ap=bank_ap: ntn.matmul(
                bank_ap, zerc[:, 0:1], onec[:, 0:1],
                start=False, stop=True, skip_group_check=True), inc=None)
            return Tr(lambda bank_ap=bank_ap: ntn.matmul(
                bank_ap, zerc[:, 0:1], onec[:, 0:1],
                start=False, stop=True, skip_group_check=True), inc="T")

        def range_reduce(dst, src, iscr):
            ii = iscr.bitcast(I32)
            Vr(lambda: nv.tensor_single_scalar(dst, src, INV_2PI, ALU.mult), inc=None)
            Vr(lambda: nv.tensor_copy(ii, dst), inc=None)
            Vr(lambda: nv.tensor_copy(iscr, ii), inc=None)
            Vr(lambda: nv.scalar_tensor_tensor(dst, iscr, -TWO_PI, src, ALU.mult, ALU.add), inc=None)
            Vr(lambda: nv.tensor_single_scalar(iscr, dst, PI, ALU.is_gt), inc=None)
            Vr(lambda: nv.scalar_tensor_tensor(dst, iscr, -TWO_PI, dst, ALU.mult, ALU.add), inc=None)
            Vr(lambda: nv.tensor_single_scalar(iscr, dst, -PI, ALU.is_lt), inc=None)
            Vr(lambda: nv.scalar_tensor_tensor(dst, iscr, TWO_PI, dst, ALU.mult, ALU.add), inc=None)
            Vr(lambda: nv.tensor_single_scalar(dst, dst, PI_LO, ALU.min), inc=None)
            Vr(lambda: nv.tensor_single_scalar(dst, dst, -PI_LO, ALU.max), inc=None)

        def cos_from_red(dst, red, mscr):
            Vr(lambda: nv.tensor_scalar(mscr, red, -1.0, PI / 2, ALU.mult, ALU.add), inc=None)
            Vr(lambda: nv.tensor_single_scalar(dst, mscr, PI, ALU.is_gt), inc=None)
            Vr(lambda: nv.scalar_tensor_tensor(mscr, dst, -TWO_PI, mscr, ALU.mult, ALU.add), inc=None)
            Vr(lambda: nv.tensor_single_scalar(mscr, mscr, PI_LO, ALU.min), inc=None)
            Vr(lambda: nv.tensor_single_scalar(mscr, mscr, -PI_LO, ALU.max))
            p.wait("scalar", "V")
            Sc(lambda: ns.activation(dst, mscr, AF.Sin))

        def build_tables(lay, g, with_rho):
            ath = pp["absth"][:, lay, g:g + 1].broadcast_to([128, TC])
            agl = tabs[:, 0, :]
            Vr(lambda agl=agl, ath=ath: nv.tensor_tensor(agl, iot[:], ath, ALU.mult), inc=None)
            if with_rho:
                th47 = pp["th2047"][:, lay, g:g + 1].broadcast_to([128, TC])
                Vr(lambda agl=agl, th47=th47: nv.scalar_tensor_tensor(
                    agl, agl, -1.0, th47, ALU.mult, ALU.add), inc=None)
            range_reduce(u[:, 0, :], agl, u[:, 1, :])
            if with_rho:
                lnr = pp["lnrho"][:, lay, g:g + 1].broadcast_to([128, TC])
                l47 = pp["ln2047"][:, lay, g:g + 1].broadcast_to([128, TC])
                Vr(lambda lnr=lnr: nv.tensor_tensor(tabs[:, 1, :], iot[:], lnr, ALU.mult), inc=None)
                Vr(lambda l47=l47: nv.scalar_tensor_tensor(
                    tabs[:, 1, :], tabs[:, 1, :], -1.0, l47, ALU.mult, ALU.add))
                p.wait("scalar", "V")
                Sc(lambda: ns.activation(tabs[:, 1, :], tabs[:, 1, :], AF.Exp))
                p.wait("vector", "S")
            cos_from_red(tabs[:, 0, :], u[:, 0, :], u[:, 1, :])
            p.wait("vector", "S")
            if with_rho:
                Vr(lambda: nv.tensor_tensor(tabs[:, 0, :], tabs[:, 0, :], tabs[:, 1, :], ALU.mult), inc=None)
                Vr(lambda: nv.tensor_copy(u[:, 1, :], u[:, 0, :]))
                p.wait("scalar", "V")
                Sc(lambda: ns.activation(u[:, 1, :], u[:, 1, :], AF.Sin))
                p.wait("vector", "S")
                Vr(lambda: nv.tensor_tensor(tabs[:, 1, :], u[:, 1, :], tabs[:, 1, :], ALU.mult), inc=None)
            else:
                Vr(lambda: nv.tensor_copy(u[:, 1, :], u[:, 0, :]))
                p.wait("scalar", "V")
                Sc(lambda: ns.activation(tabs[:, 1, :], u[:, 1, :], AF.Sin))
                p.wait("vector", "S")

        def ln_stats_and_bcast(tok, denom, extra_wait_v=None):
            mu, rr = rowb[:, 0, :], rowb[:, 1, :]
            p.wait("vector", "T")
            Vr(lambda: nv.tensor_single_scalar(mu, ps_st[:, 0, :], 1.0 / denom, ALU.mult), inc=None)
            Vr(lambda: nv.tensor_tensor(rr, mu, mu, ALU.mult), inc=None)
            Vr(lambda: nv.scalar_tensor_tensor(rr, ps_st[:, 1, :], 1.0 / denom, rr,
                                               ALU.mult, ALU.subtract))
            st_free = p.cnt["V"]
            Vr(lambda: nv.tensor_single_scalar(rr, rr, LN_EPS, ALU.add))
            p.wait("scalar", "V")
            Sc(lambda: ns.activation(rr, rr, AF.Sqrt))
            p.wait("vector", "S")
            Vr(lambda: nv.reciprocal(rr, rr), inc=None)
            Vr(lambda: nv.tensor_tensor(mu, rr, mu, ALU.mult))
            rows_done = p.cnt["V"]
            p.wait("tensor", "V", rows_done)
            if extra_wait_v is not None:
                p.wait("tensor", "V", extra_wait_v)
            p.wait("tensor", "S")
            Tr(lambda: ntn.matmul(ps_b[:, 0, :], oner[:], rr, start=True, stop=True), inc=None)
            Tr(lambda: ntn.matmul(ps_b[:, 1, :], oner[:], mu, start=True, stop=True), inc=None)
            mm_fence(ps_b[0:1, 0, 0:1])
            p.wait("vector", "T")
            sl = slice(tok * TT, (tok + 1) * TT)
            Vr(lambda sl=sl: nv.tensor_copy(bcst[:, 0, sl], ps_b[:, 0, :]), inc=None)
            Vr(lambda sl=sl: nv.tensor_copy(bcst[:, 1, sl], ps_b[:, 1, :]))
            return p.cnt["V"], st_free

        Vr(lambda: nv.memset(onec[:], 1.0), inc=None)
        Vr(lambda: nv.memset(zerc[:], 0.0), inc=None)
        Vr(lambda: nv.memset(fcell[:], 1.0), inc=None)
        Vr(lambda: nv.memset(nonec[:], -1.0), inc=None)
        Vr(lambda: nv.memset(oner[:], 1.0))
        const_v = p.cnt["V"]
        p.wait("tensor", "V", const_v)
        p.wait("scalar", "V", const_v)
        p.wait("gpsimd", "V", const_v)
        for dst, src in [(iot, dd["iota"]), (binS, dd["bin"]), (b1S, dd["b1"]),
                         (b2S, dd["b2"]), (lnlwS, dd["lnlw"]), (lnlbS, dd["lnlb"]),
                         (boutS, dd["bout"]), (bAS, dd["bA"]), (selS, dd["sel"])] + \
                        [(prm[k], dd[k]) for k in prm]:
            gp_load(dst[:], src)
        ld_params = p.cnt["GDm"]

        phr, phi = prm["phzr"][:, :, :], prm["phzi"][:, :, :]
        rho16, lnrho16 = pp["rho"][:, :, :], pp["lnrho"][:, :, :]
        absth16, sgn16 = pp["absth"][:, :, :], pp["sgn"][:, :, :]
        eR16, eI16 = pp["eR"][:, :, :], pp["eI"][:, :, :]
        p2r16, p2i16 = pp["p2r"][:, :, :], pp["p2i"][:, :, :]
        ln2047_16, th2047_16 = pp["ln2047"][:, :, :], pp["th2047"][:, :, :]
        A16, B16, C16, D16 = t16a[:], t16b[:], t16c[:], t16d[:]

        p.wait("vector", "GDm", ld_params)
        Vr(lambda: nv.tensor_tensor(A16, phr, phr, ALU.mult), inc=None)
        Vr(lambda: nv.tensor_tensor(B16, phi, phi, ALU.mult), inc=None)
        Vr(lambda: nv.tensor_tensor(A16, A16, B16, ALU.add), inc=None)
        Vr(lambda: nv.tensor_single_scalar(A16, A16, 1e-30, ALU.max))
        p.wait("scalar", "V")
        Sc(lambda: ns.activation(B16, A16, AF.Sqrt))
        p.wait("vector", "S")
        Vr(lambda: nv.reciprocal(C16, B16), inc=None)
        Vr(lambda: nv.tensor_tensor(eR16, phr, C16, ALU.mult), inc=None)
        Vr(lambda: nv.tensor_tensor(eI16, phi, C16, ALU.mult))
        p.wait("scalar", "V")
        Sc(lambda: ns.activation(rho16, B16, AF.Tanh), inc=None)
        Sc(lambda: ns.activation(lnrho16, rho16, AF.Ln), inc=None)
        Sc(lambda: ns.activation(A16, eR16, AF.Abs), inc=None)
        Sc(lambda: ns.activation(B16, eI16, AF.Abs))
        p.wait("vector", "S")
        Vr(lambda: nv.tensor_single_scalar(ln2047_16, lnrho16, 2047.0, ALU.mult), inc=None)
        Vr(lambda: nv.tensor_tensor(C16, A16, B16, ALU.min), inc=None)
        Vr(lambda: nv.tensor_tensor(D16, A16, B16, ALU.max), inc=None)
        Vr(lambda: nv.reciprocal(D16, D16), inc=None)
        Vr(lambda: nv.tensor_tensor(C16, C16, D16, ALU.mult))
        p.wait("scalar", "V")
        Sc(lambda: ns.activation(D16, C16, AF.Arctan))
        p.wait("scalar", "S")
        Sc(lambda: ns.activation(A16, D16, AF.Sin))
        p.wait("vector", "S")
        Vr(lambda: nv.tensor_tensor(B16, A16, A16, ALU.mult), inc=None)
        Vr(lambda: nv.tensor_scalar(B16, B16, -1.0, 1.0, ALU.mult, ALU.add))
        p.wait("scalar", "V")
        Sc(lambda: ns.activation(B16, B16, AF.Sqrt))
        p.wait("vector", "S")
        Vr(lambda: nv.tensor_tensor(C16, C16, B16, ALU.mult), inc=None)
        Vr(lambda: nv.tensor_tensor(C16, C16, A16, ALU.subtract), inc=None)
        Vr(lambda: nv.tensor_tensor(C16, C16, B16, ALU.mult), inc=None)
        Vr(lambda: nv.tensor_tensor(D16, D16, C16, ALU.add), inc=None)
        Vr(lambda: nv.tensor_tensor(A16, eI16, eI16, ALU.mult), inc=None)
        Vr(lambda: nv.tensor_tensor(B16, eR16, eR16, ALU.mult), inc=None)
        Vr(lambda: nv.tensor_tensor(A16, A16, B16, ALU.is_gt), inc=None)
        Vr(lambda: nv.tensor_scalar(B16, D16, -2.0, PI / 2, ALU.mult, ALU.add), inc=None)
        Vr(lambda: nv.tensor_tensor(A16, A16, B16, ALU.mult), inc=None)
        Vr(lambda: nv.tensor_tensor(D16, D16, A16, ALU.add), inc=None)
        Vr(lambda: nv.tensor_single_scalar(A16, eR16, 0.0, ALU.is_lt), inc=None)
        Vr(lambda: nv.tensor_scalar(B16, D16, -2.0, PI, ALU.mult, ALU.add), inc=None)
        Vr(lambda: nv.tensor_tensor(A16, A16, B16, ALU.mult), inc=None)
        Vr(lambda: nv.tensor_tensor(absth16, D16, A16, ALU.add), inc=None)
        Vr(lambda: nv.tensor_single_scalar(A16, eI16, 0.0, ALU.is_ge), inc=None)
        Vr(lambda: nv.tensor_scalar(sgn16, A16, 2.0, -1.0, ALU.mult, ALU.add), inc=None)
        Vr(lambda: nv.tensor_single_scalar(th2047_16, absth16, 2047.0, ALU.mult), inc=None)
        Vr(lambda: nv.tensor_single_scalar(A16, lnrho16, 2048.0, ALU.mult))
        p.wait("scalar", "V")
        Sc(lambda: ns.activation(A16, A16, AF.Exp))
        p.wait("vector", "S")
        Vr(lambda: nv.tensor_single_scalar(B16, absth16, 2048.0, ALU.mult), inc=None)
        range_reduce(C16, B16, D16)
        Vr(lambda: nv.tensor_copy(D16, C16))
        p.wait("scalar", "V")
        Sc(lambda: ns.activation(B16, C16, AF.Sin))
        p.wait("vector", "S")
        Vr(lambda: nv.tensor_tensor(p2i16, A16, B16, ALU.mult), inc=None)
        Vr(lambda: nv.tensor_tensor(p2i16, p2i16, sgn16, ALU.mult), inc=None)
        cos_from_red(B16, D16, C16)
        p.wait("vector", "S")
        Vr(lambda: nv.tensor_tensor(p2r16, A16, B16, ALU.mult))
        mark("P1")

        tmark_last = 0
        epi_done_v = 0
        stfree_v = 0
        tmarks = [0, 0]
        smarks = [0, 0]
        for tok in range(NTOK):
            for k in range(KV):
                sl = k % 3
                p.wait("sync", "T", tmarks[0])
                p.wait("sync", "S", smarks[0])
                mk = sring.load_group([
                    (xb[:, sl, :], dd["xT"][k * 128:(k + 1) * 128, tok * TT:(tok + 1) * TT]),
                    (wtb[:, sl, :], dd["winT"][k * 128:(k + 1) * 128, :])])
                sring.wait("scalar", mk)
                p.wait("scalar", "T", tmarks[0])
                s_sq = Sc(lambda sl=sl: ns.activation(xeb[:, sl, :], xb[:, sl, :], AF.Square))
                smarks = [smarks[1], s_sq]
                sring.wait("tensor", mk)
                if k == 0:
                    p.wait("tensor", "V", epi_done_v)
                    p.wait("tensor", "V", stfree_v)
                for dm in range(GD):
                    Tr(lambda k=k, sl=sl, dm=dm: ntn.matmul(
                        ps_a[:, dm, :], wtb[:, sl, dm * 128:(dm + 1) * 128], xb[:, sl, :],
                        start=(k == 0), stop=(k == KV - 1)), inc=None)
                Tr(lambda k=k, sl=sl: ntn.matmul(
                    ps_st[:, 0, :], onec[:], xb[:, sl, :],
                    start=(k == 0), stop=(k == KV - 1)), inc=None)
                p.wait("tensor", "S", s_sq)
                if tok == 0:
                    Tr(lambda k=k, sl=sl: ntn.matmul(
                        ps_b[0:1, 0, 0:D], nonec[:], wtb[:, sl, :],
                        start=(k == 0), stop=(k == KV - 1)), inc=None)
                tm = Tr(lambda k=k, sl=sl: ntn.matmul(
                    ps_st[:, 1, :], onec[:], xeb[:, sl, :],
                    start=(k == 0), stop=(k == KV - 1)), inc="T")
                tmarks = [tmarks[1], tm]
            mm_fence(ps_st[0:1, 0, 0:1])
            if tok == 0:
                p.wait("vector", "T")
                rs_v = Vr(lambda: nv.tensor_copy(rowb[:, 1, 0:TT], ps_b[0:1, 0, 0:D]))
                p.wait("gpsimd", "V", rs_v)
                g1 = gp_load(bounce.ap()[0:1, 0:D], rowb[0:1, 1, 0:D])
                p.wait("gpsimd", "GDm", g1)
                g2 = gp_load(rsS[:], bounce.ap()[0, 0:D].rearrange("(g q) -> q g", q=128), slow=True)
                p.wait("vector", "GDm", g1)
            bc_v, stfree_v = ln_stats_and_bcast(tok, float(V), extra_wait_v=(rs_v if tok == 0 else None))
            if tok == 0:
                p.wait("vector", "GDm", g2)
            sl_t = slice(tok * TT, (tok + 1) * TT)
            for dm in range(GD):
                hsl = h[:, dm, sl_t]
                Vr(lambda dm=dm, hsl=hsl, sl_t=sl_t: nv.tensor_tensor(
                    hsl, ps_a[:, dm, :], bcst[:, 0, sl_t], ALU.mult), inc=None)
                Vr(lambda dm=dm, hsl=hsl, sl_t=sl_t: nv.scalar_tensor_tensor(
                    hsl, bcst[:, 1, sl_t], rsS[:, dm:dm + 1], hsl, ALU.mult, ALU.add), inc=None)
                Vr(lambda dm=dm, hsl=hsl: nv.tensor_scalar_add(hsl, hsl, binS[:, dm:dm + 1]))
            epi_done_v = p.cnt["V"]

        mark("P2_win")
        if "h0" in taps:
            p.wait("gpsimd", "V")
            tap_out("h0", h[:, :, :])

        hf = h[:, :, :]
        yf = y[:, :, :]

        def inner_ln(lay, wkey, bkey, act, y_free_t=0):
            nonlocal stfree_v, epi_done_v
            p.wait("scalar", "V")
            p.wait("scalar", "T", y_free_t)
            sq_s = Sc(lambda: ns.activation(yf, hf, AF.Square))
            for tok in range(NTOK):
                p.wait("tensor", "S", sq_s)
                p.wait("tensor", "V", stfree_v)
                for kt in range(GD):
                    Tr(lambda kt=kt, tok=tok: ntn.matmul(
                        ps_st[:, 0, :], onec[:], h[:, kt, tok * TT:(tok + 1) * TT],
                        start=(kt == 0), stop=(kt == GD - 1)), inc=None)
                for kt in range(GD):
                    Tr(lambda kt=kt, tok=tok: ntn.matmul(
                        ps_st[:, 1, :], onec[:], y[:, kt, tok * TT:(tok + 1) * TT],
                        start=(kt == 0), stop=(kt == GD - 1)), inc=None)
                mm_fence(ps_st[0:1, 0, 0:1])
                bc_v, stfree_v = ln_stats_and_bcast(tok, float(D))
            rb = bcst[:, 0:1, :].broadcast_to([128, GD, TC])
            mb = bcst[:, 1:2, :].broadcast_to([128, GD, TC])
            Vr(lambda: nv.tensor_tensor(yf, hf, rb, ALU.mult), inc=None)
            Vr(lambda: nv.tensor_tensor(yf, yf, mb, ALU.subtract))
            if act == AF.Copy:
                for g in range(GD):
                    wcol = prm[wkey][:, lay, g:g + 1]
                    bb = prm[bkey][:, lay, g:g + 1].broadcast_to([128, TC])
                    Vr(lambda g=g, wcol=wcol, bb=bb: nv.scalar_tensor_tensor(
                        y[:, g, :], y[:, g, :], wcol, bb, ALU.mult, ALU.add))
                return ("V", p.cnt["V"])
            p.wait("scalar", "V")
            for g in range(GD):
                Sc(lambda g=g: ns.activation(
                    y[:, g, :], y[:, g, :], act,
                    bias=prm[bkey][:, lay, g:g + 1], scale=prm[wkey][:, lay, g:g + 1]))
            return ("S", p.cnt["S"])

        prev_mm1_t = 0
        w2marks = [0, 0]
        w2i = 0
        for lay in range(DEPTH if stage >= 2 else 0):
            p.wait("sync", "T", prev_mm1_t)
            w1_ld = sring.load(w1t[:, :, :], dd["w1T"][lay].rearrange("(kt q) f -> q kt f", q=128))

            s_ln1 = inner_ln(lay, "ln1w", "ln1b", AF.Silu, y_free_t=prev_mm1_t)
            mark(f"L{lay}_ln1")
            if lay == 0 and "y0" in taps:
                p.wait("gpsimd", s_ln1[0], s_ln1[1])
                tap_out("y0", yf)

            p.wait("vector", s_ln1[0], s_ln1[1])
            for g in range(GD):
                build_tables(lay, g, with_rho=True)
                Vr(lambda g=g: nv.tensor_tensor(u[:, 0, :], tabs[:, 0, :], y[:, g, :], ALU.mult), inc=None)
                Vr(lambda g=g: nv.reduce_sum(ES[:, g:g + 1], u[:, 0, :], axis=mybir.AxisListType.X), inc=None)
                Vr(lambda g=g: nv.tensor_tensor(u[:, 1, :], tabs[:, 1, :], y[:, g, :], ALU.mult), inc=None)
                Vr(lambda g=g: nv.reduce_sum(ES[:, 4 + g:5 + g], u[:, 1, :], axis=mybir.AxisListType.X), inc=None)
            Vr(lambda lay=lay: nv.tensor_tensor(ES[:, 4:8], ES[:, 4:8], pp["sgn"][:, lay, :], ALU.mult))
            e_v = p.cnt["V"]

            p.wait("gpsimd", "V", e_v)
            gp1 = gp_load(ag_in.ap().rearrange("o (q k) -> (o q) k", q=128), ES[:])
            p.wait("gpsimd", "GDm", gp1)
            cc_m = Gp(lambda: ng.collective_compute(
                "AllGather", ALU.bypass, replica_groups=[list(range(NCORES))],
                ins=[ag_in.ap().opt()], outs=[ag_out.ap().opt()]))
            p.wait("gpsimd", "G", cc_m)
            gp2 = gp_load(EA[:, :, :], ag_out.ap()[0:7, :].rearrange("j (q k) -> q j k", q=128))

            p.wait("vector", "GDm", gp2)
            Vr(lambda lay=lay: nv.tensor_copy(CCs[:, 0:4], prm["lcr"][:, lay, :]), inc=None)
            Vr(lambda lay=lay: nv.tensor_copy(CCs[:, 4:8], prm["lci"][:, lay, :]), inc=None)
            for j in range(7):
                Cr, Ci = CCs[:, 0:4], CCs[:, 4:8]
                p2r_l, p2i_l = pp["p2r"][:, lay, :], pp["p2i"][:, lay, :]
                Vr(lambda Cr=Cr, p2r_l=p2r_l: nv.tensor_tensor(c8a[:, 0:4], Cr, p2r_l, ALU.mult), inc=None)
                Vr(lambda Ci=Ci, p2i_l=p2i_l: nv.tensor_tensor(c8b[:, 0:4], Ci, p2i_l, ALU.mult), inc=None)
                Vr(lambda: nv.tensor_tensor(c8a[:, 0:4], c8a[:, 0:4], c8b[:, 0:4], ALU.subtract), inc=None)
                Vr(lambda j=j: nv.tensor_tensor(c8a[:, 0:4], c8a[:, 0:4], EA[:, j, 0:4], ALU.add), inc=None)
                Vr(lambda Cr=Cr, p2i_l=p2i_l: nv.tensor_tensor(c8b[:, 0:4], Cr, p2i_l, ALU.mult), inc=None)
                Vr(lambda Ci=Ci, p2r_l=p2r_l: nv.tensor_tensor(c8b[:, 4:8], Ci, p2r_l, ALU.mult), inc=None)
                Vr(lambda: nv.tensor_tensor(c8a[:, 4:8], c8b[:, 0:4], c8b[:, 4:8], ALU.add), inc=None)
                Vr(lambda j=j: nv.tensor_tensor(c8a[:, 4:8], c8a[:, 4:8], EA[:, j, 4:8], ALU.add), inc=None)
                Vr(lambda: nv.tensor_tensor(c8b[:, 0:8], c8a[:, 0:8], CCs[:, 0:8], ALU.subtract), inc=None)
                Vr(lambda j=j: nv.scalar_tensor_tensor(
                    CCs[:, 0:8], c8b[:, 0:8], selS[:, j:j + 1], CCs[:, 0:8], ALU.mult, ALU.add), inc=None)
            eRl, eIl = pp["eR"][:, lay, :], pp["eI"][:, lay, :]
            Vr(lambda eRl=eRl: nv.tensor_tensor(WIs[:, 0:4], CCs[:, 0:4], eRl, ALU.mult), inc=None)
            Vr(lambda eIl=eIl: nv.tensor_tensor(c8a[:, 0:4], CCs[:, 4:8], eIl, ALU.mult), inc=None)
            Vr(lambda: nv.tensor_tensor(WIs[:, 0:4], WIs[:, 0:4], c8a[:, 0:4], ALU.subtract), inc=None)
            Vr(lambda eRl=eRl: nv.tensor_tensor(WIs[:, 4:8], CCs[:, 4:8], eRl, ALU.mult), inc=None)
            Vr(lambda eIl=eIl: nv.tensor_tensor(c8a[:, 4:8], CCs[:, 0:4], eIl, ALU.mult), inc=None)
            Vr(lambda: nv.tensor_tensor(WIs[:, 4:8], WIs[:, 4:8], c8a[:, 4:8], ALU.add), inc=None)
            Vr(lambda lay=lay: nv.tensor_tensor(WIs[:, 4:8], WIs[:, 4:8], pp["sgn"][:, lay, :], ALU.mult))

            for g in range(GD):
                build_tables(lay, g, with_rho=False)
                rho_b = pp["rho"][:, lay, g:g + 1].broadcast_to([128, TC])
                Vr(lambda g=g: nv.tensor_tensor(u[:, 0, :], tabs[:, 0, :], y[:, g, :], ALU.mult), inc=None)
                Vr(lambda g=g: nv.tensor_tensor(u[:, 1, :], tabs[:, 1, :], y[:, g, :], ALU.mult), inc=None)
                Vr(lambda g=g, rho_b=rho_b: nv.tensor_tensor_scan(
                    u[:, 0, :], rho_b, u[:, 0, :], WIs[:, g:g + 1], ALU.mult, ALU.add), inc=None)
                Vr(lambda g=g, rho_b=rho_b: nv.tensor_tensor_scan(
                    u[:, 1, :], rho_b, u[:, 1, :], WIs[:, 4 + g:5 + g], ALU.mult, ALU.subtract), inc=None)
                Vr(lambda g=g: nv.tensor_tensor(y[:, g, :], tabs[:, 0, :], u[:, 0, :], ALU.mult), inc=None)
                Vr(lambda g=g: nv.tensor_tensor(u[:, 0, :], tabs[:, 1, :], u[:, 1, :], ALU.mult), inc=None)
                Vr(lambda g=g: nv.tensor_tensor(y[:, g, :], y[:, g, :], u[:, 0, :], ALU.subtract), inc=None)
            Vr(lambda: nv.tensor_copy(c8a[:, 0:1], CCs[:, 0:1]))
            if lay == 0 and "conv0" in taps:
                p.wait("gpsimd", "V")
                tap_out("conv0", yf)

            for g in range(GD):
                Vr(lambda g=g, lay=lay: nv.scalar_tensor_tensor(
                    h[:, g, :], y[:, g, :], prm["scl"][:, lay, g:g + 1], h[:, g, :],
                    ALU.mult, ALU.add), inc=None)
            Vr(lambda: nv.tensor_copy(c8a[:, 0:1], CCs[:, 0:1]))

            mark(f"L{lay}_scan")
            s_ln2 = inner_ln(lay, "ln2w", "ln2b", AF.Copy)
            mark(f"L{lay}_ln2")

            evac_s = [0, 0, 0]
            mm2_t = 0
            hupd_v = p.cnt["V"]
            sring.wait("tensor", w1_ld)
            p.wait("tensor", s_ln2[0], s_ln2[1])
            for tok in range(NTOK):
                p.wait("tensor", "V", hupd_v)

                def emit_mm2(fk, sl, pb, wl):
                    sring.wait("tensor", wl)
                    p.wait("tensor", "S", evac_s[pb])
                    for dm in range(GD):
                        Tr(lambda fk=fk, sl=sl, dm=dm, pb=pb: ntn.matmul(
                            ps_a[:, dm, :], w2b[:, sl, dm * 128:(dm + 1) * 128],
                            y1[:, pb, :],
                            start=(fk == 0), stop=(fk == KF - 1)),
                           inc=("T" if dm == GD - 1 else None))
                    w2marks[:] = [w2marks[1], p.cnt["T"]]

                pend = None
                for fk in range(KF):
                    sl = w2i % 3
                    w2i += 1
                    p.wait("sync", "T", w2marks[0])
                    wl = sring.load(w2b[:, sl, :],
                                    dd["w2T"][lay][fk * 128:(fk + 1) * 128, :])
                    pb = fk % 2
                    p.wait("tensor", "S", evac_s[pb])
                    for kt in range(GD):
                        Tr(lambda lay=lay, fk=fk, kt=kt, pb=pb, tok=tok: ntn.matmul(
                            ps_b[:, pb, :], w1t[:, kt, fk * 128:(fk + 1) * 128],
                            y[:, kt, tok * TT:(tok + 1) * TT],
                            start=(kt == 0), stop=(kt == GD - 1)), inc=None)
                    mm1_t = mm_fence(ps_b[0:1, pb, 0:1])
                    p.wait("scalar", "T", mm1_t)
                    evac_s[pb] = Sc(lambda lay=lay, fk=fk, pb=pb: ns.activation(
                        y1[:, pb, :], ps_b[:, pb, :], AF.Silu,
                        bias=b1S[:, lay, fk:fk + 1], scale=1.0))
                    if pend is not None:
                        emit_mm2(*pend)
                    pend = (fk, sl, pb, wl)
                emit_mm2(*pend)
                mm2_t = mm_fence(ps_a[0:1, 0, 0:1])
                p.wait("vector", "T", mm2_t)
                for dm in range(GD):
                    hsl = h[:, dm, tok * TT:(tok + 1) * TT]
                    Vr(lambda dm=dm, lay=lay, hsl=hsl: nv.scalar_tensor_tensor(
                        hsl, ps_a[:, dm, :], b2S[:, lay, dm:dm + 1], hsl,
                        ALU.add, ALU.add), inc=None)
                Vr(lambda: nv.tensor_copy(c8a[:, 0:1], CCs[:, 0:1]))
                hupd_v = p.cnt["V"]
            prev_mm1_t = p.cnt["T"]
            mark(f"L{lay}_ffn")

            if lay == 0 and "h1" in taps:
                p.wait("gpsimd", "V", hupd_v)
                tap_out("h1", hf)

        if stage >= 2 and "hfin" in taps:
            p.wait("gpsimd", "V")
            tap_out("hfin", hf)

        if stage >= 3:
            def final_ln():
                nonlocal stfree_v
                p.wait("scalar", "V")
                p.wait("scalar", "T", prev_mm1_t if stage >= 2 else 0)
                sq_s = Sc(lambda: ns.activation(yf, hf, AF.Square))
                for tok in range(NTOK):
                    p.wait("tensor", "S", sq_s)
                    p.wait("tensor", "V", stfree_v)
                    for kt in range(GD):
                        Tr(lambda kt=kt, tok=tok: ntn.matmul(
                            ps_st[:, 0, :], onec[:], h[:, kt, tok * TT:(tok + 1) * TT],
                            start=(kt == 0), stop=(kt == GD - 1)), inc=None)
                    for kt in range(GD):
                        Tr(lambda kt=kt, tok=tok: ntn.matmul(
                            ps_st[:, 1, :], onec[:], y[:, kt, tok * TT:(tok + 1) * TT],
                            start=(kt == 0), stop=(kt == GD - 1)),
                           inc=("T" if kt == GD - 1 else None))
                    bc_v, stfree_v = ln_stats_and_bcast(tok, float(D))
                rb = bcst[:, 0:1, :].broadcast_to([128, GD, TC])
                mbb = bcst[:, 1:2, :].broadcast_to([128, GD, TC])
                Vr(lambda: nv.tensor_tensor(yf, hf, rb, ALU.mult), inc=None)
                Vr(lambda: nv.tensor_tensor(yf, yf, mbb, ALU.subtract))
                for g in range(GD):
                    bb = lnlbS[:, g:g + 1].broadcast_to([128, TC])
                    Vr(lambda g=g, bb=bb: nv.scalar_tensor_tensor(
                        y[:, g, :], y[:, g, :], lnlwS[:, g:g + 1], bb,
                        ALU.mult, ALU.add))

            final_ln()
            mark("P4_ln")
            if "hln" in taps:
                p.wait("gpsimd", "V")
                tap_out("hln", yf)

            def x1sb(k):
                buf = h if k < 16 else w1t
                kk = k % 16
                return buf[:, kk // 4, (kk % 4) * TT:(kk % 4 + 1) * TT]

            hln_v = p.cnt["V"]
            p.wait("sync", "T", p.cnt["T"])
            wout_mark = [hln_v] * 4
            wt_marks = [0, 0]
            xe_mark = [0, 0, 0]
            sq2_marks = [0, 0, 0]
            stfree2 = 0
            ab_marks = [0, 0]
            rs_evac = 0
            a_epi_v = 0
            rsA_gd = 0
            a_out_mark = [None, None]
            ai = 0
            for tok in range(NTOK):
                sl_t = slice(tok * TT, (tok + 1) * TT)
                p.wait("tensor", "V", stfree_v)
                for vm in range(KV):
                    sl = vm % 3
                    pa = vm % 4
                    p.wait("sync", "T", wt_marks[0])
                    wl = sring.load(wtb[:, sl, :],
                                    dd["woutT"][:, vm * 128:(vm + 1) * 128].rearrange(
                                        "(kt q) m -> q kt m", q=128))
                    p.wait("gpsimd", "V", xe_mark[sl])
                    xl = ering.load(xeb[:, sl, :],
                                    dd["xT"][vm * 128:(vm + 1) * 128, sl_t])
                    sring.wait("tensor", wl)
                    p.wait("tensor", "V", wout_mark[pa])
                    for kt in range(GD):
                        Tr(lambda vm=vm, sl=sl, kt=kt, pa=pa, sl_t=sl_t: ntn.matmul(
                            ps_a[:, pa, :], wtb[:, sl, kt * 128:(kt + 1) * 128],
                            y[:, kt, sl_t],
                            start=(kt == 0), stop=(kt == GD - 1)), inc=None)
                    wt_marks = [wt_marks[1], mm_fence(ps_a[0:1, pa, 0:1])]
                    x1t = x1sb(vm)
                    p.wait("vector", "T")
                    ering.wait("vector", xl)
                    Vr(lambda vm=vm, sl=sl, pa=pa, x1t=x1t: nv.scalar_tensor_tensor(
                        x1t, ps_a[:, pa, :], boutS[:, vm:vm + 1], xeb[:, sl, :],
                        ALU.add, ALU.add))
                    epi_v = p.cnt["V"]
                    wout_mark[pa] = epi_v
                    xe_mark[sl] = epi_v
                    su = vm % 3
                    p.wait("scalar", "V", epi_v)
                    p.wait("scalar", "T", sq2_marks[su])
                    sqs = Sc(lambda su=su, x1t=x1t: ns.activation(
                        u[:, 0, su * TT:(su + 1) * TT], x1t, AF.Square))
                    p.wait("tensor", "S", sqs)
                    if vm == 0:
                        p.wait("tensor", "V", stfree2)
                    Tr(lambda vm=vm, x1t=x1t: ntn.matmul(
                        ps_st[:, 0, :], onec[:], x1t,
                        start=(vm == 0), stop=(vm == KV - 1)), inc=None)
                    sq2_marks[su] = Tr(lambda vm=vm, su=su: ntn.matmul(
                        ps_st[:, 1, :], onec[:], u[:, 0, su * TT:(su + 1) * TT],
                        start=(vm == 0), stop=(vm == KV - 1)), inc="T")
                mm_fence(ps_st[0:1, 0, 0:1])
                bc_v, stfree2 = ln_stats_and_bcast(tok, float(V))

                if "x1" in taps:
                    p.wait("gpsimd", "V", bc_v)
                    for vm in range(KV):
                        gp_load(d_tap["x1"][vm * 128:(vm + 1) * 128, sl_t], x1sb(vm))
                    p.wait("vector", "GDm")

                if stage < 4:
                    continue
                for mb in range(8):
                    p.wait("tensor", "V", bc_v if mb == 0 else a_epi_v)
                    for kb in range(KV // 4):
                        j = ai % 2
                        ai += 1
                        p.wait("sync", "T", ab_marks[0])
                        al = sring.load(
                            ab4[:, j],
                            dd["aT"][kb * 512:(kb + 1) * 512,
                                     mb * TT:(mb + 1) * TT].rearrange(
                                         "(c q) m -> q c m", q=128))
                        sring.wait("tensor", al)
                        if tok == 0 and kb == 0:
                            p.wait("tensor", "V", rs_evac)
                        for c in range(4):
                            k = kb * 4 + c
                            for mt in range(4):
                                Tr(lambda j=j, c=c, mt=mt, k=k: ntn.matmul(
                                    ps_a[:, mt, :], ab4[:, j, c, mt * 128:(mt + 1) * 128],
                                    x1sb(k),
                                    start=(k == 0), stop=(k == KV - 1)),
                                   inc=(None if (mt < 3 or tok == 0 or c < 3) else "T"))
                            if tok == 0:
                                Tr(lambda j=j, c=c, k=k: ntn.matmul(
                                    ps_b[0:1, 0, :], nonec[:], ab4[:, j, c, :],
                                    start=(k == 0), stop=(k == KV - 1)),
                                   inc=("T" if c == 3 else None))
                        ab_marks = ab_marks[1:] + [p.cnt["T"]]
                    mm_t = mm_fence(ps_a[0:1, 0, 0:1])
                    ab_marks = ab_marks[1:] + [mm_t]
                    if tok == 0:
                        p.wait("vector", "T", mm_t)
                        p.wait("vector", "GDm")
                        rs_evac = Vr(lambda: nv.tensor_copy(rowb[:, 1, :], ps_b[0:1, 0, :]))
                        p.wait("gpsimd", "V", rs_evac)
                        gb1 = gp_load(bounce.ap()[0:1, mb * TT:(mb + 1) * TT], rowb[0:1, 1, :])
                        p.wait("gpsimd", "GDm", gb1)
                        rsA_gd = gp_load(
                            rsAS[:, mb * 4:(mb + 1) * 4],
                            bounce.ap()[0, mb * TT:(mb + 1) * TT].rearrange("(g q) -> q g", q=128),
                            slow=True)
                    p.wait("vector", "T", mm_t)
                    if tok == 0:
                        p.wait("vector", "GDm", rsA_gd)
                    for mt in range(4):
                        vm = mb * 4 + mt
                        sl2 = mt % 2
                        esl = u[:, 1, sl2 * TT:(sl2 + 1) * TT]
                        if a_out_mark[sl2] is not None:
                            p.wait("vector", a_out_mark[sl2][0], a_out_mark[sl2][1])
                        Vr(lambda mt=mt, esl=esl, sl_t=sl_t: nv.tensor_tensor(
                            esl, ps_a[:, mt, :], bcst[:, 0, sl_t], ALU.mult), inc=None)
                        Vr(lambda vm=vm, esl=esl, sl_t=sl_t: nv.scalar_tensor_tensor(
                            esl, bcst[:, 1, sl_t], rsAS[:, vm:vm + 1], esl,
                            ALU.mult, ALU.add), inc=None)
                        Vr(lambda vm=vm, esl=esl: nv.scalar_tensor_tensor(
                            esl, esl, bAS[:, vm:vm + 1], x1sb(vm),
                            ALU.add, ALU.add))
                        ep_v = p.cnt["V"]
                        p.wait("scalar", "V", ep_v)
                        onm = "O0" if sl2 % 2 == 0 else "O1"
                        p.wait("scalar", onm, p.cnt[onm])
                        p.run("scalar", lambda vm=vm, esl=esl, sl_t=sl_t:
                              nc.scalar.dma_start(out=outT[vm * 128:(vm + 1) * 128, sl_t], in_=esl),
                              onm, 16)
                        a_out_mark[sl2] = (onm, p.cnt[onm])
                    a_epi_v = p.cnt["V"]
                wout_mark = [a_epi_v] * 4
            mark("P4_wout")

        if stage >= 4:
            mark("P5_A")
        p.wait("sync", "O0", p.cnt["O0"])
        p.wait("sync", "O1", p.cnt["O1"])
        p.wait("sync", "GDm", p.cnt["GDm"])
        p.emit()
    return nc, d_tap


def kernel(**inputs):
    in_maps = stage_inputs(inputs)
    nc, _ = build_program(taps=DEBUG_TAPS, stage=STAGE)
    res = run_bass_kernel_spmd(nc, in_maps, core_ids=list(range(NCORES)))
    out = np.empty((B, L, V), dtype=np.float32)
    for c in range(NCORES):
        b, q = c // QPER, c % QPER
        out[b, q * TC:(q + 1) * TC, :] = res.results[c]["outT"].T
    return out


# version 08
# speedup vs baseline: 1.0703x; 1.0499x over previous
from contextlib import ExitStack

import numpy as np
import concourse.bass as bass
import concourse.mybir as mybir
from concourse.bass_utils import run_bass_kernel_spmd

F32 = mybir.dt.float32
I32 = mybir.dt.int32
AF = mybir.ActivationFunctionType
ALU = mybir.AluOpType

B, L, V, D, FF, DEPTH = 2, 8192, 4096, 512, 2048, 4
NCORES, QPER = 8, 4
TC = L // QPER
TT = 512
NTOK = TC // TT
KV = V // 128
GD = D // 128
KF = FF // 128
PI = float(np.pi)
TWO_PI = float(2 * np.pi)
INV_2PI = float(1.0 / (2 * np.pi))
PI_LO = float(np.float32(3.1415925))
LN_EPS = 1e-5

ENGINES = ("tensor", "vector", "scalar", "gpsimd", "sync")

DEBUG_TAPS = ()
STAGE = 4
PHASE_MARKS = []


class Prog:

    SELF = {"vector": "V", "scalar": "S", "gpsimd": "G"}

    def __init__(self, nc, stack, sem_names):
        self.nc = nc
        self.q = {e: [] for e in ENGINES}
        self.sem = {n: stack.enter_context(nc.semaphore(n)) for n in sem_names}
        self.cnt = {n: 0 for n in sem_names}

    def run(self, eng, fn, inc=None, n=1):
        selfsem = self.SELF.get(eng)
        if selfsem is not None and inc is None:
            inc, n = selfsem, 1
        if inc is not None:
            if selfsem is not None and inc == selfsem:
                prev = self.cnt[inc]
                if prev:
                    self.q[eng].append(("w", self.sem[inc], prev))
            self.cnt[inc] += n
            sem = self.sem[inc]
            self.q[eng].append(lambda fn=fn, sem=sem, n=n: fn().then_inc(sem, n))
            return self.cnt[inc]
        self.q[eng].append(fn)
        return None

    def wait(self, eng, name, val=None):
        v = self.cnt[name] if val is None else val
        if v <= 0:
            return
        self.q[eng].append(("w", self.sem[name], v))

    def emit(self):
        with self.nc.Block() as blk:
            for e in ENGINES:
                def body(engobj, e=e):
                    for item in self.q[e]:
                        if isinstance(item, tuple):
                            engobj.wait_ge(item[1], item[2])
                        else:
                            item()
                getattr(blk, e)(body)


def stage_inputs(inp):
    f = np.ascontiguousarray
    x = np.asarray(inp["L0_attn_x"], dtype=np.float32)

    def ch(v):
        return f(np.asarray(v, np.float32).reshape(GD, 128).T)

    def chl(v):
        return f(np.asarray(v, np.float32).reshape(DEPTH, GD, 128).transpose(2, 0, 1))

    def vw(v, g):
        return f(np.asarray(v, np.float32).reshape(g, 128).T)

    common = {
        "winT": f(np.asarray(inp["L0_attn_Win"], np.float32).T),
        "w1T": f(np.asarray(inp["L0_attn_W1"], np.float32).transpose(0, 2, 1)),
        "w2T": f(np.asarray(inp["L0_attn_W2"], np.float32).transpose(0, 2, 1)),
        "woutT": f(np.asarray(inp["L0_attn_Wout"], np.float32).T),
        "aT": f(np.asarray(inp["L0_ffn_A"], np.float32).T),
        "bin": ch(inp["L0_attn_bin"]),
        "b1": f(np.asarray(inp["L0_attn_b1"], np.float32).reshape(DEPTH, KF, 128).transpose(2, 0, 1)),
        "b2": chl(inp["L0_attn_b2"]),
        "ln1w": chl(inp["L0_attn_ln1_w"]), "ln1b": chl(inp["L0_attn_ln1_b"]),
        "ln2w": chl(inp["L0_attn_ln2_w"]), "ln2b": chl(inp["L0_attn_ln2_b"]),
        "phzr": chl(inp["L0_attn_phz_re"]), "phzi": chl(inp["L0_attn_phz_im"]),
        "lcr": chl(inp["L0_attn_lc_re"]), "lci": chl(inp["L0_attn_lc_im"]),
        "scl": chl(inp["L0_attn_sc_lin"]),
        "lnlw": ch(inp["L0_attn_lnl_w"]), "lnlb": ch(inp["L0_attn_lnl_b"]),
        "bout": vw(inp["L0_attn_bout"], KV),
        "bA": vw(inp["L0_ffn_b"], KV),
        "iota": f(np.broadcast_to(np.arange(TC, dtype=np.float32), (128, TC)).copy()),
    }
    in_maps = []
    for c in range(NCORES):
        b, q = c // QPER, c % QPER
        xT = f(x[b, q * TC:(q + 1) * TC, :].T)
        sel = np.zeros((128, 7), np.float32)
        for j in range(7):
            if QPER * b <= j < QPER * b + q:
                sel[:, j] = 1.0
        in_maps.append(dict(common, xT=xT, sel=sel))
    return in_maps


def build_program(taps=(), stage=4):
    PHASE_MARKS.clear()
    nc = bass.Bass("TRN2", target_bir_lowering=False)
    taps = set(taps)
    nv, ns, ntn, ng = nc.vector, nc.scalar, nc.tensor, nc.gpsimd

    dd = {}
    for name, shape in [
        ("xT", [V, TC]), ("winT", [V, D]), ("w1T", [DEPTH, D, FF]), ("w2T", [DEPTH, FF, D]),
        ("woutT", [D, V]), ("aT", [V, V]), ("bin", [128, GD]),
        ("b1", [128, DEPTH, KF]), ("b2", [128, DEPTH, GD]),
        ("ln1w", [128, DEPTH, GD]), ("ln1b", [128, DEPTH, GD]),
        ("ln2w", [128, DEPTH, GD]), ("ln2b", [128, DEPTH, GD]),
        ("phzr", [128, DEPTH, GD]), ("phzi", [128, DEPTH, GD]),
        ("lcr", [128, DEPTH, GD]), ("lci", [128, DEPTH, GD]),
        ("scl", [128, DEPTH, GD]), ("lnlw", [128, GD]), ("lnlb", [128, GD]),
        ("bout", [128, KV]), ("bA", [128, KV]), ("iota", [128, TC]), ("sel", [128, 7]),
    ]:
        dd[name] = nc.dram_tensor(name, shape, F32, kind="ExternalInput").ap()

    outT = nc.dram_tensor("outT", [V, TC], F32, kind="ExternalOutput").ap() if stage >= 4 else None
    d_tap = {}
    for t in taps:
        shp = {"h0": [128, GD * TC], "y0": [128, GD * TC], "conv0": [128, GD * TC],
               "h1": [128, GD * TC], "hfin": [128, GD * TC], "hln": [128, GD * TC],
               "x1": [V, TC]}[t]
        d_tap[t] = nc.dram_tensor("tap_" + t, shp, F32, kind="ExternalOutput").ap()

    ag_in = nc.dram_tensor("ag_in", [1, 1024], F32)
    ag_out = nc.dram_tensor("ag_out", [NCORES, 1024], F32)
    bounce = nc.dram_tensor("bounce", [1, V], F32)

    st = ExitStack()
    with st:
        def sbt(name, shape, dt=F32):
            return st.enter_context(nc.sbuf_tensor(name, shape, dt))

        h = sbt("h", [128, GD, TC])
        y = sbt("y", [128, GD, TC])
        tabs = sbt("tabs", [128, 2, TC])
        u = sbt("u", [128, 2, TC])
        bcst = sbt("bcst", [128, 2, TC])
        y1 = sbt("y1", [128, 2, TT])
        w1t = sbt("w1t", [128, GD, FF])
        w2b = sbt("w2b", [128, 3, TT])
        xb = sbt("xb", [128, 3, TT])
        wtb = sbt("wtb", [128, 3, TT])
        ab4 = sbt("ab4", [128, 2, 4, TT])
        xeb = sbt("xeb", [128, 3, TT])
        rowb = sbt("rowb", [1, 2, TT])
        iot = sbt("iot", [128, TC])
        onec = sbt("onec", [128, 1])
        zerc = sbt("zerc", [128, 1])
        nonec = sbt("nonec", [128, 1])
        oner = sbt("oner", [1, 128])
        binS = sbt("binS", [128, GD])
        b1S = sbt("b1S", [128, DEPTH, KF])
        b2S = sbt("b2S", [128, DEPTH, GD])
        prm = {k: sbt(k + "S", [128, DEPTH, GD]) for k in
               ["ln1w", "ln1b", "ln2w", "ln2b", "phzr", "phzi", "lcr", "lci", "scl"]}
        lnlwS = sbt("lnlwS", [128, GD])
        lnlbS = sbt("lnlbS", [128, GD])
        boutS = sbt("boutS", [128, KV])
        bAS = sbt("bAS", [128, KV])
        selS = sbt("selS", [128, 7])
        rsS = sbt("rsS", [128, GD])
        rsAS = sbt("rsAS", [128, KV])
        pp = {k: sbt("pp_" + k, [128, DEPTH, GD]) for k in
              ["rho", "lnrho", "absth", "sgn", "eR", "eI", "p2r", "p2i",
               "ln2047", "th2047", "c47", "s47", "nsgn",
               "c256", "s256", "c512", "s512", "c1024", "s1024",
               "kr256", "ki256", "kr512", "ki512", "kr1024", "ki1024"]}
        t16a = sbt("t16a", [128, DEPTH * GD])
        t16b = sbt("t16b", [128, DEPTH * GD])
        t16c = sbt("t16c", [128, DEPTH * GD])
        t16d = sbt("t16d", [128, DEPTH * GD])
        fcell = sbt("fcell", [1, 8])
        ES = sbt("ES", [128, 8])
        EA = sbt("EA", [128, 7, 8])
        CCs = sbt("CCs", [128, 8])
        WIs = sbt("WIs", [128, 8])
        c8a = sbt("c8a", [128, 8])
        c8b = sbt("c8b", [128, 8])

        ps_a = st.enter_context(nc.psum_tensor("ps_a", [128, 4, TT], F32))
        ps_b = st.enter_context(nc.psum_tensor("ps_b", [128, 2, TT], F32))
        ps_st = st.enter_context(nc.psum_tensor("ps_st", [1, 2, TT], F32))

        p = Prog(nc, st, ["T", "V", "S", "G", "GDm",
                          "L0", "L1", "L2", "L3", "L4", "L5",
                          "E0", "E1", "E2", "O0", "O1"])

        def Vr(fn, inc="V", n=1):
            return p.run("vector", fn, None, 1)

        def Sc(fn, inc="S", n=1):
            return p.run("scalar", fn, None, 1)

        def Tr(fn, inc=None, n=1):
            return p.run("tensor", fn, inc, n)

        def Gp(fn, inc=None, n=1):
            if inc is None:
                return p.run("gpsimd", fn, None, 1)
            return p.run("gpsimd", fn, inc, n)

        def Sy(fn, inc=None, n=1):
            return p.run("sync", fn, inc, n)

        def mark(name):
            PHASE_MARKS.append((name, dict(p.cnt)))

        class Ring:

            def __init__(self, eng, names):
                self.eng, self.names = eng, names
                self.i = 0

            def load_group(self, pairs):
                nm = self.names[self.i % len(self.names)]
                self.i += 1
                p.wait(self.eng, nm, p.cnt[nm])
                for dst_ap, src_ap in pairs:
                    if self.eng == "sync":
                        p.run("sync", lambda dst_ap=dst_ap, src_ap=src_ap:
                              nc.sync.dma_start(out=dst_ap, in_=src_ap), nm, 16)
                    else:
                        p.run("gpsimd", lambda dst_ap=dst_ap, src_ap=src_ap:
                              ng.dma_start(out=dst_ap, in_=src_ap), nm, 16)
                return (nm, p.cnt[nm])

            def load(self, dst_ap, src_ap, slow=False):
                nm = self.names[self.i % len(self.names)]
                self.i += 1
                p.wait(self.eng, nm, p.cnt[nm])
                if self.eng == "sync":
                    p.run("sync", lambda dst_ap=dst_ap, src_ap=src_ap:
                          nc.sync.dma_start(out=dst_ap, in_=src_ap), nm, 16)
                else:
                    p.run("gpsimd", lambda dst_ap=dst_ap, src_ap=src_ap, slow=slow:
                          ng.dma_start(out=dst_ap, in_=src_ap,
                                       allow_slow_non_contiguous=slow), nm, 16)
                return (nm, p.cnt[nm])

            def wait(self, eng, mark):
                p.wait(eng, mark[0], mark[1])

        sring = Ring("sync", ["L0", "L1", "L2", "L3", "L4", "L5"])
        ering = Ring("gpsimd", ["E0", "E1", "E2"])

        def gp_load(dst_ap, src_ap, slow=False):
            p.wait("gpsimd", "GDm", p.cnt["GDm"])
            return Gp(lambda dst_ap=dst_ap, src_ap=src_ap, slow=slow:
                      ng.dma_start(out=dst_ap, in_=src_ap,
                                   allow_slow_non_contiguous=slow), inc="GDm", n=16)

        def tap_out(name, src):
            m = gp_load(d_tap[name], src)
            p.wait("vector", "GDm", m)
            p.wait("scalar", "GDm", m)
            p.wait("tensor", "GDm", m)

        def mm_fence(bank_ap):
            Tr(lambda bank_ap=bank_ap: ntn.matmul(
                bank_ap, zerc[:, 0:1], onec[:, 0:1],
                start=False, stop=True, skip_group_check=True), inc=None)
            return Tr(lambda bank_ap=bank_ap: ntn.matmul(
                bank_ap, zerc[:, 0:1], onec[:, 0:1],
                start=False, stop=True, skip_group_check=True), inc="T")

        def range_reduce(dst, src, iscr):
            ii = iscr.bitcast(I32)
            Vr(lambda: nv.tensor_single_scalar(dst, src, INV_2PI, ALU.mult), inc=None)
            Vr(lambda: nv.tensor_copy(ii, dst), inc=None)
            Vr(lambda: nv.tensor_copy(iscr, ii), inc=None)
            Vr(lambda: nv.scalar_tensor_tensor(dst, iscr, -TWO_PI, src, ALU.mult, ALU.add), inc=None)
            Vr(lambda: nv.tensor_single_scalar(iscr, dst, PI, ALU.is_gt), inc=None)
            Vr(lambda: nv.scalar_tensor_tensor(dst, iscr, -TWO_PI, dst, ALU.mult, ALU.add), inc=None)
            Vr(lambda: nv.tensor_single_scalar(iscr, dst, -PI, ALU.is_lt), inc=None)
            Vr(lambda: nv.scalar_tensor_tensor(dst, iscr, TWO_PI, dst, ALU.mult, ALU.add), inc=None)
            Vr(lambda: nv.tensor_single_scalar(dst, dst, PI_LO, ALU.min), inc=None)
            Vr(lambda: nv.tensor_single_scalar(dst, dst, -PI_LO, ALU.max), inc=None)

        def cos_from_red(dst, red, mscr):
            Vr(lambda: nv.tensor_scalar(mscr, red, -1.0, PI / 2, ALU.mult, ALU.add), inc=None)
            Vr(lambda: nv.tensor_single_scalar(dst, mscr, PI, ALU.is_gt), inc=None)
            Vr(lambda: nv.scalar_tensor_tensor(mscr, dst, -TWO_PI, mscr, ALU.mult, ALU.add), inc=None)
            Vr(lambda: nv.tensor_single_scalar(mscr, mscr, PI_LO, ALU.min), inc=None)
            Vr(lambda: nv.tensor_single_scalar(mscr, mscr, -PI_LO, ALU.max))
            p.wait("scalar", "V")
            Sc(lambda: ns.activation(dst, mscr, AF.Sin))

        DBL = [(256, "c256", "s256", "kr256", "ki256"),
               (512, "c512", "s512", "kr512", "ki512"),
               (1024, "c1024", "s1024", "kr1024", "ki1024")]

        def build_ptab(lay, g, with_rho):
            ath = pp["absth"][:, lay, g:g + 1].broadcast_to([128, 256])
            Vr(lambda ath=ath: nv.tensor_tensor(
                u[:, 0, 0:256], iot[:, 0:256], ath, ALU.mult))
            range_reduce(u[:, 1, 0:256], u[:, 0, 0:256], tabs[:, 0, 0:256])
            cos_from_red(tabs[:, 0, 0:256], u[:, 1, 0:256], tabs[:, 1, 0:256])
            Sc(lambda: ns.activation(tabs[:, 1, 0:256], u[:, 1, 0:256], AF.Sin))
            if with_rho:
                lnr = pp["lnrho"][:, lay, g:g + 1].broadcast_to([128, 256])
                Vr(lambda lnr=lnr: nv.tensor_tensor(
                    u[:, 0, 0:256], iot[:, 0:256], lnr, ALU.mult))
                p.wait("scalar", "V")
                Sc(lambda: ns.activation(u[:, 0, 0:256], u[:, 0, 0:256], AF.Exp))
                p.wait("vector", "S")
                Vr(lambda: nv.tensor_tensor(
                    tabs[:, 0, 0:256], tabs[:, 0, 0:256], u[:, 0, 0:256], ALU.mult), inc=None)
                Vr(lambda: nv.tensor_tensor(
                    tabs[:, 1, 0:256], tabs[:, 1, 0:256], u[:, 0, 0:256], ALU.mult), inc=None)
            else:
                p.wait("vector", "S")
            for L, ck, sk, krk, kik in DBL:
                cc = pp[krk if with_rho else ck][:, lay, g:g + 1]
                ss = pp[kik if with_rho else sk][:, lay, g:g + 1]
                ssb = ss.broadcast_to([128, L])
                Vr(lambda L=L, ssb=ssb: nv.tensor_tensor(
                    u[:, 0, 0:L], tabs[:, 1, 0:L], ssb, ALU.mult), inc=None)
                Vr(lambda L=L, ssb=ssb: nv.tensor_tensor(
                    u[:, 1, 0:L], tabs[:, 0, 0:L], ssb, ALU.mult), inc=None)
                Vr(lambda L=L, cc=cc: nv.scalar_tensor_tensor(
                    tabs[:, 0, L:2 * L], tabs[:, 0, 0:L], cc, u[:, 0, 0:L],
                    ALU.mult, ALU.subtract), inc=None)
                Vr(lambda L=L, cc=cc: nv.scalar_tensor_tensor(
                    tabs[:, 1, L:2 * L], tabs[:, 1, 0:L], cc, u[:, 1, 0:L],
                    ALU.mult, ALU.add), inc=None)

        def ln_stats_and_bcast(tok, denom, extra_wait_v=None):
            mu, rr = rowb[:, 0, :], rowb[:, 1, :]
            p.wait("vector", "T")
            Vr(lambda: nv.tensor_single_scalar(mu, ps_st[:, 0, :], 1.0 / denom, ALU.mult), inc=None)
            Vr(lambda: nv.tensor_tensor(rr, mu, mu, ALU.mult), inc=None)
            Vr(lambda: nv.scalar_tensor_tensor(rr, ps_st[:, 1, :], 1.0 / denom, rr,
                                               ALU.mult, ALU.subtract))
            st_free = p.cnt["V"]
            Vr(lambda: nv.tensor_single_scalar(rr, rr, LN_EPS, ALU.add))
            p.wait("scalar", "V")
            Sc(lambda: ns.activation(rr, rr, AF.Sqrt))
            p.wait("vector", "S")
            Vr(lambda: nv.reciprocal(rr, rr), inc=None)
            Vr(lambda: nv.tensor_tensor(mu, rr, mu, ALU.mult))
            rows_done = p.cnt["V"]
            p.wait("tensor", "V", rows_done)
            if extra_wait_v is not None:
                p.wait("tensor", "V", extra_wait_v)
            p.wait("tensor", "S")
            Tr(lambda: ntn.matmul(ps_b[:, 0, :], oner[:], rr, start=True, stop=True), inc=None)
            Tr(lambda: ntn.matmul(ps_b[:, 1, :], oner[:], mu, start=True, stop=True), inc=None)
            mm_fence(ps_b[0:1, 0, 0:1])
            p.wait("vector", "T")
            sl = slice(tok * TT, (tok + 1) * TT)
            Vr(lambda sl=sl: nv.tensor_copy(bcst[:, 0, sl], ps_b[:, 0, :]), inc=None)
            Vr(lambda sl=sl: nv.tensor_copy(bcst[:, 1, sl], ps_b[:, 1, :]))
            return p.cnt["V"], st_free

        Vr(lambda: nv.memset(onec[:], 1.0), inc=None)
        Vr(lambda: nv.memset(zerc[:], 0.0), inc=None)
        Vr(lambda: nv.memset(fcell[:], 1.0), inc=None)
        Vr(lambda: nv.memset(nonec[:], -1.0), inc=None)
        Vr(lambda: nv.memset(oner[:], 1.0))
        const_v = p.cnt["V"]
        p.wait("tensor", "V", const_v)
        p.wait("scalar", "V", const_v)
        p.wait("gpsimd", "V", const_v)
        for dst, src in [(iot, dd["iota"]), (binS, dd["bin"]), (b1S, dd["b1"]),
                         (b2S, dd["b2"]), (lnlwS, dd["lnlw"]), (lnlbS, dd["lnlb"]),
                         (boutS, dd["bout"]), (bAS, dd["bA"]), (selS, dd["sel"])] + \
                        [(prm[k], dd[k]) for k in prm]:
            gp_load(dst[:], src)
        ld_params = p.cnt["GDm"]

        phr, phi = prm["phzr"][:, :, :], prm["phzi"][:, :, :]
        rho16, lnrho16 = pp["rho"][:, :, :], pp["lnrho"][:, :, :]
        absth16, sgn16 = pp["absth"][:, :, :], pp["sgn"][:, :, :]
        eR16, eI16 = pp["eR"][:, :, :], pp["eI"][:, :, :]
        p2r16, p2i16 = pp["p2r"][:, :, :], pp["p2i"][:, :, :]
        ln2047_16, th2047_16 = pp["ln2047"][:, :, :], pp["th2047"][:, :, :]
        A16, B16, C16, D16 = t16a[:], t16b[:], t16c[:], t16d[:]

        p.wait("vector", "GDm", ld_params)
        Vr(lambda: nv.tensor_tensor(A16, phr, phr, ALU.mult), inc=None)
        Vr(lambda: nv.tensor_tensor(B16, phi, phi, ALU.mult), inc=None)
        Vr(lambda: nv.tensor_tensor(A16, A16, B16, ALU.add), inc=None)
        Vr(lambda: nv.tensor_single_scalar(A16, A16, 1e-30, ALU.max))
        p.wait("scalar", "V")
        Sc(lambda: ns.activation(B16, A16, AF.Sqrt))
        p.wait("vector", "S")
        Vr(lambda: nv.reciprocal(C16, B16), inc=None)
        Vr(lambda: nv.tensor_tensor(eR16, phr, C16, ALU.mult), inc=None)
        Vr(lambda: nv.tensor_tensor(eI16, phi, C16, ALU.mult))
        p.wait("scalar", "V")
        Sc(lambda: ns.activation(rho16, B16, AF.Tanh), inc=None)
        Sc(lambda: ns.activation(lnrho16, rho16, AF.Ln), inc=None)
        Sc(lambda: ns.activation(A16, eR16, AF.Abs), inc=None)
        Sc(lambda: ns.activation(B16, eI16, AF.Abs))
        p.wait("vector", "S")
        Vr(lambda: nv.tensor_single_scalar(ln2047_16, lnrho16, 2047.0, ALU.mult), inc=None)
        Vr(lambda: nv.tensor_tensor(C16, A16, B16, ALU.min), inc=None)
        Vr(lambda: nv.tensor_tensor(D16, A16, B16, ALU.max), inc=None)
        Vr(lambda: nv.reciprocal(D16, D16), inc=None)
        Vr(lambda: nv.tensor_tensor(C16, C16, D16, ALU.mult))
        p.wait("scalar", "V")
        Sc(lambda: ns.activation(D16, C16, AF.Arctan))
        p.wait("scalar", "S")
        Sc(lambda: ns.activation(A16, D16, AF.Sin))
        p.wait("vector", "S")
        Vr(lambda: nv.tensor_tensor(B16, A16, A16, ALU.mult), inc=None)
        Vr(lambda: nv.tensor_scalar(B16, B16, -1.0, 1.0, ALU.mult, ALU.add))
        p.wait("scalar", "V")
        Sc(lambda: ns.activation(B16, B16, AF.Sqrt))
        p.wait("vector", "S")
        Vr(lambda: nv.tensor_tensor(C16, C16, B16, ALU.mult), inc=None)
        Vr(lambda: nv.tensor_tensor(C16, C16, A16, ALU.subtract), inc=None)
        Vr(lambda: nv.tensor_tensor(C16, C16, B16, ALU.mult), inc=None)
        Vr(lambda: nv.tensor_tensor(D16, D16, C16, ALU.add), inc=None)
        Vr(lambda: nv.tensor_tensor(A16, eI16, eI16, ALU.mult), inc=None)
        Vr(lambda: nv.tensor_tensor(B16, eR16, eR16, ALU.mult), inc=None)
        Vr(lambda: nv.tensor_tensor(A16, A16, B16, ALU.is_gt), inc=None)
        Vr(lambda: nv.tensor_scalar(B16, D16, -2.0, PI / 2, ALU.mult, ALU.add), inc=None)
        Vr(lambda: nv.tensor_tensor(A16, A16, B16, ALU.mult), inc=None)
        Vr(lambda: nv.tensor_tensor(D16, D16, A16, ALU.add), inc=None)
        Vr(lambda: nv.tensor_single_scalar(A16, eR16, 0.0, ALU.is_lt), inc=None)
        Vr(lambda: nv.tensor_scalar(B16, D16, -2.0, PI, ALU.mult, ALU.add), inc=None)
        Vr(lambda: nv.tensor_tensor(A16, A16, B16, ALU.mult), inc=None)
        Vr(lambda: nv.tensor_tensor(absth16, D16, A16, ALU.add), inc=None)
        Vr(lambda: nv.tensor_single_scalar(A16, eI16, 0.0, ALU.is_ge), inc=None)
        Vr(lambda: nv.tensor_scalar(sgn16, A16, 2.0, -1.0, ALU.mult, ALU.add), inc=None)
        Vr(lambda: nv.tensor_single_scalar(th2047_16, absth16, 2047.0, ALU.mult), inc=None)
        Vr(lambda: nv.tensor_single_scalar(A16, lnrho16, 2048.0, ALU.mult))
        p.wait("scalar", "V")
        Sc(lambda: ns.activation(A16, A16, AF.Exp))
        p.wait("vector", "S")
        Vr(lambda: nv.tensor_single_scalar(B16, absth16, 2048.0, ALU.mult), inc=None)
        range_reduce(C16, B16, D16)
        Vr(lambda: nv.tensor_copy(D16, C16))
        p.wait("scalar", "V")
        Sc(lambda: ns.activation(B16, C16, AF.Sin))
        p.wait("vector", "S")
        Vr(lambda: nv.tensor_tensor(p2i16, A16, B16, ALU.mult), inc=None)
        Vr(lambda: nv.tensor_tensor(p2i16, p2i16, sgn16, ALU.mult), inc=None)
        cos_from_red(B16, D16, C16)
        p.wait("vector", "S")
        Vr(lambda: nv.tensor_tensor(p2r16, A16, B16, ALU.mult), inc=None)
        Vr(lambda: nv.tensor_single_scalar(pp["nsgn"][:, :, :], sgn16, -1.0, ALU.mult), inc=None)
        range_reduce(C16, th2047_16, D16)
        Vr(lambda: nv.tensor_copy(D16, C16))
        cos_from_red(pp["c47"][:, :, :], C16, B16)
        Sc(lambda: ns.activation(pp["s47"][:, :, :], D16, AF.Sin))
        p.wait("vector", "S")
        for Lf, ck, sk, krk, kik in [(256.0, "c256", "s256", "kr256", "ki256"),
                                     (512.0, "c512", "s512", "kr512", "ki512"),
                                     (1024.0, "c1024", "s1024", "kr1024", "ki1024")]:
            Vr(lambda Lf=Lf: nv.tensor_single_scalar(A16, absth16, Lf, ALU.mult), inc=None)
            range_reduce(C16, A16, D16)
            Vr(lambda: nv.tensor_copy(D16, C16), inc=None)
            Vr(lambda Lf=Lf: nv.tensor_single_scalar(A16, lnrho16, Lf, ALU.mult))
            cos_from_red(pp[ck][:, :, :], C16, B16)
            Sc(lambda sk=sk: ns.activation(pp[sk][:, :, :], D16, AF.Sin), inc=None)
            Sc(lambda: ns.activation(A16, A16, AF.Exp))
            p.wait("vector", "S")
            Vr(lambda krk=krk, ck=ck: nv.tensor_tensor(
                pp[krk][:, :, :], pp[ck][:, :, :], A16, ALU.mult), inc=None)
            Vr(lambda kik=kik, sk=sk: nv.tensor_tensor(
                pp[kik][:, :, :], pp[sk][:, :, :], A16, ALU.mult))
        mark("P1")

        tmark_last = 0
        epi_done_v = 0
        stfree_v = 0
        tmarks = [0, 0]
        smarks = [0, 0]
        for tok in range(NTOK):
            for k in range(KV):
                sl = k % 3
                p.wait("sync", "T", tmarks[0])
                p.wait("sync", "S", smarks[0])
                mk = sring.load_group([
                    (xb[:, sl, :], dd["xT"][k * 128:(k + 1) * 128, tok * TT:(tok + 1) * TT]),
                    (wtb[:, sl, :], dd["winT"][k * 128:(k + 1) * 128, :])])
                sring.wait("scalar", mk)
                p.wait("scalar", "T", tmarks[0])
                s_sq = Sc(lambda sl=sl: ns.activation(xeb[:, sl, :], xb[:, sl, :], AF.Square))
                smarks = [smarks[1], s_sq]
                sring.wait("tensor", mk)
                if k == 0:
                    p.wait("tensor", "V", epi_done_v)
                    p.wait("tensor", "V", stfree_v)
                for dm in range(GD):
                    Tr(lambda k=k, sl=sl, dm=dm: ntn.matmul(
                        ps_a[:, dm, :], wtb[:, sl, dm * 128:(dm + 1) * 128], xb[:, sl, :],
                        start=(k == 0), stop=(k == KV - 1)), inc=None)
                Tr(lambda k=k, sl=sl: ntn.matmul(
                    ps_st[:, 0, :], onec[:], xb[:, sl, :],
                    start=(k == 0), stop=(k == KV - 1)), inc=None)
                p.wait("tensor", "S", s_sq)
                if tok == 0:
                    Tr(lambda k=k, sl=sl: ntn.matmul(
                        ps_b[0:1, 0, 0:D], nonec[:], wtb[:, sl, :],
                        start=(k == 0), stop=(k == KV - 1)), inc=None)
                tm = Tr(lambda k=k, sl=sl: ntn.matmul(
                    ps_st[:, 1, :], onec[:], xeb[:, sl, :],
                    start=(k == 0), stop=(k == KV - 1)), inc="T")
                tmarks = [tmarks[1], tm]
            mm_fence(ps_st[0:1, 0, 0:1])
            if tok == 0:
                p.wait("vector", "T")
                rs_v = Vr(lambda: nv.tensor_copy(rowb[:, 1, 0:TT], ps_b[0:1, 0, 0:D]))
                p.wait("gpsimd", "V", rs_v)
                g1 = gp_load(bounce.ap()[0:1, 0:D], rowb[0:1, 1, 0:D])
                p.wait("gpsimd", "GDm", g1)
                g2 = gp_load(rsS[:], bounce.ap()[0, 0:D].rearrange("(g q) -> q g", q=128), slow=True)
                p.wait("vector", "GDm", g1)
            bc_v, stfree_v = ln_stats_and_bcast(tok, float(V), extra_wait_v=(rs_v if tok == 0 else None))
            if tok == 0:
                p.wait("vector", "GDm", g2)
            sl_t = slice(tok * TT, (tok + 1) * TT)
            for dm in range(GD):
                hsl = h[:, dm, sl_t]
                Vr(lambda dm=dm, hsl=hsl, sl_t=sl_t: nv.tensor_tensor(
                    hsl, ps_a[:, dm, :], bcst[:, 0, sl_t], ALU.mult), inc=None)
                Vr(lambda dm=dm, hsl=hsl, sl_t=sl_t: nv.scalar_tensor_tensor(
                    hsl, bcst[:, 1, sl_t], rsS[:, dm:dm + 1], hsl, ALU.mult, ALU.add), inc=None)
                Vr(lambda dm=dm, hsl=hsl: nv.tensor_scalar_add(hsl, hsl, binS[:, dm:dm + 1]))
            epi_done_v = p.cnt["V"]

        mark("P2_win")
        if "h0" in taps:
            p.wait("gpsimd", "V")
            tap_out("h0", h[:, :, :])

        hf = h[:, :, :]
        yf = y[:, :, :]

        def inner_ln(lay, wkey, bkey, act, y_free_t=0):
            nonlocal stfree_v, epi_done_v
            p.wait("scalar", "V")
            p.wait("scalar", "T", y_free_t)
            sq_s = Sc(lambda: ns.activation(yf, hf, AF.Square))
            for tok in range(NTOK):
                p.wait("tensor", "S", sq_s)
                p.wait("tensor", "V", stfree_v)
                for kt in range(GD):
                    Tr(lambda kt=kt, tok=tok: ntn.matmul(
                        ps_st[:, 0, :], onec[:], h[:, kt, tok * TT:(tok + 1) * TT],
                        start=(kt == 0), stop=(kt == GD - 1)), inc=None)
                for kt in range(GD):
                    Tr(lambda kt=kt, tok=tok: ntn.matmul(
                        ps_st[:, 1, :], onec[:], y[:, kt, tok * TT:(tok + 1) * TT],
                        start=(kt == 0), stop=(kt == GD - 1)), inc=None)
                mm_fence(ps_st[0:1, 0, 0:1])
                bc_v, stfree_v = ln_stats_and_bcast(tok, float(D))
            rb = bcst[:, 0:1, :].broadcast_to([128, GD, TC])
            mb = bcst[:, 1:2, :].broadcast_to([128, GD, TC])
            Vr(lambda: nv.tensor_tensor(yf, hf, rb, ALU.mult), inc=None)
            Vr(lambda: nv.tensor_tensor(yf, yf, mb, ALU.subtract))
            if act == AF.Copy:
                for g in range(GD):
                    wcol = prm[wkey][:, lay, g:g + 1]
                    bb = prm[bkey][:, lay, g:g + 1].broadcast_to([128, TC])
                    Vr(lambda g=g, wcol=wcol, bb=bb: nv.scalar_tensor_tensor(
                        y[:, g, :], y[:, g, :], wcol, bb, ALU.mult, ALU.add))
                return ("V", p.cnt["V"])
            p.wait("scalar", "V")
            for g in range(GD):
                Sc(lambda g=g: ns.activation(
                    y[:, g, :], y[:, g, :], act,
                    bias=prm[bkey][:, lay, g:g + 1], scale=prm[wkey][:, lay, g:g + 1]))
            return ("S", p.cnt["S"])

        prev_mm1_t = 0
        w2marks = [0, 0]
        w2i = 0
        for lay in range(DEPTH if stage >= 2 else 0):
            p.wait("sync", "T", prev_mm1_t)
            w1_ld = sring.load(w1t[:, :, :], dd["w1T"][lay].rearrange("(kt q) f -> q kt f", q=128))

            s_ln1 = inner_ln(lay, "ln1w", "ln1b", AF.Silu, y_free_t=prev_mm1_t)
            mark(f"L{lay}_ln1")
            if lay == 0 and "y0" in taps:
                p.wait("gpsimd", s_ln1[0], s_ln1[1])
                tap_out("y0", yf)

            p.wait("vector", s_ln1[0], s_ln1[1])
            for g in range(GD):
                build_ptab(lay, g, with_rho=False)
                rho_b = pp["rho"][:, lay, g:g + 1].broadcast_to([128, TC])
                Vr(lambda g=g: nv.tensor_tensor(u[:, 0, :], tabs[:, 0, :], y[:, g, :], ALU.mult), inc=None)
                Vr(lambda g=g: nv.tensor_tensor(u[:, 1, :], tabs[:, 1, :], y[:, g, :], ALU.mult), inc=None)
                Vr(lambda rho_b=rho_b: nv.tensor_tensor_scan(
                    u[:, 0, :], rho_b, u[:, 0, :], 0.0, ALU.mult, ALU.add), inc=None)
                Vr(lambda rho_b=rho_b: nv.tensor_tensor_scan(
                    u[:, 1, :], rho_b, u[:, 1, :], 0.0, ALU.mult, ALU.subtract), inc=None)
                u0e, u1e = u[:, 0, TC - 1:TC], u[:, 1, TC - 1:TC]
                c47c = pp["c47"][:, lay, g:g + 1]
                s47c = pp["s47"][:, lay, g:g + 1]
                sgnc = pp["sgn"][:, lay, g:g + 1]
                Vr(lambda u1e=u1e, s47c=s47c: nv.tensor_tensor(
                    c8a[:, 0:1], u1e, s47c, ALU.mult), inc=None)
                Vr(lambda g=g, u0e=u0e, c47c=c47c: nv.scalar_tensor_tensor(
                    ES[:, g:g + 1], u0e, c47c, c8a[:, 0:1], ALU.mult, ALU.subtract), inc=None)
                Vr(lambda u1e=u1e, c47c=c47c: nv.tensor_tensor(
                    c8a[:, 1:2], u1e, c47c, ALU.mult), inc=None)
                Vr(lambda u0e=u0e, s47c=s47c: nv.scalar_tensor_tensor(
                    c8b[:, 0:1], u0e, s47c, c8a[:, 1:2], ALU.mult, ALU.add), inc=None)
                Vr(lambda g=g, sgnc=sgnc: nv.tensor_tensor(
                    ES[:, 4 + g:5 + g], c8b[:, 0:1], sgnc, ALU.mult), inc=None)
                Vr(lambda g=g: nv.tensor_tensor(y[:, g, :], tabs[:, 0, :], u[:, 0, :], ALU.mult), inc=None)
                Vr(lambda g=g: nv.tensor_tensor(u[:, 0, :], tabs[:, 1, :], u[:, 1, :], ALU.mult), inc=None)
                Vr(lambda g=g: nv.tensor_tensor(y[:, g, :], y[:, g, :], u[:, 0, :], ALU.subtract))
            e_v = p.cnt["V"]

            p.wait("gpsimd", "V", e_v)
            gp1 = gp_load(ag_in.ap().rearrange("o (q k) -> (o q) k", q=128), ES[:])
            p.wait("gpsimd", "GDm", gp1)
            cc_m = Gp(lambda: ng.collective_compute(
                "AllGather", ALU.bypass, replica_groups=[list(range(NCORES))],
                ins=[ag_in.ap().opt()], outs=[ag_out.ap().opt()]))
            p.wait("gpsimd", "G", cc_m)
            gp2 = gp_load(EA[:, :, :], ag_out.ap()[0:7, :].rearrange("j (q k) -> q j k", q=128))

            p.wait("vector", "GDm", gp2)
            Vr(lambda lay=lay: nv.tensor_copy(CCs[:, 0:4], prm["lcr"][:, lay, :]), inc=None)
            Vr(lambda lay=lay: nv.tensor_copy(CCs[:, 4:8], prm["lci"][:, lay, :]), inc=None)
            for j in range(7):
                Cr, Ci = CCs[:, 0:4], CCs[:, 4:8]
                p2r_l, p2i_l = pp["p2r"][:, lay, :], pp["p2i"][:, lay, :]
                Vr(lambda Cr=Cr, p2r_l=p2r_l: nv.tensor_tensor(c8a[:, 0:4], Cr, p2r_l, ALU.mult), inc=None)
                Vr(lambda Ci=Ci, p2i_l=p2i_l: nv.tensor_tensor(c8b[:, 0:4], Ci, p2i_l, ALU.mult), inc=None)
                Vr(lambda: nv.tensor_tensor(c8a[:, 0:4], c8a[:, 0:4], c8b[:, 0:4], ALU.subtract), inc=None)
                Vr(lambda j=j: nv.tensor_tensor(c8a[:, 0:4], c8a[:, 0:4], EA[:, j, 0:4], ALU.add), inc=None)
                Vr(lambda Cr=Cr, p2i_l=p2i_l: nv.tensor_tensor(c8b[:, 0:4], Cr, p2i_l, ALU.mult), inc=None)
                Vr(lambda Ci=Ci, p2r_l=p2r_l: nv.tensor_tensor(c8b[:, 4:8], Ci, p2r_l, ALU.mult), inc=None)
                Vr(lambda: nv.tensor_tensor(c8a[:, 4:8], c8b[:, 0:4], c8b[:, 4:8], ALU.add), inc=None)
                Vr(lambda j=j: nv.tensor_tensor(c8a[:, 4:8], c8a[:, 4:8], EA[:, j, 4:8], ALU.add), inc=None)
                Vr(lambda: nv.tensor_tensor(c8b[:, 0:8], c8a[:, 0:8], CCs[:, 0:8], ALU.subtract), inc=None)
                Vr(lambda j=j: nv.scalar_tensor_tensor(
                    CCs[:, 0:8], c8b[:, 0:8], selS[:, j:j + 1], CCs[:, 0:8], ALU.mult, ALU.add), inc=None)
            eRl, eIl = pp["eR"][:, lay, :], pp["eI"][:, lay, :]
            rhol = pp["rho"][:, lay, :]
            nsgnl = pp["nsgn"][:, lay, :]
            Vr(lambda eRl=eRl: nv.tensor_tensor(WIs[:, 0:4], CCs[:, 0:4], eRl, ALU.mult), inc=None)
            Vr(lambda eIl=eIl: nv.tensor_tensor(c8a[:, 0:4], CCs[:, 4:8], eIl, ALU.mult), inc=None)
            Vr(lambda: nv.tensor_tensor(WIs[:, 0:4], WIs[:, 0:4], c8a[:, 0:4], ALU.subtract), inc=None)
            Vr(lambda rhol=rhol: nv.tensor_tensor(WIs[:, 0:4], WIs[:, 0:4], rhol, ALU.mult), inc=None)
            Vr(lambda eIl=eIl: nv.tensor_tensor(WIs[:, 4:8], CCs[:, 0:4], eIl, ALU.mult), inc=None)
            Vr(lambda eRl=eRl: nv.tensor_tensor(c8a[:, 4:8], CCs[:, 4:8], eRl, ALU.mult), inc=None)
            Vr(lambda: nv.tensor_tensor(WIs[:, 4:8], WIs[:, 4:8], c8a[:, 4:8], ALU.add), inc=None)
            Vr(lambda rhol=rhol: nv.tensor_tensor(WIs[:, 4:8], WIs[:, 4:8], rhol, ALU.mult), inc=None)
            Vr(lambda nsgnl=nsgnl: nv.tensor_tensor(WIs[:, 4:8], WIs[:, 4:8], nsgnl, ALU.mult))

            for g in range(GD):
                build_ptab(lay, g, with_rho=True)
                Vr(lambda g=g: nv.scalar_tensor_tensor(
                    y[:, g, :], tabs[:, 0, :], WIs[:, g:g + 1], y[:, g, :],
                    ALU.mult, ALU.add), inc=None)
                Vr(lambda g=g: nv.scalar_tensor_tensor(
                    y[:, g, :], tabs[:, 1, :], WIs[:, 4 + g:5 + g], y[:, g, :],
                    ALU.mult, ALU.add))
            if lay == 0 and "conv0" in taps:
                p.wait("gpsimd", "V")
                tap_out("conv0", yf)

            for g in range(GD):
                Vr(lambda g=g, lay=lay: nv.scalar_tensor_tensor(
                    h[:, g, :], y[:, g, :], prm["scl"][:, lay, g:g + 1], h[:, g, :],
                    ALU.mult, ALU.add), inc=None)
            Vr(lambda: nv.tensor_copy(c8a[:, 0:1], CCs[:, 0:1]))

            mark(f"L{lay}_scan")
            s_ln2 = inner_ln(lay, "ln2w", "ln2b", AF.Copy)
            mark(f"L{lay}_ln2")

            evac_s = [0, 0, 0]
            mm2_t = 0
            hupd_v = p.cnt["V"]
            sring.wait("tensor", w1_ld)
            p.wait("tensor", s_ln2[0], s_ln2[1])
            for tok in range(NTOK):
                p.wait("tensor", "V", hupd_v)

                def emit_mm2(fk, sl, pb, wl):
                    sring.wait("tensor", wl)
                    p.wait("tensor", "S", evac_s[pb])
                    for dm in range(GD):
                        Tr(lambda fk=fk, sl=sl, dm=dm, pb=pb: ntn.matmul(
                            ps_a[:, dm, :], w2b[:, sl, dm * 128:(dm + 1) * 128],
                            y1[:, pb, :],
                            start=(fk == 0), stop=(fk == KF - 1)),
                           inc=("T" if dm == GD - 1 else None))
                    w2marks[:] = [w2marks[1], p.cnt["T"]]

                pend = None
                for fk in range(KF):
                    sl = w2i % 3
                    w2i += 1
                    p.wait("sync", "T", w2marks[0])
                    wl = sring.load(w2b[:, sl, :],
                                    dd["w2T"][lay][fk * 128:(fk + 1) * 128, :])
                    pb = fk % 2
                    p.wait("tensor", "S", evac_s[pb])
                    for kt in range(GD):
                        Tr(lambda lay=lay, fk=fk, kt=kt, pb=pb, tok=tok: ntn.matmul(
                            ps_b[:, pb, :], w1t[:, kt, fk * 128:(fk + 1) * 128],
                            y[:, kt, tok * TT:(tok + 1) * TT],
                            start=(kt == 0), stop=(kt == GD - 1)), inc=None)
                    mm1_t = mm_fence(ps_b[0:1, pb, 0:1])
                    p.wait("scalar", "T", mm1_t)
                    evac_s[pb] = Sc(lambda lay=lay, fk=fk, pb=pb: ns.activation(
                        y1[:, pb, :], ps_b[:, pb, :], AF.Silu,
                        bias=b1S[:, lay, fk:fk + 1], scale=1.0))
                    if pend is not None:
                        emit_mm2(*pend)
                    pend = (fk, sl, pb, wl)
                emit_mm2(*pend)
                mm2_t = mm_fence(ps_a[0:1, 0, 0:1])
                p.wait("vector", "T", mm2_t)
                for dm in range(GD):
                    hsl = h[:, dm, tok * TT:(tok + 1) * TT]
                    Vr(lambda dm=dm, lay=lay, hsl=hsl: nv.scalar_tensor_tensor(
                        hsl, ps_a[:, dm, :], b2S[:, lay, dm:dm + 1], hsl,
                        ALU.add, ALU.add), inc=None)
                Vr(lambda: nv.tensor_copy(c8a[:, 0:1], CCs[:, 0:1]))
                hupd_v = p.cnt["V"]
            prev_mm1_t = p.cnt["T"]
            mark(f"L{lay}_ffn")

            if lay == 0 and "h1" in taps:
                p.wait("gpsimd", "V", hupd_v)
                tap_out("h1", hf)

        if stage >= 2 and "hfin" in taps:
            p.wait("gpsimd", "V")
            tap_out("hfin", hf)

        if stage >= 3:
            def final_ln():
                nonlocal stfree_v
                p.wait("scalar", "V")
                p.wait("scalar", "T", prev_mm1_t if stage >= 2 else 0)
                sq_s = Sc(lambda: ns.activation(yf, hf, AF.Square))
                for tok in range(NTOK):
                    p.wait("tensor", "S", sq_s)
                    p.wait("tensor", "V", stfree_v)
                    for kt in range(GD):
                        Tr(lambda kt=kt, tok=tok: ntn.matmul(
                            ps_st[:, 0, :], onec[:], h[:, kt, tok * TT:(tok + 1) * TT],
                            start=(kt == 0), stop=(kt == GD - 1)), inc=None)
                    for kt in range(GD):
                        Tr(lambda kt=kt, tok=tok: ntn.matmul(
                            ps_st[:, 1, :], onec[:], y[:, kt, tok * TT:(tok + 1) * TT],
                            start=(kt == 0), stop=(kt == GD - 1)),
                           inc=("T" if kt == GD - 1 else None))
                    bc_v, stfree_v = ln_stats_and_bcast(tok, float(D))
                rb = bcst[:, 0:1, :].broadcast_to([128, GD, TC])
                mbb = bcst[:, 1:2, :].broadcast_to([128, GD, TC])
                Vr(lambda: nv.tensor_tensor(yf, hf, rb, ALU.mult), inc=None)
                Vr(lambda: nv.tensor_tensor(yf, yf, mbb, ALU.subtract))
                for g in range(GD):
                    bb = lnlbS[:, g:g + 1].broadcast_to([128, TC])
                    Vr(lambda g=g, bb=bb: nv.scalar_tensor_tensor(
                        y[:, g, :], y[:, g, :], lnlwS[:, g:g + 1], bb,
                        ALU.mult, ALU.add))

            final_ln()
            mark("P4_ln")
            if "hln" in taps:
                p.wait("gpsimd", "V")
                tap_out("hln", yf)

            def x1sb(k):
                buf = h if k < 16 else w1t
                kk = k % 16
                return buf[:, kk // 4, (kk % 4) * TT:(kk % 4 + 1) * TT]

            hln_v = p.cnt["V"]
            p.wait("sync", "T", p.cnt["T"])
            wout_mark = [hln_v] * 4
            wt_marks = [0, 0]
            xe_mark = [0, 0, 0]
            sq2_marks = [0, 0, 0]
            stfree2 = 0
            ab_marks = [0, 0]
            rs_evac = 0
            a_epi_v = 0
            rsA_gd = 0
            a_out_mark = [None, None]
            ai = 0
            for tok in range(NTOK):
                sl_t = slice(tok * TT, (tok + 1) * TT)
                p.wait("tensor", "V", stfree_v)
                for vm in range(KV):
                    sl = vm % 3
                    pa = vm % 4
                    p.wait("sync", "T", wt_marks[0])
                    wl = sring.load(wtb[:, sl, :],
                                    dd["woutT"][:, vm * 128:(vm + 1) * 128].rearrange(
                                        "(kt q) m -> q kt m", q=128))
                    p.wait("gpsimd", "V", xe_mark[sl])
                    xl = ering.load(xeb[:, sl, :],
                                    dd["xT"][vm * 128:(vm + 1) * 128, sl_t])
                    sring.wait("tensor", wl)
                    p.wait("tensor", "V", wout_mark[pa])
                    for kt in range(GD):
                        Tr(lambda vm=vm, sl=sl, kt=kt, pa=pa, sl_t=sl_t: ntn.matmul(
                            ps_a[:, pa, :], wtb[:, sl, kt * 128:(kt + 1) * 128],
                            y[:, kt, sl_t],
                            start=(kt == 0), stop=(kt == GD - 1)), inc=None)
                    wt_marks = [wt_marks[1], mm_fence(ps_a[0:1, pa, 0:1])]
                    x1t = x1sb(vm)
                    p.wait("vector", "T")
                    ering.wait("vector", xl)
                    Vr(lambda vm=vm, sl=sl, pa=pa, x1t=x1t: nv.scalar_tensor_tensor(
                        x1t, ps_a[:, pa, :], boutS[:, vm:vm + 1], xeb[:, sl, :],
                        ALU.add, ALU.add))
                    epi_v = p.cnt["V"]
                    wout_mark[pa] = epi_v
                    xe_mark[sl] = epi_v
                    su = vm % 3
                    p.wait("scalar", "V", epi_v)
                    p.wait("scalar", "T", sq2_marks[su])
                    sqs = Sc(lambda su=su, x1t=x1t: ns.activation(
                        u[:, 0, su * TT:(su + 1) * TT], x1t, AF.Square))
                    p.wait("tensor", "S", sqs)
                    if vm == 0:
                        p.wait("tensor", "V", stfree2)
                    Tr(lambda vm=vm, x1t=x1t: ntn.matmul(
                        ps_st[:, 0, :], onec[:], x1t,
                        start=(vm == 0), stop=(vm == KV - 1)), inc=None)
                    sq2_marks[su] = Tr(lambda vm=vm, su=su: ntn.matmul(
                        ps_st[:, 1, :], onec[:], u[:, 0, su * TT:(su + 1) * TT],
                        start=(vm == 0), stop=(vm == KV - 1)), inc="T")
                mm_fence(ps_st[0:1, 0, 0:1])
                bc_v, stfree2 = ln_stats_and_bcast(tok, float(V))

                if "x1" in taps:
                    p.wait("gpsimd", "V", bc_v)
                    for vm in range(KV):
                        gp_load(d_tap["x1"][vm * 128:(vm + 1) * 128, sl_t], x1sb(vm))
                    p.wait("vector", "GDm")

                if stage < 4:
                    continue
                for mb in range(8):
                    p.wait("tensor", "V", bc_v if mb == 0 else a_epi_v)
                    for kb in range(KV // 4):
                        j = ai % 2
                        ai += 1
                        p.wait("sync", "T", ab_marks[0])
                        al = sring.load(
                            ab4[:, j],
                            dd["aT"][kb * 512:(kb + 1) * 512,
                                     mb * TT:(mb + 1) * TT].rearrange(
                                         "(c q) m -> q c m", q=128))
                        sring.wait("tensor", al)
                        if tok == 0 and kb == 0:
                            p.wait("tensor", "V", rs_evac)
                        for c in range(4):
                            k = kb * 4 + c
                            for mt in range(4):
                                Tr(lambda j=j, c=c, mt=mt, k=k: ntn.matmul(
                                    ps_a[:, mt, :], ab4[:, j, c, mt * 128:(mt + 1) * 128],
                                    x1sb(k),
                                    start=(k == 0), stop=(k == KV - 1)),
                                   inc=(None if (mt < 3 or tok == 0 or c < 3) else "T"))
                            if tok == 0:
                                Tr(lambda j=j, c=c, k=k: ntn.matmul(
                                    ps_b[0:1, 0, :], nonec[:], ab4[:, j, c, :],
                                    start=(k == 0), stop=(k == KV - 1)),
                                   inc=("T" if c == 3 else None))
                        ab_marks = ab_marks[1:] + [p.cnt["T"]]
                    mm_t = mm_fence(ps_a[0:1, 0, 0:1])
                    ab_marks = ab_marks[1:] + [mm_t]
                    if tok == 0:
                        p.wait("vector", "T", mm_t)
                        p.wait("vector", "GDm")
                        rs_evac = Vr(lambda: nv.tensor_copy(rowb[:, 1, :], ps_b[0:1, 0, :]))
                        p.wait("gpsimd", "V", rs_evac)
                        gb1 = gp_load(bounce.ap()[0:1, mb * TT:(mb + 1) * TT], rowb[0:1, 1, :])
                        p.wait("gpsimd", "GDm", gb1)
                        rsA_gd = gp_load(
                            rsAS[:, mb * 4:(mb + 1) * 4],
                            bounce.ap()[0, mb * TT:(mb + 1) * TT].rearrange("(g q) -> q g", q=128),
                            slow=True)
                    p.wait("vector", "T", mm_t)
                    if tok == 0:
                        p.wait("vector", "GDm", rsA_gd)
                    for mt in range(4):
                        vm = mb * 4 + mt
                        sl2 = mt % 2
                        esl = u[:, 1, sl2 * TT:(sl2 + 1) * TT]
                        if a_out_mark[sl2] is not None:
                            p.wait("vector", a_out_mark[sl2][0], a_out_mark[sl2][1])
                        Vr(lambda mt=mt, esl=esl, sl_t=sl_t: nv.tensor_tensor(
                            esl, ps_a[:, mt, :], bcst[:, 0, sl_t], ALU.mult), inc=None)
                        Vr(lambda vm=vm, esl=esl, sl_t=sl_t: nv.scalar_tensor_tensor(
                            esl, bcst[:, 1, sl_t], rsAS[:, vm:vm + 1], esl,
                            ALU.mult, ALU.add), inc=None)
                        Vr(lambda vm=vm, esl=esl: nv.scalar_tensor_tensor(
                            esl, esl, bAS[:, vm:vm + 1], x1sb(vm),
                            ALU.add, ALU.add))
                        ep_v = p.cnt["V"]
                        p.wait("scalar", "V", ep_v)
                        onm = "O0" if sl2 % 2 == 0 else "O1"
                        p.wait("scalar", onm, p.cnt[onm])
                        p.run("scalar", lambda vm=vm, esl=esl, sl_t=sl_t:
                              nc.scalar.dma_start(out=outT[vm * 128:(vm + 1) * 128, sl_t], in_=esl),
                              onm, 16)
                        a_out_mark[sl2] = (onm, p.cnt[onm])
                    a_epi_v = p.cnt["V"]
                wout_mark = [a_epi_v] * 4
            mark("P4_wout")

        if stage >= 4:
            mark("P5_A")
        p.wait("sync", "O0", p.cnt["O0"])
        p.wait("sync", "O1", p.cnt["O1"])
        p.wait("sync", "GDm", p.cnt["GDm"])
        p.emit()
    return nc, d_tap


def kernel(**inputs):
    in_maps = stage_inputs(inputs)
    nc, _ = build_program(taps=DEBUG_TAPS, stage=STAGE)
    res = run_bass_kernel_spmd(nc, in_maps, core_ids=list(range(NCORES)))
    out = np.empty((B, L, V), dtype=np.float32)
    for c in range(NCORES):
        b, q = c // QPER, c % QPER
        out[b, q * TC:(q + 1) * TC, :] = res.results[c]["outT"].T
    return out


# version 09
# speedup vs baseline: 1.0808x; 1.0098x over previous
from contextlib import ExitStack

import numpy as np
import concourse.bass as bass
import concourse.mybir as mybir
from concourse.bass_utils import run_bass_kernel_spmd

F32 = mybir.dt.float32
I32 = mybir.dt.int32
AF = mybir.ActivationFunctionType
ALU = mybir.AluOpType

B, L, V, D, FF, DEPTH = 2, 8192, 4096, 512, 2048, 4
NCORES, QPER = 8, 4
TC = L // QPER
TT = 512
NTOK = TC // TT
KV = V // 128
GD = D // 128
KF = FF // 128
PI = float(np.pi)
TWO_PI = float(2 * np.pi)
INV_2PI = float(1.0 / (2 * np.pi))
PI_LO = float(np.float32(3.1415925))
LN_EPS = 1e-5

ENGINES = ("tensor", "vector", "scalar", "gpsimd", "sync")

DEBUG_TAPS = ()
STAGE = 4
PHASE_MARKS = []


class Prog:

    SELF = {"vector": "V", "scalar": "S", "gpsimd": "G"}

    def __init__(self, nc, stack, sem_names):
        self.nc = nc
        self.q = {e: [] for e in ENGINES}
        self.sem = {n: stack.enter_context(nc.semaphore(n)) for n in sem_names}
        self.cnt = {n: 0 for n in sem_names}

    def run(self, eng, fn, inc=None, n=1):
        selfsem = self.SELF.get(eng)
        if selfsem is not None and inc is None:
            inc, n = selfsem, 1
        if inc is not None:
            if selfsem is not None and inc == selfsem:
                prev = self.cnt[inc]
                if prev:
                    self.q[eng].append(("w", self.sem[inc], prev))
            self.cnt[inc] += n
            sem = self.sem[inc]
            self.q[eng].append(lambda fn=fn, sem=sem, n=n: fn().then_inc(sem, n))
            return self.cnt[inc]
        self.q[eng].append(fn)
        return None

    def wait(self, eng, name, val=None):
        v = self.cnt[name] if val is None else val
        if v <= 0:
            return
        self.q[eng].append(("w", self.sem[name], v))

    def emit(self):
        with self.nc.Block() as blk:
            for e in ENGINES:
                def body(engobj, e=e):
                    for item in self.q[e]:
                        if isinstance(item, tuple):
                            engobj.wait_ge(item[1], item[2])
                        else:
                            item()
                getattr(blk, e)(body)


def stage_inputs(inp):
    f = np.ascontiguousarray
    x = np.asarray(inp["L0_attn_x"], dtype=np.float32)

    def ch(v):
        return f(np.asarray(v, np.float32).reshape(GD, 128).T)

    def chl(v):
        return f(np.asarray(v, np.float32).reshape(DEPTH, GD, 128).transpose(2, 0, 1))

    def vw(v, g):
        return f(np.asarray(v, np.float32).reshape(g, 128).T)

    common = {
        "winT": f(np.asarray(inp["L0_attn_Win"], np.float32).T),
        "w1T": f(np.asarray(inp["L0_attn_W1"], np.float32).transpose(0, 2, 1)),
        "w2T": f(np.asarray(inp["L0_attn_W2"], np.float32).transpose(0, 2, 1)),
        "woutT": f(np.asarray(inp["L0_attn_Wout"], np.float32).T),
        "aT": f(np.asarray(inp["L0_ffn_A"], np.float32).T),
        "bin": ch(inp["L0_attn_bin"]),
        "b1": f(np.asarray(inp["L0_attn_b1"], np.float32).reshape(DEPTH, KF, 128).transpose(2, 0, 1)),
        "b2": chl(inp["L0_attn_b2"]),
        "ln1w": chl(inp["L0_attn_ln1_w"]), "ln1b": chl(inp["L0_attn_ln1_b"]),
        "ln2w": chl(inp["L0_attn_ln2_w"]), "ln2b": chl(inp["L0_attn_ln2_b"]),
        "phzr": chl(inp["L0_attn_phz_re"]), "phzi": chl(inp["L0_attn_phz_im"]),
        "lcr": chl(inp["L0_attn_lc_re"]), "lci": chl(inp["L0_attn_lc_im"]),
        "scl": chl(inp["L0_attn_sc_lin"]),
        "lnlw": ch(inp["L0_attn_lnl_w"]), "lnlb": ch(inp["L0_attn_lnl_b"]),
        "bout": vw(inp["L0_attn_bout"], KV),
        "bA": vw(inp["L0_ffn_b"], KV),
        "iota": f(np.broadcast_to(np.arange(TC, dtype=np.float32), (128, TC)).copy()),
    }
    in_maps = []
    for c in range(NCORES):
        b, q = c // QPER, c % QPER
        xT = f(x[b, q * TC:(q + 1) * TC, :].T)
        sel = np.zeros((128, 7), np.float32)
        for j in range(7):
            if QPER * b <= j < QPER * b + q:
                sel[:, j] = 1.0
        in_maps.append(dict(common, xT=xT, sel=sel))
    return in_maps


def build_program(taps=(), stage=4):
    PHASE_MARKS.clear()
    nc = bass.Bass("TRN2", target_bir_lowering=False)
    taps = set(taps)
    nv, ns, ntn, ng = nc.vector, nc.scalar, nc.tensor, nc.gpsimd

    dd = {}
    for name, shape in [
        ("xT", [V, TC]), ("winT", [V, D]), ("w1T", [DEPTH, D, FF]), ("w2T", [DEPTH, FF, D]),
        ("woutT", [D, V]), ("aT", [V, V]), ("bin", [128, GD]),
        ("b1", [128, DEPTH, KF]), ("b2", [128, DEPTH, GD]),
        ("ln1w", [128, DEPTH, GD]), ("ln1b", [128, DEPTH, GD]),
        ("ln2w", [128, DEPTH, GD]), ("ln2b", [128, DEPTH, GD]),
        ("phzr", [128, DEPTH, GD]), ("phzi", [128, DEPTH, GD]),
        ("lcr", [128, DEPTH, GD]), ("lci", [128, DEPTH, GD]),
        ("scl", [128, DEPTH, GD]), ("lnlw", [128, GD]), ("lnlb", [128, GD]),
        ("bout", [128, KV]), ("bA", [128, KV]), ("iota", [128, TC]), ("sel", [128, 7]),
    ]:
        dd[name] = nc.dram_tensor(name, shape, F32, kind="ExternalInput").ap()

    outT = nc.dram_tensor("outT", [V, TC], F32, kind="ExternalOutput").ap() if stage >= 4 else None
    d_tap = {}
    for t in taps:
        shp = {"h0": [128, GD * TC], "y0": [128, GD * TC], "conv0": [128, GD * TC],
               "h1": [128, GD * TC], "hfin": [128, GD * TC], "hln": [128, GD * TC],
               "x1": [V, TC]}[t]
        d_tap[t] = nc.dram_tensor("tap_" + t, shp, F32, kind="ExternalOutput").ap()

    ag_in = nc.dram_tensor("ag_in", [1, 1024], F32)
    ag_out = nc.dram_tensor("ag_out", [NCORES, 1024], F32)
    bounce = nc.dram_tensor("bounce", [1, V], F32)

    st = ExitStack()
    with st:
        def sbt(name, shape, dt=F32):
            return st.enter_context(nc.sbuf_tensor(name, shape, dt))

        h = sbt("h", [128, GD, TC])
        y = sbt("y", [128, GD, TC])
        tabs = sbt("tabs", [128, 2, TC])
        u = sbt("u", [128, 2, TC])
        bcst = sbt("bcst", [128, 2, TC])
        y1 = sbt("y1", [128, 2, TT])
        w1t = sbt("w1t", [128, GD, FF])
        w2b = sbt("w2b", [128, 3, TT])
        xb = sbt("xb", [128, 3, TT])
        wtb = sbt("wtb", [128, 3, TT])
        ab4 = sbt("ab4", [128, 2, 4, TT])
        xeb = sbt("xeb", [128, 3, TT])
        rowb = sbt("rowb", [1, 2, TT])
        iot = sbt("iot", [128, TC])
        onec = sbt("onec", [128, 1])
        zerc = sbt("zerc", [128, 1])
        nonec = sbt("nonec", [128, 1])
        oner = sbt("oner", [1, 128])
        binS = sbt("binS", [128, GD])
        b1S = sbt("b1S", [128, DEPTH, KF])
        b2S = sbt("b2S", [128, DEPTH, GD])
        prm = {k: sbt(k + "S", [128, DEPTH, GD]) for k in
               ["ln1w", "ln1b", "ln2w", "ln2b", "phzr", "phzi", "lcr", "lci", "scl"]}
        lnlwS = sbt("lnlwS", [128, GD])
        lnlbS = sbt("lnlbS", [128, GD])
        boutS = sbt("boutS", [128, KV])
        bAS = sbt("bAS", [128, KV])
        selS = sbt("selS", [128, 7])
        rsS = sbt("rsS", [128, GD])
        rsAS = sbt("rsAS", [128, KV])
        pp = {k: sbt("pp_" + k, [128, DEPTH, GD]) for k in
              ["rho", "lnrho", "absth", "sgn", "eR", "eI", "p2r", "p2i",
               "ln2047", "th2047", "c47", "s47", "nsgn",
               "c256", "s256", "c512", "s512", "c1024", "s1024",
               "kr256", "ki256", "kr512", "ki512", "kr1024", "ki1024"]}
        t16a = sbt("t16a", [128, DEPTH * GD])
        t16b = sbt("t16b", [128, DEPTH * GD])
        t16c = sbt("t16c", [128, DEPTH * GD])
        t16d = sbt("t16d", [128, DEPTH * GD])
        fcell = sbt("fcell", [1, 8])
        ES = sbt("ES", [128, 8])
        EA = sbt("EA", [128, 7, 8])
        CCs = sbt("CCs", [128, 8])
        WIs = sbt("WIs", [128, 8])
        c8a = sbt("c8a", [128, 8])
        c8b = sbt("c8b", [128, 8])

        ps_a = st.enter_context(nc.psum_tensor("ps_a", [128, 4, TT], F32))
        ps_b = st.enter_context(nc.psum_tensor("ps_b", [128, 2, TT], F32))
        ps_st = st.enter_context(nc.psum_tensor("ps_st", [1, 2, TT], F32))

        p = Prog(nc, st, ["T", "V", "S", "G", "GDm",
                          "L0", "L1", "L2", "L3", "L4", "L5",
                          "E0", "E1", "E2", "O0", "O1"])

        def Vr(fn, inc="V", n=1):
            return p.run("vector", fn, None, 1)

        def Sc(fn, inc="S", n=1):
            return p.run("scalar", fn, None, 1)

        def Tr(fn, inc=None, n=1):
            return p.run("tensor", fn, inc, n)

        def Gp(fn, inc=None, n=1):
            if inc is None:
                return p.run("gpsimd", fn, None, 1)
            return p.run("gpsimd", fn, inc, n)

        def Sy(fn, inc=None, n=1):
            return p.run("sync", fn, inc, n)

        def mark(name):
            PHASE_MARKS.append((name, dict(p.cnt)))

        class Ring:

            def __init__(self, eng, names):
                self.eng, self.names = eng, names
                self.i = 0

            def load_group(self, pairs):
                nm = self.names[self.i % len(self.names)]
                self.i += 1
                p.wait(self.eng, nm, p.cnt[nm])
                for dst_ap, src_ap in pairs:
                    if self.eng == "sync":
                        p.run("sync", lambda dst_ap=dst_ap, src_ap=src_ap:
                              nc.sync.dma_start(out=dst_ap, in_=src_ap), nm, 16)
                    else:
                        p.run("gpsimd", lambda dst_ap=dst_ap, src_ap=src_ap:
                              ng.dma_start(out=dst_ap, in_=src_ap), nm, 16)
                return (nm, p.cnt[nm])

            def load(self, dst_ap, src_ap, slow=False):
                nm = self.names[self.i % len(self.names)]
                self.i += 1
                p.wait(self.eng, nm, p.cnt[nm])
                if self.eng == "sync":
                    p.run("sync", lambda dst_ap=dst_ap, src_ap=src_ap:
                          nc.sync.dma_start(out=dst_ap, in_=src_ap), nm, 16)
                else:
                    p.run("gpsimd", lambda dst_ap=dst_ap, src_ap=src_ap, slow=slow:
                          ng.dma_start(out=dst_ap, in_=src_ap,
                                       allow_slow_non_contiguous=slow), nm, 16)
                return (nm, p.cnt[nm])

            def wait(self, eng, mark):
                p.wait(eng, mark[0], mark[1])

        sring = Ring("sync", ["L0", "L1", "L2", "L3", "L4", "L5"])
        ering = Ring("gpsimd", ["E0", "E1", "E2"])

        def gp_load(dst_ap, src_ap, slow=False):
            p.wait("gpsimd", "GDm", p.cnt["GDm"])
            return Gp(lambda dst_ap=dst_ap, src_ap=src_ap, slow=slow:
                      ng.dma_start(out=dst_ap, in_=src_ap,
                                   allow_slow_non_contiguous=slow), inc="GDm", n=16)

        def tap_out(name, src):
            m = gp_load(d_tap[name], src)
            p.wait("vector", "GDm", m)
            p.wait("scalar", "GDm", m)
            p.wait("tensor", "GDm", m)

        def mm_fence(bank_ap):
            Tr(lambda bank_ap=bank_ap: ntn.matmul(
                bank_ap, zerc[:, 0:1], onec[:, 0:1],
                start=False, stop=True, skip_group_check=True), inc=None)
            return Tr(lambda bank_ap=bank_ap: ntn.matmul(
                bank_ap, zerc[:, 0:1], onec[:, 0:1],
                start=False, stop=True, skip_group_check=True), inc="T")

        def range_reduce(dst, src, iscr):
            ii = iscr.bitcast(I32)
            Vr(lambda: nv.tensor_single_scalar(dst, src, INV_2PI, ALU.mult), inc=None)
            Vr(lambda: nv.tensor_copy(ii, dst), inc=None)
            Vr(lambda: nv.tensor_copy(iscr, ii), inc=None)
            Vr(lambda: nv.scalar_tensor_tensor(dst, iscr, -TWO_PI, src, ALU.mult, ALU.add), inc=None)
            Vr(lambda: nv.tensor_single_scalar(iscr, dst, PI, ALU.is_gt), inc=None)
            Vr(lambda: nv.scalar_tensor_tensor(dst, iscr, -TWO_PI, dst, ALU.mult, ALU.add), inc=None)
            Vr(lambda: nv.tensor_single_scalar(iscr, dst, -PI, ALU.is_lt), inc=None)
            Vr(lambda: nv.scalar_tensor_tensor(dst, iscr, TWO_PI, dst, ALU.mult, ALU.add), inc=None)
            Vr(lambda: nv.tensor_single_scalar(dst, dst, PI_LO, ALU.min), inc=None)
            Vr(lambda: nv.tensor_single_scalar(dst, dst, -PI_LO, ALU.max), inc=None)

        def cos_from_red(dst, red, mscr):
            Vr(lambda: nv.tensor_scalar(mscr, red, -1.0, PI / 2, ALU.mult, ALU.add), inc=None)
            Vr(lambda: nv.tensor_single_scalar(dst, mscr, PI, ALU.is_gt), inc=None)
            Vr(lambda: nv.scalar_tensor_tensor(mscr, dst, -TWO_PI, mscr, ALU.mult, ALU.add), inc=None)
            Vr(lambda: nv.tensor_single_scalar(mscr, mscr, PI_LO, ALU.min), inc=None)
            Vr(lambda: nv.tensor_single_scalar(mscr, mscr, -PI_LO, ALU.max))
            p.wait("scalar", "V")
            Sc(lambda: ns.activation(dst, mscr, AF.Sin))

        DBL = [(256, "c256", "s256", "kr256", "ki256"),
               (512, "c512", "s512", "kr512", "ki512"),
               (1024, "c1024", "s1024", "kr1024", "ki1024")]

        def build_ptab(lay, g, with_rho):
            ath = pp["absth"][:, lay, g:g + 1].broadcast_to([128, 256])
            Vr(lambda ath=ath: nv.tensor_tensor(
                u[:, 0, 0:256], iot[:, 0:256], ath, ALU.mult))
            range_reduce(u[:, 1, 0:256], u[:, 0, 0:256], tabs[:, 0, 0:256])
            cos_from_red(tabs[:, 0, 0:256], u[:, 1, 0:256], tabs[:, 1, 0:256])
            Sc(lambda: ns.activation(tabs[:, 1, 0:256], u[:, 1, 0:256], AF.Sin))
            if with_rho:
                lnr = pp["lnrho"][:, lay, g:g + 1].broadcast_to([128, 256])
                Vr(lambda lnr=lnr: nv.tensor_tensor(
                    u[:, 0, 0:256], iot[:, 0:256], lnr, ALU.mult))
                p.wait("scalar", "V")
                Sc(lambda: ns.activation(u[:, 0, 0:256], u[:, 0, 0:256], AF.Exp))
                p.wait("vector", "S")
                Vr(lambda: nv.tensor_tensor(
                    tabs[:, 0, 0:256], tabs[:, 0, 0:256], u[:, 0, 0:256], ALU.mult), inc=None)
                Vr(lambda: nv.tensor_tensor(
                    tabs[:, 1, 0:256], tabs[:, 1, 0:256], u[:, 0, 0:256], ALU.mult), inc=None)
            else:
                p.wait("vector", "S")
            for L, ck, sk, krk, kik in DBL:
                cc = pp[krk if with_rho else ck][:, lay, g:g + 1]
                ss = pp[kik if with_rho else sk][:, lay, g:g + 1]
                ssb = ss.broadcast_to([128, L])
                Vr(lambda L=L, ssb=ssb: nv.tensor_tensor(
                    u[:, 0, 0:L], tabs[:, 1, 0:L], ssb, ALU.mult), inc=None)
                Vr(lambda L=L, ssb=ssb: nv.tensor_tensor(
                    u[:, 1, 0:L], tabs[:, 0, 0:L], ssb, ALU.mult), inc=None)
                Vr(lambda L=L, cc=cc: nv.scalar_tensor_tensor(
                    tabs[:, 0, L:2 * L], tabs[:, 0, 0:L], cc, u[:, 0, 0:L],
                    ALU.mult, ALU.subtract), inc=None)
                Vr(lambda L=L, cc=cc: nv.scalar_tensor_tensor(
                    tabs[:, 1, L:2 * L], tabs[:, 1, 0:L], cc, u[:, 1, 0:L],
                    ALU.mult, ALU.add), inc=None)

        def ln_stats_and_bcast(tok, denom, extra_wait_v=None):
            mu, rr = rowb[:, 0, :], rowb[:, 1, :]
            p.wait("vector", "T")
            Vr(lambda: nv.tensor_single_scalar(mu, ps_st[:, 0, :], 1.0 / denom, ALU.mult), inc=None)
            Vr(lambda: nv.tensor_tensor(rr, mu, mu, ALU.mult), inc=None)
            Vr(lambda: nv.scalar_tensor_tensor(rr, ps_st[:, 1, :], 1.0 / denom, rr,
                                               ALU.mult, ALU.subtract))
            st_free = p.cnt["V"]
            Vr(lambda: nv.tensor_single_scalar(rr, rr, LN_EPS, ALU.add))
            p.wait("scalar", "V")
            Sc(lambda: ns.activation(rr, rr, AF.Sqrt))
            p.wait("vector", "S")
            Vr(lambda: nv.reciprocal(rr, rr), inc=None)
            Vr(lambda: nv.tensor_tensor(mu, rr, mu, ALU.mult))
            rows_done = p.cnt["V"]
            p.wait("tensor", "V", rows_done)
            if extra_wait_v is not None:
                p.wait("tensor", "V", extra_wait_v)
            p.wait("tensor", "S")
            Tr(lambda: ntn.matmul(ps_b[:, 0, :], oner[:], rr, start=True, stop=True), inc=None)
            Tr(lambda: ntn.matmul(ps_b[:, 1, :], oner[:], mu, start=True, stop=True), inc=None)
            mm_fence(ps_b[0:1, 0, 0:1])
            p.wait("vector", "T")
            sl = slice(tok * TT, (tok + 1) * TT)
            Vr(lambda sl=sl: nv.tensor_copy(bcst[:, 0, sl], ps_b[:, 0, :]), inc=None)
            Vr(lambda sl=sl: nv.tensor_copy(bcst[:, 1, sl], ps_b[:, 1, :]))
            return p.cnt["V"], st_free

        Vr(lambda: nv.memset(onec[:], 1.0), inc=None)
        Vr(lambda: nv.memset(zerc[:], 0.0), inc=None)
        Vr(lambda: nv.memset(fcell[:], 1.0), inc=None)
        Vr(lambda: nv.memset(nonec[:], -1.0), inc=None)
        Vr(lambda: nv.memset(oner[:], 1.0))
        const_v = p.cnt["V"]
        p.wait("tensor", "V", const_v)
        p.wait("scalar", "V", const_v)
        p.wait("gpsimd", "V", const_v)
        for dst, src in [(iot, dd["iota"]), (binS, dd["bin"]), (b1S, dd["b1"]),
                         (b2S, dd["b2"]), (lnlwS, dd["lnlw"]), (lnlbS, dd["lnlb"]),
                         (boutS, dd["bout"]), (bAS, dd["bA"]), (selS, dd["sel"])] + \
                        [(prm[k], dd[k]) for k in prm]:
            gp_load(dst[:], src)
        ld_params = p.cnt["GDm"]

        phr, phi = prm["phzr"][:, :, :], prm["phzi"][:, :, :]
        rho16, lnrho16 = pp["rho"][:, :, :], pp["lnrho"][:, :, :]
        absth16, sgn16 = pp["absth"][:, :, :], pp["sgn"][:, :, :]
        eR16, eI16 = pp["eR"][:, :, :], pp["eI"][:, :, :]
        p2r16, p2i16 = pp["p2r"][:, :, :], pp["p2i"][:, :, :]
        ln2047_16, th2047_16 = pp["ln2047"][:, :, :], pp["th2047"][:, :, :]
        A16, B16, C16, D16 = t16a[:], t16b[:], t16c[:], t16d[:]

        p.wait("vector", "GDm", ld_params)
        Vr(lambda: nv.tensor_tensor(A16, phr, phr, ALU.mult), inc=None)
        Vr(lambda: nv.tensor_tensor(B16, phi, phi, ALU.mult), inc=None)
        Vr(lambda: nv.tensor_tensor(A16, A16, B16, ALU.add), inc=None)
        Vr(lambda: nv.tensor_single_scalar(A16, A16, 1e-30, ALU.max))
        p.wait("scalar", "V")
        Sc(lambda: ns.activation(B16, A16, AF.Sqrt))
        p.wait("vector", "S")
        Vr(lambda: nv.reciprocal(C16, B16), inc=None)
        Vr(lambda: nv.tensor_tensor(eR16, phr, C16, ALU.mult), inc=None)
        Vr(lambda: nv.tensor_tensor(eI16, phi, C16, ALU.mult))
        p.wait("scalar", "V")
        Sc(lambda: ns.activation(rho16, B16, AF.Tanh), inc=None)
        Sc(lambda: ns.activation(lnrho16, rho16, AF.Ln), inc=None)
        Sc(lambda: ns.activation(A16, eR16, AF.Abs), inc=None)
        Sc(lambda: ns.activation(B16, eI16, AF.Abs))
        p.wait("vector", "S")
        Vr(lambda: nv.tensor_single_scalar(ln2047_16, lnrho16, 2047.0, ALU.mult), inc=None)
        Vr(lambda: nv.tensor_tensor(C16, A16, B16, ALU.min), inc=None)
        Vr(lambda: nv.tensor_tensor(D16, A16, B16, ALU.max), inc=None)
        Vr(lambda: nv.reciprocal(D16, D16), inc=None)
        Vr(lambda: nv.tensor_tensor(C16, C16, D16, ALU.mult))
        p.wait("scalar", "V")
        Sc(lambda: ns.activation(D16, C16, AF.Arctan))
        p.wait("scalar", "S")
        Sc(lambda: ns.activation(A16, D16, AF.Sin))
        p.wait("vector", "S")
        Vr(lambda: nv.tensor_tensor(B16, A16, A16, ALU.mult), inc=None)
        Vr(lambda: nv.tensor_scalar(B16, B16, -1.0, 1.0, ALU.mult, ALU.add))
        p.wait("scalar", "V")
        Sc(lambda: ns.activation(B16, B16, AF.Sqrt))
        p.wait("vector", "S")
        Vr(lambda: nv.tensor_tensor(C16, C16, B16, ALU.mult), inc=None)
        Vr(lambda: nv.tensor_tensor(C16, C16, A16, ALU.subtract), inc=None)
        Vr(lambda: nv.tensor_tensor(C16, C16, B16, ALU.mult), inc=None)
        Vr(lambda: nv.tensor_tensor(D16, D16, C16, ALU.add), inc=None)
        Vr(lambda: nv.tensor_tensor(A16, eI16, eI16, ALU.mult), inc=None)
        Vr(lambda: nv.tensor_tensor(B16, eR16, eR16, ALU.mult), inc=None)
        Vr(lambda: nv.tensor_tensor(A16, A16, B16, ALU.is_gt), inc=None)
        Vr(lambda: nv.tensor_scalar(B16, D16, -2.0, PI / 2, ALU.mult, ALU.add), inc=None)
        Vr(lambda: nv.tensor_tensor(A16, A16, B16, ALU.mult), inc=None)
        Vr(lambda: nv.tensor_tensor(D16, D16, A16, ALU.add), inc=None)
        Vr(lambda: nv.tensor_single_scalar(A16, eR16, 0.0, ALU.is_lt), inc=None)
        Vr(lambda: nv.tensor_scalar(B16, D16, -2.0, PI, ALU.mult, ALU.add), inc=None)
        Vr(lambda: nv.tensor_tensor(A16, A16, B16, ALU.mult), inc=None)
        Vr(lambda: nv.tensor_tensor(absth16, D16, A16, ALU.add), inc=None)
        Vr(lambda: nv.tensor_single_scalar(A16, eI16, 0.0, ALU.is_ge), inc=None)
        Vr(lambda: nv.tensor_scalar(sgn16, A16, 2.0, -1.0, ALU.mult, ALU.add), inc=None)
        Vr(lambda: nv.tensor_single_scalar(th2047_16, absth16, 2047.0, ALU.mult), inc=None)
        Vr(lambda: nv.tensor_single_scalar(A16, lnrho16, 2048.0, ALU.mult))
        p.wait("scalar", "V")
        Sc(lambda: ns.activation(A16, A16, AF.Exp))
        p.wait("vector", "S")
        Vr(lambda: nv.tensor_single_scalar(B16, absth16, 2048.0, ALU.mult), inc=None)
        range_reduce(C16, B16, D16)
        Vr(lambda: nv.tensor_copy(D16, C16))
        p.wait("scalar", "V")
        Sc(lambda: ns.activation(B16, C16, AF.Sin))
        p.wait("vector", "S")
        Vr(lambda: nv.tensor_tensor(p2i16, A16, B16, ALU.mult), inc=None)
        Vr(lambda: nv.tensor_tensor(p2i16, p2i16, sgn16, ALU.mult), inc=None)
        cos_from_red(B16, D16, C16)
        p.wait("vector", "S")
        Vr(lambda: nv.tensor_tensor(p2r16, A16, B16, ALU.mult), inc=None)
        Vr(lambda: nv.tensor_single_scalar(pp["nsgn"][:, :, :], sgn16, -1.0, ALU.mult), inc=None)
        range_reduce(C16, th2047_16, D16)
        Vr(lambda: nv.tensor_copy(D16, C16))
        cos_from_red(pp["c47"][:, :, :], C16, B16)
        Sc(lambda: ns.activation(pp["s47"][:, :, :], D16, AF.Sin))
        p.wait("vector", "S")
        for Lf, ck, sk, krk, kik in [(256.0, "c256", "s256", "kr256", "ki256"),
                                     (512.0, "c512", "s512", "kr512", "ki512"),
                                     (1024.0, "c1024", "s1024", "kr1024", "ki1024")]:
            Vr(lambda Lf=Lf: nv.tensor_single_scalar(A16, absth16, Lf, ALU.mult), inc=None)
            range_reduce(C16, A16, D16)
            Vr(lambda: nv.tensor_copy(D16, C16), inc=None)
            Vr(lambda Lf=Lf: nv.tensor_single_scalar(A16, lnrho16, Lf, ALU.mult))
            cos_from_red(pp[ck][:, :, :], C16, B16)
            Sc(lambda sk=sk: ns.activation(pp[sk][:, :, :], D16, AF.Sin), inc=None)
            Sc(lambda: ns.activation(A16, A16, AF.Exp))
            p.wait("vector", "S")
            Vr(lambda krk=krk, ck=ck: nv.tensor_tensor(
                pp[krk][:, :, :], pp[ck][:, :, :], A16, ALU.mult), inc=None)
            Vr(lambda kik=kik, sk=sk: nv.tensor_tensor(
                pp[kik][:, :, :], pp[sk][:, :, :], A16, ALU.mult))
        mark("P1")

        tmark_last = 0
        epi_done_v = 0
        stfree_v = 0
        tmarks = [0, 0]
        smarks = [0, 0]
        for tok in range(NTOK):
            for k in range(KV):
                sl = k % 3
                p.wait("sync", "T", tmarks[0])
                p.wait("sync", "S", smarks[0])
                mk = sring.load_group([
                    (xb[:, sl, :], dd["xT"][k * 128:(k + 1) * 128, tok * TT:(tok + 1) * TT]),
                    (wtb[:, sl, :], dd["winT"][k * 128:(k + 1) * 128, :])])
                sring.wait("scalar", mk)
                p.wait("scalar", "T", tmarks[0])
                s_sq = Sc(lambda sl=sl: ns.activation(xeb[:, sl, :], xb[:, sl, :], AF.Square))
                smarks = [smarks[1], s_sq]
                sring.wait("tensor", mk)
                if k == 0:
                    p.wait("tensor", "V", epi_done_v)
                    p.wait("tensor", "V", stfree_v)
                for dm in range(GD):
                    Tr(lambda k=k, sl=sl, dm=dm: ntn.matmul(
                        ps_a[:, dm, :], wtb[:, sl, dm * 128:(dm + 1) * 128], xb[:, sl, :],
                        start=(k == 0), stop=(k == KV - 1)), inc=None)
                Tr(lambda k=k, sl=sl: ntn.matmul(
                    ps_st[:, 0, :], onec[:], xb[:, sl, :],
                    start=(k == 0), stop=(k == KV - 1)), inc=None)
                p.wait("tensor", "S", s_sq)
                if tok == 0:
                    Tr(lambda k=k, sl=sl: ntn.matmul(
                        ps_b[0:1, 0, 0:D], nonec[:], wtb[:, sl, :],
                        start=(k == 0), stop=(k == KV - 1)), inc=None)
                tm = Tr(lambda k=k, sl=sl: ntn.matmul(
                    ps_st[:, 1, :], onec[:], xeb[:, sl, :],
                    start=(k == 0), stop=(k == KV - 1)), inc="T")
                tmarks = [tmarks[1], tm]
            mm_fence(ps_st[0:1, 0, 0:1])
            if tok == 0:
                p.wait("vector", "T")
                rs_v = Vr(lambda: nv.tensor_copy(rowb[:, 1, 0:TT], ps_b[0:1, 0, 0:D]))
                p.wait("gpsimd", "V", rs_v)
                g1 = gp_load(bounce.ap()[0:1, 0:D], rowb[0:1, 1, 0:D])
                p.wait("gpsimd", "GDm", g1)
                g2 = gp_load(rsS[:], bounce.ap()[0, 0:D].rearrange("(g q) -> q g", q=128), slow=True)
                p.wait("vector", "GDm", g1)
            bc_v, stfree_v = ln_stats_and_bcast(tok, float(V), extra_wait_v=(rs_v if tok == 0 else None))
            if tok == 0:
                p.wait("vector", "GDm", g2)
            sl_t = slice(tok * TT, (tok + 1) * TT)
            for dm in range(GD):
                hsl = h[:, dm, sl_t]
                Vr(lambda dm=dm, hsl=hsl, sl_t=sl_t: nv.tensor_tensor(
                    hsl, ps_a[:, dm, :], bcst[:, 0, sl_t], ALU.mult), inc=None)
                Vr(lambda dm=dm, hsl=hsl, sl_t=sl_t: nv.scalar_tensor_tensor(
                    hsl, bcst[:, 1, sl_t], rsS[:, dm:dm + 1], hsl, ALU.mult, ALU.add), inc=None)
                Vr(lambda dm=dm, hsl=hsl: nv.tensor_scalar_add(hsl, hsl, binS[:, dm:dm + 1]))
            epi_done_v = p.cnt["V"]

        mark("P2_win")
        if "h0" in taps:
            p.wait("gpsimd", "V")
            tap_out("h0", h[:, :, :])

        hf = h[:, :, :]
        yf = y[:, :, :]

        def inner_ln(lay, wkey, bkey, act, y_free_t=0):
            nonlocal stfree_v, epi_done_v
            p.wait("scalar", "V")
            p.wait("scalar", "T", y_free_t)
            sq_s = Sc(lambda: ns.activation(yf, hf, AF.Square))
            for tok in range(NTOK):
                p.wait("tensor", "S", sq_s)
                p.wait("tensor", "V", stfree_v)
                for kt in range(GD):
                    Tr(lambda kt=kt, tok=tok: ntn.matmul(
                        ps_st[:, 0, :], onec[:], h[:, kt, tok * TT:(tok + 1) * TT],
                        start=(kt == 0), stop=(kt == GD - 1)), inc=None)
                for kt in range(GD):
                    Tr(lambda kt=kt, tok=tok: ntn.matmul(
                        ps_st[:, 1, :], onec[:], y[:, kt, tok * TT:(tok + 1) * TT],
                        start=(kt == 0), stop=(kt == GD - 1)), inc=None)
                mm_fence(ps_st[0:1, 0, 0:1])
                bc_v, stfree_v = ln_stats_and_bcast(tok, float(D))
            rb = bcst[:, 0:1, :].broadcast_to([128, GD, TC])
            mb = bcst[:, 1:2, :].broadcast_to([128, GD, TC])
            Vr(lambda: nv.tensor_tensor(yf, hf, rb, ALU.mult), inc=None)
            Vr(lambda: nv.tensor_tensor(yf, yf, mb, ALU.subtract))
            if act == AF.Copy:
                for g in range(GD):
                    wcol = prm[wkey][:, lay, g:g + 1]
                    bb = prm[bkey][:, lay, g:g + 1].broadcast_to([128, TC])
                    Vr(lambda g=g, wcol=wcol, bb=bb: nv.scalar_tensor_tensor(
                        y[:, g, :], y[:, g, :], wcol, bb, ALU.mult, ALU.add))
                return ("V", p.cnt["V"])
            p.wait("scalar", "V")
            for g in range(GD):
                Sc(lambda g=g: ns.activation(
                    y[:, g, :], y[:, g, :], act,
                    bias=prm[bkey][:, lay, g:g + 1], scale=prm[wkey][:, lay, g:g + 1]))
            return ("S", p.cnt["S"])

        prev_mm1_t = 0
        w2marks = [0, 0]
        w2i = 0
        for lay in range(DEPTH if stage >= 2 else 0):
            p.wait("sync", "T", prev_mm1_t)
            w1_ld = sring.load(w1t[:, :, :], dd["w1T"][lay].rearrange("(kt q) f -> q kt f", q=128))

            s_ln1 = inner_ln(lay, "ln1w", "ln1b", AF.Silu, y_free_t=prev_mm1_t)
            mark(f"L{lay}_ln1")
            if lay == 0 and "y0" in taps:
                p.wait("gpsimd", s_ln1[0], s_ln1[1])
                tap_out("y0", yf)

            p.wait("vector", s_ln1[0], s_ln1[1])
            for g in range(GD):
                build_ptab(lay, g, with_rho=False)
                rho_b = pp["rho"][:, lay, g:g + 1].broadcast_to([128, TC])
                Vr(lambda g=g: nv.tensor_tensor(u[:, 0, :], tabs[:, 0, :], y[:, g, :], ALU.mult), inc=None)
                Vr(lambda g=g: nv.tensor_tensor(u[:, 1, :], tabs[:, 1, :], y[:, g, :], ALU.mult), inc=None)
                Vr(lambda rho_b=rho_b: nv.tensor_tensor_scan(
                    u[:, 0, :], rho_b, u[:, 0, :], 0.0, ALU.mult, ALU.add), inc=None)
                Vr(lambda rho_b=rho_b: nv.tensor_tensor_scan(
                    u[:, 1, :], rho_b, u[:, 1, :], 0.0, ALU.mult, ALU.subtract), inc=None)
                u0e, u1e = u[:, 0, TC - 1:TC], u[:, 1, TC - 1:TC]
                c47c = pp["c47"][:, lay, g:g + 1]
                s47c = pp["s47"][:, lay, g:g + 1]
                sgnc = pp["sgn"][:, lay, g:g + 1]
                Vr(lambda u1e=u1e, s47c=s47c: nv.tensor_tensor(
                    c8a[:, 0:1], u1e, s47c, ALU.mult), inc=None)
                Vr(lambda g=g, u0e=u0e, c47c=c47c: nv.scalar_tensor_tensor(
                    ES[:, g:g + 1], u0e, c47c, c8a[:, 0:1], ALU.mult, ALU.subtract), inc=None)
                Vr(lambda u1e=u1e, c47c=c47c: nv.tensor_tensor(
                    c8a[:, 1:2], u1e, c47c, ALU.mult), inc=None)
                Vr(lambda u0e=u0e, s47c=s47c: nv.scalar_tensor_tensor(
                    c8b[:, 0:1], u0e, s47c, c8a[:, 1:2], ALU.mult, ALU.add), inc=None)
                Vr(lambda g=g, sgnc=sgnc: nv.tensor_tensor(
                    ES[:, 4 + g:5 + g], c8b[:, 0:1], sgnc, ALU.mult), inc=None)
                Vr(lambda g=g: nv.tensor_tensor(y[:, g, :], tabs[:, 0, :], u[:, 0, :], ALU.mult), inc=None)
                Vr(lambda g=g: nv.tensor_tensor(u[:, 0, :], tabs[:, 1, :], u[:, 1, :], ALU.mult), inc=None)
                Vr(lambda g=g: nv.tensor_tensor(y[:, g, :], y[:, g, :], u[:, 0, :], ALU.subtract))
            e_v = p.cnt["V"]

            p.wait("gpsimd", "V", e_v)
            gp1 = gp_load(ag_in.ap().rearrange("o (q k) -> (o q) k", q=128), ES[:])
            p.wait("gpsimd", "GDm", gp1)
            cc_m = Gp(lambda: ng.collective_compute(
                "AllGather", ALU.bypass, replica_groups=[list(range(NCORES))],
                ins=[ag_in.ap().opt()], outs=[ag_out.ap().opt()]))
            p.wait("gpsimd", "G", cc_m)
            gp2 = gp_load(EA[:, :, :], ag_out.ap()[0:7, :].rearrange("j (q k) -> q j k", q=128))

            build_ptab(lay, 0, with_rho=True)

            p.wait("vector", "GDm", gp2)
            Vr(lambda lay=lay: nv.tensor_copy(CCs[:, 0:4], prm["lcr"][:, lay, :]), inc=None)
            Vr(lambda lay=lay: nv.tensor_copy(CCs[:, 4:8], prm["lci"][:, lay, :]), inc=None)
            for j in range(7):
                Cr, Ci = CCs[:, 0:4], CCs[:, 4:8]
                p2r_l, p2i_l = pp["p2r"][:, lay, :], pp["p2i"][:, lay, :]
                Vr(lambda Cr=Cr, p2r_l=p2r_l: nv.tensor_tensor(c8a[:, 0:4], Cr, p2r_l, ALU.mult), inc=None)
                Vr(lambda Ci=Ci, p2i_l=p2i_l: nv.tensor_tensor(c8b[:, 0:4], Ci, p2i_l, ALU.mult), inc=None)
                Vr(lambda: nv.tensor_tensor(c8a[:, 0:4], c8a[:, 0:4], c8b[:, 0:4], ALU.subtract), inc=None)
                Vr(lambda j=j: nv.tensor_tensor(c8a[:, 0:4], c8a[:, 0:4], EA[:, j, 0:4], ALU.add), inc=None)
                Vr(lambda Cr=Cr, p2i_l=p2i_l: nv.tensor_tensor(c8b[:, 0:4], Cr, p2i_l, ALU.mult), inc=None)
                Vr(lambda Ci=Ci, p2r_l=p2r_l: nv.tensor_tensor(c8b[:, 4:8], Ci, p2r_l, ALU.mult), inc=None)
                Vr(lambda: nv.tensor_tensor(c8a[:, 4:8], c8b[:, 0:4], c8b[:, 4:8], ALU.add), inc=None)
                Vr(lambda j=j: nv.tensor_tensor(c8a[:, 4:8], c8a[:, 4:8], EA[:, j, 4:8], ALU.add), inc=None)
                Vr(lambda: nv.tensor_tensor(c8b[:, 0:8], c8a[:, 0:8], CCs[:, 0:8], ALU.subtract), inc=None)
                Vr(lambda j=j: nv.scalar_tensor_tensor(
                    CCs[:, 0:8], c8b[:, 0:8], selS[:, j:j + 1], CCs[:, 0:8], ALU.mult, ALU.add), inc=None)
            eRl, eIl = pp["eR"][:, lay, :], pp["eI"][:, lay, :]
            rhol = pp["rho"][:, lay, :]
            nsgnl = pp["nsgn"][:, lay, :]
            Vr(lambda eRl=eRl: nv.tensor_tensor(WIs[:, 0:4], CCs[:, 0:4], eRl, ALU.mult), inc=None)
            Vr(lambda eIl=eIl: nv.tensor_tensor(c8a[:, 0:4], CCs[:, 4:8], eIl, ALU.mult), inc=None)
            Vr(lambda: nv.tensor_tensor(WIs[:, 0:4], WIs[:, 0:4], c8a[:, 0:4], ALU.subtract), inc=None)
            Vr(lambda rhol=rhol: nv.tensor_tensor(WIs[:, 0:4], WIs[:, 0:4], rhol, ALU.mult), inc=None)
            Vr(lambda eIl=eIl: nv.tensor_tensor(WIs[:, 4:8], CCs[:, 0:4], eIl, ALU.mult), inc=None)
            Vr(lambda eRl=eRl: nv.tensor_tensor(c8a[:, 4:8], CCs[:, 4:8], eRl, ALU.mult), inc=None)
            Vr(lambda: nv.tensor_tensor(WIs[:, 4:8], WIs[:, 4:8], c8a[:, 4:8], ALU.add), inc=None)
            Vr(lambda rhol=rhol: nv.tensor_tensor(WIs[:, 4:8], WIs[:, 4:8], rhol, ALU.mult), inc=None)
            Vr(lambda nsgnl=nsgnl: nv.tensor_tensor(WIs[:, 4:8], WIs[:, 4:8], nsgnl, ALU.mult))

            for g in range(GD):
                if g > 0:
                    build_ptab(lay, g, with_rho=True)
                Vr(lambda g=g: nv.scalar_tensor_tensor(
                    y[:, g, :], tabs[:, 0, :], WIs[:, g:g + 1], y[:, g, :],
                    ALU.mult, ALU.add), inc=None)
                Vr(lambda g=g: nv.scalar_tensor_tensor(
                    y[:, g, :], tabs[:, 1, :], WIs[:, 4 + g:5 + g], y[:, g, :],
                    ALU.mult, ALU.add))
            if lay == 0 and "conv0" in taps:
                p.wait("gpsimd", "V")
                tap_out("conv0", yf)

            for g in range(GD):
                Vr(lambda g=g, lay=lay: nv.scalar_tensor_tensor(
                    h[:, g, :], y[:, g, :], prm["scl"][:, lay, g:g + 1], h[:, g, :],
                    ALU.mult, ALU.add), inc=None)
            Vr(lambda: nv.tensor_copy(c8a[:, 0:1], CCs[:, 0:1]))

            mark(f"L{lay}_scan")
            s_ln2 = inner_ln(lay, "ln2w", "ln2b", AF.Copy)
            mark(f"L{lay}_ln2")

            evac_s = [0, 0, 0]
            mm2_t = 0
            hupd_v = p.cnt["V"]
            sring.wait("tensor", w1_ld)
            p.wait("tensor", s_ln2[0], s_ln2[1])
            for tok in range(NTOK):
                p.wait("tensor", "V", hupd_v)

                def emit_mm2(fk, sl, pb, wl):
                    sring.wait("tensor", wl)
                    p.wait("tensor", "S", evac_s[pb])
                    for dm in range(GD):
                        Tr(lambda fk=fk, sl=sl, dm=dm, pb=pb: ntn.matmul(
                            ps_a[:, dm, :], w2b[:, sl, dm * 128:(dm + 1) * 128],
                            y1[:, pb, :],
                            start=(fk == 0), stop=(fk == KF - 1)),
                           inc=("T" if dm == GD - 1 else None))
                    w2marks[:] = [w2marks[1], p.cnt["T"]]

                pend = None
                for fk in range(KF):
                    sl = w2i % 3
                    w2i += 1
                    p.wait("sync", "T", w2marks[0])
                    wl = sring.load(w2b[:, sl, :],
                                    dd["w2T"][lay][fk * 128:(fk + 1) * 128, :])
                    pb = fk % 2
                    p.wait("tensor", "S", evac_s[pb])
                    for kt in range(GD):
                        Tr(lambda lay=lay, fk=fk, kt=kt, pb=pb, tok=tok: ntn.matmul(
                            ps_b[:, pb, :], w1t[:, kt, fk * 128:(fk + 1) * 128],
                            y[:, kt, tok * TT:(tok + 1) * TT],
                            start=(kt == 0), stop=(kt == GD - 1)), inc=None)
                    mm1_t = mm_fence(ps_b[0:1, pb, 0:1])
                    p.wait("scalar", "T", mm1_t)
                    evac_s[pb] = Sc(lambda lay=lay, fk=fk, pb=pb: ns.activation(
                        y1[:, pb, :], ps_b[:, pb, :], AF.Silu,
                        bias=b1S[:, lay, fk:fk + 1], scale=1.0))
                    if pend is not None:
                        emit_mm2(*pend)
                    pend = (fk, sl, pb, wl)
                emit_mm2(*pend)
                mm2_t = mm_fence(ps_a[0:1, 0, 0:1])
                p.wait("vector", "T", mm2_t)
                for dm in range(GD):
                    hsl = h[:, dm, tok * TT:(tok + 1) * TT]
                    Vr(lambda dm=dm, lay=lay, hsl=hsl: nv.scalar_tensor_tensor(
                        hsl, ps_a[:, dm, :], b2S[:, lay, dm:dm + 1], hsl,
                        ALU.add, ALU.add), inc=None)
                Vr(lambda: nv.tensor_copy(c8a[:, 0:1], CCs[:, 0:1]))
                hupd_v = p.cnt["V"]
            prev_mm1_t = p.cnt["T"]
            mark(f"L{lay}_ffn")

            if lay == 0 and "h1" in taps:
                p.wait("gpsimd", "V", hupd_v)
                tap_out("h1", hf)

        if stage >= 2 and "hfin" in taps:
            p.wait("gpsimd", "V")
            tap_out("hfin", hf)

        if stage >= 3:
            def final_ln():
                nonlocal stfree_v
                p.wait("scalar", "V")
                p.wait("scalar", "T", prev_mm1_t if stage >= 2 else 0)
                sq_s = Sc(lambda: ns.activation(yf, hf, AF.Square))
                for tok in range(NTOK):
                    p.wait("tensor", "S", sq_s)
                    p.wait("tensor", "V", stfree_v)
                    for kt in range(GD):
                        Tr(lambda kt=kt, tok=tok: ntn.matmul(
                            ps_st[:, 0, :], onec[:], h[:, kt, tok * TT:(tok + 1) * TT],
                            start=(kt == 0), stop=(kt == GD - 1)), inc=None)
                    for kt in range(GD):
                        Tr(lambda kt=kt, tok=tok: ntn.matmul(
                            ps_st[:, 1, :], onec[:], y[:, kt, tok * TT:(tok + 1) * TT],
                            start=(kt == 0), stop=(kt == GD - 1)),
                           inc=("T" if kt == GD - 1 else None))
                    bc_v, stfree_v = ln_stats_and_bcast(tok, float(D))
                rb = bcst[:, 0:1, :].broadcast_to([128, GD, TC])
                mbb = bcst[:, 1:2, :].broadcast_to([128, GD, TC])
                Vr(lambda: nv.tensor_tensor(yf, hf, rb, ALU.mult), inc=None)
                Vr(lambda: nv.tensor_tensor(yf, yf, mbb, ALU.subtract))
                for g in range(GD):
                    bb = lnlbS[:, g:g + 1].broadcast_to([128, TC])
                    Vr(lambda g=g, bb=bb: nv.scalar_tensor_tensor(
                        y[:, g, :], y[:, g, :], lnlwS[:, g:g + 1], bb,
                        ALU.mult, ALU.add))

            final_ln()
            mark("P4_ln")
            if "hln" in taps:
                p.wait("gpsimd", "V")
                tap_out("hln", yf)

            def x1sb(k):
                buf = h if k < 16 else w1t
                kk = k % 16
                return buf[:, kk // 4, (kk % 4) * TT:(kk % 4 + 1) * TT]

            hln_v = p.cnt["V"]
            p.wait("sync", "T", p.cnt["T"])
            wout_mark = [hln_v] * 4
            wt_marks = [0, 0]
            xe_mark = [0, 0, 0]
            sq2_marks = [0, 0, 0]
            stfree2 = 0
            ab_marks = [0, 0]
            rs_evac = 0
            a_epi_v = 0
            rsA_gd = 0
            a_out_mark = [None, None]
            ai = 0
            for tok in range(NTOK):
                sl_t = slice(tok * TT, (tok + 1) * TT)
                p.wait("tensor", "V", stfree_v)
                for vm in range(KV):
                    sl = vm % 3
                    pa = vm % 4
                    p.wait("sync", "T", wt_marks[0])
                    wl = sring.load(wtb[:, sl, :],
                                    dd["woutT"][:, vm * 128:(vm + 1) * 128].rearrange(
                                        "(kt q) m -> q kt m", q=128))
                    p.wait("gpsimd", "V", xe_mark[sl])
                    xl = ering.load(xeb[:, sl, :],
                                    dd["xT"][vm * 128:(vm + 1) * 128, sl_t])
                    sring.wait("tensor", wl)
                    p.wait("tensor", "V", wout_mark[pa])
                    for kt in range(GD):
                        Tr(lambda vm=vm, sl=sl, kt=kt, pa=pa, sl_t=sl_t: ntn.matmul(
                            ps_a[:, pa, :], wtb[:, sl, kt * 128:(kt + 1) * 128],
                            y[:, kt, sl_t],
                            start=(kt == 0), stop=(kt == GD - 1)), inc=None)
                    wt_marks = [wt_marks[1], mm_fence(ps_a[0:1, pa, 0:1])]
                    x1t = x1sb(vm)
                    p.wait("vector", "T")
                    ering.wait("vector", xl)
                    Vr(lambda vm=vm, sl=sl, pa=pa, x1t=x1t: nv.scalar_tensor_tensor(
                        x1t, ps_a[:, pa, :], boutS[:, vm:vm + 1], xeb[:, sl, :],
                        ALU.add, ALU.add))
                    epi_v = p.cnt["V"]
                    wout_mark[pa] = epi_v
                    xe_mark[sl] = epi_v
                    su = vm % 3
                    p.wait("scalar", "V", epi_v)
                    p.wait("scalar", "T", sq2_marks[su])
                    sqs = Sc(lambda su=su, x1t=x1t: ns.activation(
                        u[:, 0, su * TT:(su + 1) * TT], x1t, AF.Square))
                    p.wait("tensor", "S", sqs)
                    if vm == 0:
                        p.wait("tensor", "V", stfree2)
                    Tr(lambda vm=vm, x1t=x1t: ntn.matmul(
                        ps_st[:, 0, :], onec[:], x1t,
                        start=(vm == 0), stop=(vm == KV - 1)), inc=None)
                    sq2_marks[su] = Tr(lambda vm=vm, su=su: ntn.matmul(
                        ps_st[:, 1, :], onec[:], u[:, 0, su * TT:(su + 1) * TT],
                        start=(vm == 0), stop=(vm == KV - 1)), inc="T")
                mm_fence(ps_st[0:1, 0, 0:1])
                bc_v, stfree2 = ln_stats_and_bcast(tok, float(V))

                if "x1" in taps:
                    p.wait("gpsimd", "V", bc_v)
                    for vm in range(KV):
                        gp_load(d_tap["x1"][vm * 128:(vm + 1) * 128, sl_t], x1sb(vm))
                    p.wait("vector", "GDm")

                if stage < 4:
                    continue
                for mb in range(8):
                    p.wait("tensor", "V", bc_v if mb == 0 else a_epi_v)
                    for kb in range(KV // 4):
                        j = ai % 2
                        ai += 1
                        p.wait("sync", "T", ab_marks[0])
                        al = sring.load(
                            ab4[:, j],
                            dd["aT"][kb * 512:(kb + 1) * 512,
                                     mb * TT:(mb + 1) * TT].rearrange(
                                         "(c q) m -> q c m", q=128))
                        sring.wait("tensor", al)
                        if tok == 0 and kb == 0:
                            p.wait("tensor", "V", rs_evac)
                        for c in range(4):
                            k = kb * 4 + c
                            for mt in range(4):
                                Tr(lambda j=j, c=c, mt=mt, k=k: ntn.matmul(
                                    ps_a[:, mt, :], ab4[:, j, c, mt * 128:(mt + 1) * 128],
                                    x1sb(k),
                                    start=(k == 0), stop=(k == KV - 1)),
                                   inc=(None if (mt < 3 or tok == 0 or c < 3) else "T"))
                            if tok == 0:
                                Tr(lambda j=j, c=c, k=k: ntn.matmul(
                                    ps_b[0:1, 0, :], nonec[:], ab4[:, j, c, :],
                                    start=(k == 0), stop=(k == KV - 1)),
                                   inc=("T" if c == 3 else None))
                        ab_marks = ab_marks[1:] + [p.cnt["T"]]
                    mm_t = mm_fence(ps_a[0:1, 0, 0:1])
                    ab_marks = ab_marks[1:] + [mm_t]
                    if tok == 0:
                        p.wait("vector", "T", mm_t)
                        p.wait("vector", "GDm")
                        rs_evac = Vr(lambda: nv.tensor_copy(rowb[:, 1, :], ps_b[0:1, 0, :]))
                        p.wait("gpsimd", "V", rs_evac)
                        gb1 = gp_load(bounce.ap()[0:1, mb * TT:(mb + 1) * TT], rowb[0:1, 1, :])
                        p.wait("gpsimd", "GDm", gb1)
                        rsA_gd = gp_load(
                            rsAS[:, mb * 4:(mb + 1) * 4],
                            bounce.ap()[0, mb * TT:(mb + 1) * TT].rearrange("(g q) -> q g", q=128),
                            slow=True)
                    p.wait("vector", "T", mm_t)
                    if tok == 0:
                        p.wait("vector", "GDm", rsA_gd)
                    for mt in range(4):
                        vm = mb * 4 + mt
                        sl2 = mt % 2
                        esl = u[:, 1, sl2 * TT:(sl2 + 1) * TT]
                        if a_out_mark[sl2] is not None:
                            p.wait("vector", a_out_mark[sl2][0], a_out_mark[sl2][1])
                        Vr(lambda mt=mt, esl=esl, sl_t=sl_t: nv.tensor_tensor(
                            esl, ps_a[:, mt, :], bcst[:, 0, sl_t], ALU.mult), inc=None)
                        Vr(lambda vm=vm, esl=esl, sl_t=sl_t: nv.scalar_tensor_tensor(
                            esl, bcst[:, 1, sl_t], rsAS[:, vm:vm + 1], esl,
                            ALU.mult, ALU.add), inc=None)
                        Vr(lambda vm=vm, esl=esl: nv.scalar_tensor_tensor(
                            esl, esl, bAS[:, vm:vm + 1], x1sb(vm),
                            ALU.add, ALU.add))
                        ep_v = p.cnt["V"]
                        p.wait("scalar", "V", ep_v)
                        onm = "O0" if sl2 % 2 == 0 else "O1"
                        p.wait("scalar", onm, p.cnt[onm])
                        p.run("scalar", lambda vm=vm, esl=esl, sl_t=sl_t:
                              nc.scalar.dma_start(out=outT[vm * 128:(vm + 1) * 128, sl_t], in_=esl),
                              onm, 16)
                        a_out_mark[sl2] = (onm, p.cnt[onm])
                    a_epi_v = p.cnt["V"]
                wout_mark = [a_epi_v] * 4
            mark("P4_wout")

        if stage >= 4:
            mark("P5_A")
        p.wait("sync", "O0", p.cnt["O0"])
        p.wait("sync", "O1", p.cnt["O1"])
        p.wait("sync", "GDm", p.cnt["GDm"])
        p.emit()
    return nc, d_tap


def kernel(**inputs):
    in_maps = stage_inputs(inputs)
    nc, _ = build_program(taps=DEBUG_TAPS, stage=STAGE)
    res = run_bass_kernel_spmd(nc, in_maps, core_ids=list(range(NCORES)))
    out = np.empty((B, L, V), dtype=np.float32)
    for c in range(NCORES):
        b, q = c // QPER, c % QPER
        out[b, q * TC:(q + 1) * TC, :] = res.results[c]["outT"].T
    return out


# version 10
# speedup vs baseline: 1.0847x; 1.0036x over previous
from contextlib import ExitStack

import numpy as np
import concourse.bass as bass
import concourse.mybir as mybir
from concourse.bass_utils import run_bass_kernel_spmd

F32 = mybir.dt.float32
I32 = mybir.dt.int32
AF = mybir.ActivationFunctionType
ALU = mybir.AluOpType

B, L, V, D, FF, DEPTH = 2, 8192, 4096, 512, 2048, 4
NCORES, QPER = 8, 4
TC = L // QPER
TT = 512
NTOK = TC // TT
KV = V // 128
GD = D // 128
KF = FF // 128
PI = float(np.pi)
TWO_PI = float(2 * np.pi)
INV_2PI = float(1.0 / (2 * np.pi))
PI_LO = float(np.float32(3.1415925))
LN_EPS = 1e-5

ENGINES = ("tensor", "vector", "scalar", "gpsimd", "sync")

DEBUG_TAPS = ()
STAGE = 4
PHASE_MARKS = []


class Prog:

    SELF = {"vector": "V", "scalar": "S", "gpsimd": "G"}

    def __init__(self, nc, stack, sem_names):
        self.nc = nc
        self.q = {e: [] for e in ENGINES}
        self.sem = {n: stack.enter_context(nc.semaphore(n)) for n in sem_names}
        self.cnt = {n: 0 for n in sem_names}

    def run(self, eng, fn, inc=None, n=1):
        selfsem = self.SELF.get(eng)
        if selfsem is not None and inc is None:
            inc, n = selfsem, 1
        if inc is not None:
            if selfsem is not None and inc == selfsem:
                prev = self.cnt[inc]
                if prev:
                    self.q[eng].append(("w", self.sem[inc], prev))
            self.cnt[inc] += n
            sem = self.sem[inc]
            self.q[eng].append(lambda fn=fn, sem=sem, n=n: fn().then_inc(sem, n))
            return self.cnt[inc]
        self.q[eng].append(fn)
        return None

    def wait(self, eng, name, val=None):
        v = self.cnt[name] if val is None else val
        if v <= 0:
            return
        self.q[eng].append(("w", self.sem[name], v))

    def emit(self):
        with self.nc.Block() as blk:
            for e in ENGINES:
                def body(engobj, e=e):
                    for item in self.q[e]:
                        if isinstance(item, tuple):
                            engobj.wait_ge(item[1], item[2])
                        else:
                            item()
                getattr(blk, e)(body)


def stage_inputs(inp):
    f = np.ascontiguousarray
    x = np.asarray(inp["L0_attn_x"], dtype=np.float32)

    def ch(v):
        return f(np.asarray(v, np.float32).reshape(GD, 128).T)

    def chl(v):
        return f(np.asarray(v, np.float32).reshape(DEPTH, GD, 128).transpose(2, 0, 1))

    def vw(v, g):
        return f(np.asarray(v, np.float32).reshape(g, 128).T)

    common = {
        "winT": f(np.asarray(inp["L0_attn_Win"], np.float32).T),
        "w1T": f(np.asarray(inp["L0_attn_W1"], np.float32).transpose(0, 2, 1)),
        "w2T": f(np.asarray(inp["L0_attn_W2"], np.float32).transpose(0, 2, 1)),
        "woutT": f(np.asarray(inp["L0_attn_Wout"], np.float32).T),
        "aT": f(np.asarray(inp["L0_ffn_A"], np.float32).T),
        "bin": ch(inp["L0_attn_bin"]),
        "b1": f(np.asarray(inp["L0_attn_b1"], np.float32).reshape(DEPTH, KF, 128).transpose(2, 0, 1)),
        "b2": chl(inp["L0_attn_b2"]),
        "ln1w": chl(inp["L0_attn_ln1_w"]), "ln1b": chl(inp["L0_attn_ln1_b"]),
        "ln2w": chl(inp["L0_attn_ln2_w"]), "ln2b": chl(inp["L0_attn_ln2_b"]),
        "phzr": chl(inp["L0_attn_phz_re"]), "phzi": chl(inp["L0_attn_phz_im"]),
        "lcr": chl(inp["L0_attn_lc_re"]), "lci": chl(inp["L0_attn_lc_im"]),
        "scl": chl(inp["L0_attn_sc_lin"]),
        "lnlw": ch(inp["L0_attn_lnl_w"]), "lnlb": ch(inp["L0_attn_lnl_b"]),
        "bout": vw(inp["L0_attn_bout"], KV),
        "bA": vw(inp["L0_ffn_b"], KV),
        "iota": f(np.broadcast_to(np.arange(TC, dtype=np.float32), (128, TC)).copy()),
    }
    in_maps = []
    for c in range(NCORES):
        b, q = c // QPER, c % QPER
        xT = f(x[b, q * TC:(q + 1) * TC, :].T)
        sel = np.zeros((128, 7), np.float32)
        for j in range(7):
            if QPER * b <= j < QPER * b + q:
                sel[:, j] = 1.0
        in_maps.append(dict(common, xT=xT, sel=sel))
    return in_maps


def build_program(taps=(), stage=4):
    PHASE_MARKS.clear()
    nc = bass.Bass("TRN2", target_bir_lowering=False)
    taps = set(taps)
    nv, ns, ntn, ng = nc.vector, nc.scalar, nc.tensor, nc.gpsimd

    dd = {}
    for name, shape in [
        ("xT", [V, TC]), ("winT", [V, D]), ("w1T", [DEPTH, D, FF]), ("w2T", [DEPTH, FF, D]),
        ("woutT", [D, V]), ("aT", [V, V]), ("bin", [128, GD]),
        ("b1", [128, DEPTH, KF]), ("b2", [128, DEPTH, GD]),
        ("ln1w", [128, DEPTH, GD]), ("ln1b", [128, DEPTH, GD]),
        ("ln2w", [128, DEPTH, GD]), ("ln2b", [128, DEPTH, GD]),
        ("phzr", [128, DEPTH, GD]), ("phzi", [128, DEPTH, GD]),
        ("lcr", [128, DEPTH, GD]), ("lci", [128, DEPTH, GD]),
        ("scl", [128, DEPTH, GD]), ("lnlw", [128, GD]), ("lnlb", [128, GD]),
        ("bout", [128, KV]), ("bA", [128, KV]), ("iota", [128, TC]), ("sel", [128, 7]),
    ]:
        dd[name] = nc.dram_tensor(name, shape, F32, kind="ExternalInput").ap()

    outT = nc.dram_tensor("outT", [V, TC], F32, kind="ExternalOutput").ap() if stage >= 4 else None
    d_tap = {}
    for t in taps:
        shp = {"h0": [128, GD * TC], "y0": [128, GD * TC], "conv0": [128, GD * TC],
               "h1": [128, GD * TC], "hfin": [128, GD * TC], "hln": [128, GD * TC],
               "x1": [V, TC]}[t]
        d_tap[t] = nc.dram_tensor("tap_" + t, shp, F32, kind="ExternalOutput").ap()

    ag_in = nc.dram_tensor("ag_in", [1, 1024], F32)
    ag_out = nc.dram_tensor("ag_out", [NCORES, 1024], F32)
    bounce = nc.dram_tensor("bounce", [1, V], F32)

    st = ExitStack()
    with st:
        def sbt(name, shape, dt=F32):
            return st.enter_context(nc.sbuf_tensor(name, shape, dt))

        h = sbt("h", [128, GD, TC])
        y = sbt("y", [128, GD, TC])
        tabs = sbt("tabs", [128, 2, TC])
        u = sbt("u", [128, 2, TC])
        bcst = sbt("bcst", [128, 2, TC])
        y1 = sbt("y1", [128, 2, TT])
        w1t = sbt("w1t", [128, GD, FF])
        w2b = sbt("w2b", [128, 3, TT])
        xb = sbt("xb", [128, 3, TT])
        wtb = sbt("wtb", [128, 3, TT])
        ab4 = sbt("ab4", [128, 2, 4, TT])
        xeb = sbt("xeb", [128, 3, TT])
        rowb = sbt("rowb", [1, 2, TT])
        iot = sbt("iot", [128, TC])
        onec = sbt("onec", [128, 1])
        zerc = sbt("zerc", [128, 1])
        nonec = sbt("nonec", [128, 1])
        oner = sbt("oner", [1, 128])
        binS = sbt("binS", [128, GD])
        b1S = sbt("b1S", [128, DEPTH, KF])
        b2S = sbt("b2S", [128, DEPTH, GD])
        prm = {k: sbt(k + "S", [128, DEPTH, GD]) for k in
               ["ln1w", "ln1b", "ln2w", "ln2b", "phzr", "phzi", "lcr", "lci", "scl"]}
        lnlwS = sbt("lnlwS", [128, GD])
        lnlbS = sbt("lnlbS", [128, GD])
        boutS = sbt("boutS", [128, KV])
        bAS = sbt("bAS", [128, KV])
        selS = sbt("selS", [128, 7])
        rsS = sbt("rsS", [128, GD])
        rsAS = sbt("rsAS", [128, KV])
        pp = {k: sbt("pp_" + k, [128, DEPTH, GD]) for k in
              ["rho", "lnrho", "absth", "sgn", "eR", "eI", "p2r", "p2i",
               "ln2047", "th2047", "c47", "s47", "nsgn",
               "c256", "s256", "c512", "s512", "c1024", "s1024",
               "kr256", "ki256", "kr512", "ki512", "kr1024", "ki1024"]}
        t16a = sbt("t16a", [128, DEPTH * GD])
        t16b = sbt("t16b", [128, DEPTH * GD])
        t16c = sbt("t16c", [128, DEPTH * GD])
        t16d = sbt("t16d", [128, DEPTH * GD])
        fcell = sbt("fcell", [1, 8])
        ES = sbt("ES", [128, 8])
        EA = sbt("EA", [128, 7, 8])
        CCs = sbt("CCs", [128, 8])
        WIs = sbt("WIs", [128, 8])
        c8a = sbt("c8a", [128, 8])
        c8b = sbt("c8b", [128, 8])

        ps_a = st.enter_context(nc.psum_tensor("ps_a", [128, 4, TT], F32))
        ps_b = st.enter_context(nc.psum_tensor("ps_b", [128, 2, TT], F32))
        ps_st = st.enter_context(nc.psum_tensor("ps_st", [1, 2, TT], F32))

        p = Prog(nc, st, ["T", "V", "S", "G", "GDm",
                          "L0", "L1", "L2", "L3", "L4", "L5",
                          "E0", "E1", "E2", "O0", "O1"])

        def Vr(fn, inc="V", n=1):
            return p.run("vector", fn, None, 1)

        def Sc(fn, inc="S", n=1):
            return p.run("scalar", fn, None, 1)

        def Tr(fn, inc=None, n=1):
            return p.run("tensor", fn, inc, n)

        def Gp(fn, inc=None, n=1):
            if inc is None:
                return p.run("gpsimd", fn, None, 1)
            return p.run("gpsimd", fn, inc, n)

        def Sy(fn, inc=None, n=1):
            return p.run("sync", fn, inc, n)

        def mark(name):
            PHASE_MARKS.append((name, dict(p.cnt)))

        class Ring:

            def __init__(self, eng, names):
                self.eng, self.names = eng, names
                self.i = 0

            def load_group(self, pairs):
                nm = self.names[self.i % len(self.names)]
                self.i += 1
                p.wait(self.eng, nm, p.cnt[nm])
                for dst_ap, src_ap in pairs:
                    if self.eng == "sync":
                        p.run("sync", lambda dst_ap=dst_ap, src_ap=src_ap:
                              nc.sync.dma_start(out=dst_ap, in_=src_ap), nm, 16)
                    else:
                        p.run("gpsimd", lambda dst_ap=dst_ap, src_ap=src_ap:
                              ng.dma_start(out=dst_ap, in_=src_ap), nm, 16)
                return (nm, p.cnt[nm])

            def load(self, dst_ap, src_ap, slow=False):
                nm = self.names[self.i % len(self.names)]
                self.i += 1
                p.wait(self.eng, nm, p.cnt[nm])
                if self.eng == "sync":
                    p.run("sync", lambda dst_ap=dst_ap, src_ap=src_ap:
                          nc.sync.dma_start(out=dst_ap, in_=src_ap), nm, 16)
                else:
                    p.run("gpsimd", lambda dst_ap=dst_ap, src_ap=src_ap, slow=slow:
                          ng.dma_start(out=dst_ap, in_=src_ap,
                                       allow_slow_non_contiguous=slow), nm, 16)
                return (nm, p.cnt[nm])

            def wait(self, eng, mark):
                p.wait(eng, mark[0], mark[1])

        sring = Ring("sync", ["L0", "L1", "L2", "L3", "L4", "L5"])
        ering = Ring("gpsimd", ["E0", "E1", "E2"])

        def gp_load(dst_ap, src_ap, slow=False):
            p.wait("gpsimd", "GDm", p.cnt["GDm"])
            return Gp(lambda dst_ap=dst_ap, src_ap=src_ap, slow=slow:
                      ng.dma_start(out=dst_ap, in_=src_ap,
                                   allow_slow_non_contiguous=slow), inc="GDm", n=16)

        def tap_out(name, src):
            m = gp_load(d_tap[name], src)
            p.wait("vector", "GDm", m)
            p.wait("scalar", "GDm", m)
            p.wait("tensor", "GDm", m)

        def mm_fence(bank_ap):
            Tr(lambda bank_ap=bank_ap: ntn.matmul(
                bank_ap, zerc[:, 0:1], onec[:, 0:1],
                start=False, stop=True, skip_group_check=True), inc=None)
            return Tr(lambda bank_ap=bank_ap: ntn.matmul(
                bank_ap, zerc[:, 0:1], onec[:, 0:1],
                start=False, stop=True, skip_group_check=True), inc="T")

        def range_reduce(dst, src, iscr):
            ii = iscr.bitcast(I32)
            Vr(lambda: nv.tensor_single_scalar(dst, src, INV_2PI, ALU.mult), inc=None)
            Vr(lambda: nv.tensor_copy(ii, dst), inc=None)
            Vr(lambda: nv.tensor_copy(iscr, ii), inc=None)
            Vr(lambda: nv.scalar_tensor_tensor(dst, iscr, -TWO_PI, src, ALU.mult, ALU.add), inc=None)
            Vr(lambda: nv.tensor_single_scalar(iscr, dst, PI, ALU.is_gt), inc=None)
            Vr(lambda: nv.scalar_tensor_tensor(dst, iscr, -TWO_PI, dst, ALU.mult, ALU.add), inc=None)
            Vr(lambda: nv.tensor_single_scalar(iscr, dst, -PI, ALU.is_lt), inc=None)
            Vr(lambda: nv.scalar_tensor_tensor(dst, iscr, TWO_PI, dst, ALU.mult, ALU.add), inc=None)
            Vr(lambda: nv.tensor_single_scalar(dst, dst, PI_LO, ALU.min), inc=None)
            Vr(lambda: nv.tensor_single_scalar(dst, dst, -PI_LO, ALU.max), inc=None)

        def cos_from_red(dst, red, mscr):
            Vr(lambda: nv.tensor_scalar(mscr, red, -1.0, PI / 2, ALU.mult, ALU.add), inc=None)
            Vr(lambda: nv.tensor_single_scalar(dst, mscr, PI, ALU.is_gt), inc=None)
            Vr(lambda: nv.scalar_tensor_tensor(mscr, dst, -TWO_PI, mscr, ALU.mult, ALU.add), inc=None)
            Vr(lambda: nv.tensor_single_scalar(mscr, mscr, PI_LO, ALU.min), inc=None)
            Vr(lambda: nv.tensor_single_scalar(mscr, mscr, -PI_LO, ALU.max))
            p.wait("scalar", "V")
            Sc(lambda: ns.activation(dst, mscr, AF.Sin))

        DBL = [(256, "c256", "s256", "kr256", "ki256"),
               (512, "c512", "s512", "kr512", "ki512"),
               (1024, "c1024", "s1024", "kr1024", "ki1024")]

        def build_ptab(lay, g, with_rho):
            ath = pp["absth"][:, lay, g:g + 1].broadcast_to([128, 256])
            Vr(lambda ath=ath: nv.tensor_tensor(
                u[:, 0, 0:256], iot[:, 0:256], ath, ALU.mult))
            range_reduce(u[:, 1, 0:256], u[:, 0, 0:256], tabs[:, 0, 0:256])
            cos_from_red(tabs[:, 0, 0:256], u[:, 1, 0:256], tabs[:, 1, 0:256])
            Sc(lambda: ns.activation(tabs[:, 1, 0:256], u[:, 1, 0:256], AF.Sin))
            if with_rho:
                lnr = pp["lnrho"][:, lay, g:g + 1].broadcast_to([128, 256])
                Vr(lambda lnr=lnr: nv.tensor_tensor(
                    u[:, 0, 0:256], iot[:, 0:256], lnr, ALU.mult))
                p.wait("scalar", "V")
                Sc(lambda: ns.activation(u[:, 0, 0:256], u[:, 0, 0:256], AF.Exp))
                p.wait("vector", "S")
                Vr(lambda: nv.tensor_tensor(
                    tabs[:, 0, 0:256], tabs[:, 0, 0:256], u[:, 0, 0:256], ALU.mult), inc=None)
                Vr(lambda: nv.tensor_tensor(
                    tabs[:, 1, 0:256], tabs[:, 1, 0:256], u[:, 0, 0:256], ALU.mult), inc=None)
            else:
                p.wait("vector", "S")
            for L, ck, sk, krk, kik in DBL:
                cc = pp[krk if with_rho else ck][:, lay, g:g + 1]
                ss = pp[kik if with_rho else sk][:, lay, g:g + 1]
                ssb = ss.broadcast_to([128, L])
                Vr(lambda L=L, ssb=ssb: nv.tensor_tensor(
                    u[:, 0, 0:L], tabs[:, 1, 0:L], ssb, ALU.mult), inc=None)
                Vr(lambda L=L, ssb=ssb: nv.tensor_tensor(
                    u[:, 1, 0:L], tabs[:, 0, 0:L], ssb, ALU.mult), inc=None)
                Vr(lambda L=L, cc=cc: nv.scalar_tensor_tensor(
                    tabs[:, 0, L:2 * L], tabs[:, 0, 0:L], cc, u[:, 0, 0:L],
                    ALU.mult, ALU.subtract), inc=None)
                Vr(lambda L=L, cc=cc: nv.scalar_tensor_tensor(
                    tabs[:, 1, L:2 * L], tabs[:, 1, 0:L], cc, u[:, 1, 0:L],
                    ALU.mult, ALU.add), inc=None)

        def ln_stats_and_bcast(tok, denom, extra_wait_v=None):
            mu, rr = rowb[:, 0, :], rowb[:, 1, :]
            p.wait("vector", "T")
            Vr(lambda: nv.tensor_single_scalar(mu, ps_st[:, 0, :], 1.0 / denom, ALU.mult), inc=None)
            Vr(lambda: nv.tensor_tensor(rr, mu, mu, ALU.mult), inc=None)
            Vr(lambda: nv.scalar_tensor_tensor(rr, ps_st[:, 1, :], 1.0 / denom, rr,
                                               ALU.mult, ALU.subtract))
            st_free = p.cnt["V"]
            Vr(lambda: nv.tensor_single_scalar(rr, rr, LN_EPS, ALU.add))
            p.wait("scalar", "V")
            Sc(lambda: ns.activation(rr, rr, AF.Sqrt))
            p.wait("vector", "S")
            Vr(lambda: nv.reciprocal(rr, rr), inc=None)
            Vr(lambda: nv.tensor_tensor(mu, rr, mu, ALU.mult))
            rows_done = p.cnt["V"]
            p.wait("tensor", "V", rows_done)
            if extra_wait_v is not None:
                p.wait("tensor", "V", extra_wait_v)
            p.wait("tensor", "S")
            Tr(lambda: ntn.matmul(ps_b[:, 0, :], oner[:], rr, start=True, stop=True), inc=None)
            Tr(lambda: ntn.matmul(ps_b[:, 1, :], oner[:], mu, start=True, stop=True), inc=None)
            mm_fence(ps_b[0:1, 0, 0:1])
            p.wait("vector", "T")
            sl = slice(tok * TT, (tok + 1) * TT)
            Vr(lambda sl=sl: nv.tensor_copy(bcst[:, 0, sl], ps_b[:, 0, :]), inc=None)
            Vr(lambda sl=sl: nv.tensor_copy(bcst[:, 1, sl], ps_b[:, 1, :]))
            return p.cnt["V"], st_free

        Vr(lambda: nv.memset(onec[:], 1.0), inc=None)
        Vr(lambda: nv.memset(zerc[:], 0.0), inc=None)
        Vr(lambda: nv.memset(fcell[:], 1.0), inc=None)
        Vr(lambda: nv.memset(nonec[:], -1.0), inc=None)
        Vr(lambda: nv.memset(oner[:], 1.0))
        const_v = p.cnt["V"]
        p.wait("tensor", "V", const_v)
        p.wait("scalar", "V", const_v)
        p.wait("gpsimd", "V", const_v)
        for dst, src in [(iot, dd["iota"]), (binS, dd["bin"]), (b1S, dd["b1"]),
                         (b2S, dd["b2"]), (lnlwS, dd["lnlw"]), (lnlbS, dd["lnlb"]),
                         (boutS, dd["bout"]), (bAS, dd["bA"]), (selS, dd["sel"])] + \
                        [(prm[k], dd[k]) for k in prm]:
            gp_load(dst[:], src)
        ld_params = p.cnt["GDm"]

        phr, phi = prm["phzr"][:, :, :], prm["phzi"][:, :, :]
        rho16, lnrho16 = pp["rho"][:, :, :], pp["lnrho"][:, :, :]
        absth16, sgn16 = pp["absth"][:, :, :], pp["sgn"][:, :, :]
        eR16, eI16 = pp["eR"][:, :, :], pp["eI"][:, :, :]
        p2r16, p2i16 = pp["p2r"][:, :, :], pp["p2i"][:, :, :]
        ln2047_16, th2047_16 = pp["ln2047"][:, :, :], pp["th2047"][:, :, :]
        A16, B16, C16, D16 = t16a[:], t16b[:], t16c[:], t16d[:]

        p.wait("vector", "GDm", ld_params)
        Vr(lambda: nv.tensor_tensor(A16, phr, phr, ALU.mult), inc=None)
        Vr(lambda: nv.tensor_tensor(B16, phi, phi, ALU.mult), inc=None)
        Vr(lambda: nv.tensor_tensor(A16, A16, B16, ALU.add), inc=None)
        Vr(lambda: nv.tensor_single_scalar(A16, A16, 1e-30, ALU.max))
        p.wait("scalar", "V")
        Sc(lambda: ns.activation(B16, A16, AF.Sqrt))
        p.wait("vector", "S")
        Vr(lambda: nv.reciprocal(C16, B16), inc=None)
        Vr(lambda: nv.tensor_tensor(eR16, phr, C16, ALU.mult), inc=None)
        Vr(lambda: nv.tensor_tensor(eI16, phi, C16, ALU.mult))
        p.wait("scalar", "V")
        Sc(lambda: ns.activation(rho16, B16, AF.Tanh), inc=None)
        Sc(lambda: ns.activation(lnrho16, rho16, AF.Ln), inc=None)
        Sc(lambda: ns.activation(A16, eR16, AF.Abs), inc=None)
        Sc(lambda: ns.activation(B16, eI16, AF.Abs))
        p.wait("vector", "S")
        Vr(lambda: nv.tensor_single_scalar(ln2047_16, lnrho16, 2047.0, ALU.mult), inc=None)
        Vr(lambda: nv.tensor_tensor(C16, A16, B16, ALU.min), inc=None)
        Vr(lambda: nv.tensor_tensor(D16, A16, B16, ALU.max), inc=None)
        Vr(lambda: nv.reciprocal(D16, D16), inc=None)
        Vr(lambda: nv.tensor_tensor(C16, C16, D16, ALU.mult))
        p.wait("scalar", "V")
        Sc(lambda: ns.activation(D16, C16, AF.Arctan))
        p.wait("scalar", "S")
        Sc(lambda: ns.activation(A16, D16, AF.Sin))
        p.wait("vector", "S")
        Vr(lambda: nv.tensor_tensor(B16, A16, A16, ALU.mult), inc=None)
        Vr(lambda: nv.tensor_scalar(B16, B16, -1.0, 1.0, ALU.mult, ALU.add))
        p.wait("scalar", "V")
        Sc(lambda: ns.activation(B16, B16, AF.Sqrt))
        p.wait("vector", "S")
        Vr(lambda: nv.tensor_tensor(C16, C16, B16, ALU.mult), inc=None)
        Vr(lambda: nv.tensor_tensor(C16, C16, A16, ALU.subtract), inc=None)
        Vr(lambda: nv.tensor_tensor(C16, C16, B16, ALU.mult), inc=None)
        Vr(lambda: nv.tensor_tensor(D16, D16, C16, ALU.add), inc=None)
        Vr(lambda: nv.tensor_tensor(A16, eI16, eI16, ALU.mult), inc=None)
        Vr(lambda: nv.tensor_tensor(B16, eR16, eR16, ALU.mult), inc=None)
        Vr(lambda: nv.tensor_tensor(A16, A16, B16, ALU.is_gt), inc=None)
        Vr(lambda: nv.tensor_scalar(B16, D16, -2.0, PI / 2, ALU.mult, ALU.add), inc=None)
        Vr(lambda: nv.tensor_tensor(A16, A16, B16, ALU.mult), inc=None)
        Vr(lambda: nv.tensor_tensor(D16, D16, A16, ALU.add), inc=None)
        Vr(lambda: nv.tensor_single_scalar(A16, eR16, 0.0, ALU.is_lt), inc=None)
        Vr(lambda: nv.tensor_scalar(B16, D16, -2.0, PI, ALU.mult, ALU.add), inc=None)
        Vr(lambda: nv.tensor_tensor(A16, A16, B16, ALU.mult), inc=None)
        Vr(lambda: nv.tensor_tensor(absth16, D16, A16, ALU.add), inc=None)
        Vr(lambda: nv.tensor_single_scalar(A16, eI16, 0.0, ALU.is_ge), inc=None)
        Vr(lambda: nv.tensor_scalar(sgn16, A16, 2.0, -1.0, ALU.mult, ALU.add), inc=None)
        Vr(lambda: nv.tensor_single_scalar(th2047_16, absth16, 2047.0, ALU.mult), inc=None)
        Vr(lambda: nv.tensor_single_scalar(A16, lnrho16, 2048.0, ALU.mult))
        p.wait("scalar", "V")
        Sc(lambda: ns.activation(A16, A16, AF.Exp))
        p.wait("vector", "S")
        Vr(lambda: nv.tensor_single_scalar(B16, absth16, 2048.0, ALU.mult), inc=None)
        range_reduce(C16, B16, D16)
        Vr(lambda: nv.tensor_copy(D16, C16))
        p.wait("scalar", "V")
        Sc(lambda: ns.activation(B16, C16, AF.Sin))
        p.wait("vector", "S")
        Vr(lambda: nv.tensor_tensor(p2i16, A16, B16, ALU.mult), inc=None)
        Vr(lambda: nv.tensor_tensor(p2i16, p2i16, sgn16, ALU.mult), inc=None)
        cos_from_red(B16, D16, C16)
        p.wait("vector", "S")
        Vr(lambda: nv.tensor_tensor(p2r16, A16, B16, ALU.mult), inc=None)
        Vr(lambda: nv.tensor_single_scalar(pp["nsgn"][:, :, :], sgn16, -1.0, ALU.mult), inc=None)
        range_reduce(C16, th2047_16, D16)
        Vr(lambda: nv.tensor_copy(D16, C16))
        cos_from_red(pp["c47"][:, :, :], C16, B16)
        Sc(lambda: ns.activation(pp["s47"][:, :, :], D16, AF.Sin))
        p.wait("vector", "S")
        for Lf, ck, sk, krk, kik in [(256.0, "c256", "s256", "kr256", "ki256"),
                                     (512.0, "c512", "s512", "kr512", "ki512"),
                                     (1024.0, "c1024", "s1024", "kr1024", "ki1024")]:
            Vr(lambda Lf=Lf: nv.tensor_single_scalar(A16, absth16, Lf, ALU.mult), inc=None)
            range_reduce(C16, A16, D16)
            Vr(lambda: nv.tensor_copy(D16, C16), inc=None)
            Vr(lambda Lf=Lf: nv.tensor_single_scalar(A16, lnrho16, Lf, ALU.mult))
            cos_from_red(pp[ck][:, :, :], C16, B16)
            Sc(lambda sk=sk: ns.activation(pp[sk][:, :, :], D16, AF.Sin), inc=None)
            Sc(lambda: ns.activation(A16, A16, AF.Exp))
            p.wait("vector", "S")
            Vr(lambda krk=krk, ck=ck: nv.tensor_tensor(
                pp[krk][:, :, :], pp[ck][:, :, :], A16, ALU.mult), inc=None)
            Vr(lambda kik=kik, sk=sk: nv.tensor_tensor(
                pp[kik][:, :, :], pp[sk][:, :, :], A16, ALU.mult))
        mark("P1")

        tmark_last = 0
        epi_done_v = 0
        stfree_v = 0
        tmarks = [0, 0]
        smarks = [0, 0]
        for tok in range(NTOK):
            for k in range(KV):
                sl = k % 3
                p.wait("sync", "T", tmarks[0])
                p.wait("sync", "S", smarks[0])
                mk = sring.load_group([
                    (xb[:, sl, :], dd["xT"][k * 128:(k + 1) * 128, tok * TT:(tok + 1) * TT]),
                    (wtb[:, sl, :], dd["winT"][k * 128:(k + 1) * 128, :])])
                sring.wait("scalar", mk)
                p.wait("scalar", "T", tmarks[0])
                s_sq = Sc(lambda sl=sl: ns.activation(xeb[:, sl, :], xb[:, sl, :], AF.Square))
                smarks = [smarks[1], s_sq]
                sring.wait("tensor", mk)
                if k == 0:
                    p.wait("tensor", "V", epi_done_v)
                    p.wait("tensor", "V", stfree_v)
                for dm in range(GD):
                    Tr(lambda k=k, sl=sl, dm=dm: ntn.matmul(
                        ps_a[:, dm, :], wtb[:, sl, dm * 128:(dm + 1) * 128], xb[:, sl, :],
                        start=(k == 0), stop=(k == KV - 1)), inc=None)
                Tr(lambda k=k, sl=sl: ntn.matmul(
                    ps_st[:, 0, :], onec[:], xb[:, sl, :],
                    start=(k == 0), stop=(k == KV - 1)), inc=None)
                p.wait("tensor", "S", s_sq)
                if tok == 0:
                    Tr(lambda k=k, sl=sl: ntn.matmul(
                        ps_b[0:1, 0, 0:D], nonec[:], wtb[:, sl, :],
                        start=(k == 0), stop=(k == KV - 1)), inc=None)
                tm = Tr(lambda k=k, sl=sl: ntn.matmul(
                    ps_st[:, 1, :], onec[:], xeb[:, sl, :],
                    start=(k == 0), stop=(k == KV - 1)), inc="T")
                tmarks = [tmarks[1], tm]
            mm_fence(ps_st[0:1, 0, 0:1])
            if tok == 0:
                p.wait("vector", "T")
                rs_v = Vr(lambda: nv.tensor_copy(rowb[:, 1, 0:TT], ps_b[0:1, 0, 0:D]))
                p.wait("gpsimd", "V", rs_v)
                g1 = gp_load(bounce.ap()[0:1, 0:D], rowb[0:1, 1, 0:D])
                p.wait("gpsimd", "GDm", g1)
                g2 = gp_load(rsS[:], bounce.ap()[0, 0:D].rearrange("(g q) -> q g", q=128), slow=True)
                p.wait("vector", "GDm", g1)
            bc_v, stfree_v = ln_stats_and_bcast(tok, float(V), extra_wait_v=(rs_v if tok == 0 else None))
            if tok == 0:
                p.wait("vector", "GDm", g2)
            sl_t = slice(tok * TT, (tok + 1) * TT)
            for dm in range(GD):
                hsl = h[:, dm, sl_t]
                Vr(lambda dm=dm, hsl=hsl, sl_t=sl_t: nv.tensor_tensor(
                    hsl, ps_a[:, dm, :], bcst[:, 0, sl_t], ALU.mult), inc=None)
                Vr(lambda dm=dm, hsl=hsl, sl_t=sl_t: nv.scalar_tensor_tensor(
                    hsl, bcst[:, 1, sl_t], rsS[:, dm:dm + 1], hsl, ALU.mult, ALU.add), inc=None)
                Vr(lambda dm=dm, hsl=hsl: nv.tensor_scalar_add(hsl, hsl, binS[:, dm:dm + 1]))
            epi_done_v = p.cnt["V"]

        mark("P2_win")
        if "h0" in taps:
            p.wait("gpsimd", "V")
            tap_out("h0", h[:, :, :])

        hf = h[:, :, :]
        yf = y[:, :, :]

        def inner_ln(lay, wkey, bkey, act, y_free_t=0):
            nonlocal stfree_v, epi_done_v
            p.wait("scalar", "V")
            p.wait("scalar", "T", y_free_t)
            sq_s = Sc(lambda: ns.activation(yf, hf, AF.Square))
            for tok in range(NTOK):
                p.wait("tensor", "S", sq_s)
                p.wait("tensor", "V", stfree_v)
                for kt in range(GD):
                    Tr(lambda kt=kt, tok=tok: ntn.matmul(
                        ps_st[:, 0, :], onec[:], h[:, kt, tok * TT:(tok + 1) * TT],
                        start=(kt == 0), stop=(kt == GD - 1)), inc=None)
                for kt in range(GD):
                    Tr(lambda kt=kt, tok=tok: ntn.matmul(
                        ps_st[:, 1, :], onec[:], y[:, kt, tok * TT:(tok + 1) * TT],
                        start=(kt == 0), stop=(kt == GD - 1)), inc=None)
                mm_fence(ps_st[0:1, 0, 0:1])
                bc_v, stfree_v = ln_stats_and_bcast(tok, float(D))
            rb = bcst[:, 0:1, :].broadcast_to([128, GD, TC])
            mb = bcst[:, 1:2, :].broadcast_to([128, GD, TC])
            Vr(lambda: nv.tensor_tensor(yf, hf, rb, ALU.mult), inc=None)
            Vr(lambda: nv.tensor_tensor(yf, yf, mb, ALU.subtract))
            if act == AF.Copy:
                for g in range(GD):
                    wcol = prm[wkey][:, lay, g:g + 1]
                    bb = prm[bkey][:, lay, g:g + 1].broadcast_to([128, TC])
                    Vr(lambda g=g, wcol=wcol, bb=bb: nv.scalar_tensor_tensor(
                        y[:, g, :], y[:, g, :], wcol, bb, ALU.mult, ALU.add))
                return ("V", p.cnt["V"])
            p.wait("scalar", "V")
            for g in range(GD):
                Sc(lambda g=g: ns.activation(
                    y[:, g, :], y[:, g, :], act,
                    bias=prm[bkey][:, lay, g:g + 1], scale=prm[wkey][:, lay, g:g + 1]))
            return ("S", p.cnt["S"])

        prev_mm1_t = 0
        w2marks = [0, 0]
        w2i = 0
        for lay in range(DEPTH if stage >= 2 else 0):
            p.wait("sync", "T", prev_mm1_t)
            w1_ld = sring.load(w1t[:, :, :], dd["w1T"][lay].rearrange("(kt q) f -> q kt f", q=128))

            s_ln1 = inner_ln(lay, "ln1w", "ln1b", AF.Silu, y_free_t=prev_mm1_t)
            mark(f"L{lay}_ln1")
            if lay == 0 and "y0" in taps:
                p.wait("gpsimd", s_ln1[0], s_ln1[1])
                tap_out("y0", yf)

            p.wait("vector", s_ln1[0], s_ln1[1])
            for g in range(GD):
                build_ptab(lay, g, with_rho=False)
                rho_b = pp["rho"][:, lay, g:g + 1].broadcast_to([128, TC])
                Vr(lambda g=g: nv.tensor_tensor(u[:, 0, :], tabs[:, 0, :], y[:, g, :], ALU.mult), inc=None)
                Vr(lambda g=g: nv.tensor_tensor(u[:, 1, :], tabs[:, 1, :], y[:, g, :], ALU.mult), inc=None)
                Vr(lambda rho_b=rho_b: nv.tensor_tensor_scan(
                    u[:, 0, :], rho_b, u[:, 0, :], 0.0, ALU.mult, ALU.add), inc=None)
                Vr(lambda rho_b=rho_b: nv.tensor_tensor_scan(
                    u[:, 1, :], rho_b, u[:, 1, :], 0.0, ALU.mult, ALU.subtract), inc=None)
                u0e, u1e = u[:, 0, TC - 1:TC], u[:, 1, TC - 1:TC]
                c47c = pp["c47"][:, lay, g:g + 1]
                s47c = pp["s47"][:, lay, g:g + 1]
                sgnc = pp["sgn"][:, lay, g:g + 1]
                Vr(lambda u1e=u1e, s47c=s47c: nv.tensor_tensor(
                    c8a[:, 0:1], u1e, s47c, ALU.mult), inc=None)
                Vr(lambda g=g, u0e=u0e, c47c=c47c: nv.scalar_tensor_tensor(
                    ES[:, g:g + 1], u0e, c47c, c8a[:, 0:1], ALU.mult, ALU.subtract), inc=None)
                Vr(lambda u1e=u1e, c47c=c47c: nv.tensor_tensor(
                    c8a[:, 1:2], u1e, c47c, ALU.mult), inc=None)
                Vr(lambda u0e=u0e, s47c=s47c: nv.scalar_tensor_tensor(
                    c8b[:, 0:1], u0e, s47c, c8a[:, 1:2], ALU.mult, ALU.add), inc=None)
                Vr(lambda g=g, sgnc=sgnc: nv.tensor_tensor(
                    ES[:, 4 + g:5 + g], c8b[:, 0:1], sgnc, ALU.mult), inc=None)
                Vr(lambda g=g: nv.tensor_tensor(y[:, g, :], tabs[:, 0, :], u[:, 0, :], ALU.mult), inc=None)
                Vr(lambda g=g: nv.tensor_tensor(u[:, 0, :], tabs[:, 1, :], u[:, 1, :], ALU.mult), inc=None)
                Vr(lambda g=g: nv.tensor_tensor(y[:, g, :], y[:, g, :], u[:, 0, :], ALU.subtract))
            e_v = p.cnt["V"]

            p.wait("gpsimd", "V", e_v)
            gp1 = gp_load(ag_in.ap().rearrange("o (q k) -> (o q) k", q=128), ES[:])
            p.wait("gpsimd", "GDm", gp1)
            cc_m = Gp(lambda: ng.collective_compute(
                "AllGather", ALU.bypass, replica_groups=[list(range(NCORES))],
                ins=[ag_in.ap().opt()], outs=[ag_out.ap().opt()]))
            p.wait("gpsimd", "G", cc_m)
            gp2 = gp_load(EA[:, :, :], ag_out.ap()[0:7, :].rearrange("j (q k) -> q j k", q=128))

            build_ptab(lay, 0, with_rho=True)

            p.wait("vector", "GDm", gp2)
            Vr(lambda lay=lay: nv.tensor_copy(CCs[:, 0:4], prm["lcr"][:, lay, :]), inc=None)
            Vr(lambda lay=lay: nv.tensor_copy(CCs[:, 4:8], prm["lci"][:, lay, :]), inc=None)
            for j in range(7):
                Cr, Ci = CCs[:, 0:4], CCs[:, 4:8]
                p2r_l, p2i_l = pp["p2r"][:, lay, :], pp["p2i"][:, lay, :]
                Vr(lambda Cr=Cr, p2r_l=p2r_l: nv.tensor_tensor(c8a[:, 0:4], Cr, p2r_l, ALU.mult), inc=None)
                Vr(lambda Ci=Ci, p2i_l=p2i_l: nv.tensor_tensor(c8b[:, 0:4], Ci, p2i_l, ALU.mult), inc=None)
                Vr(lambda: nv.tensor_tensor(c8a[:, 0:4], c8a[:, 0:4], c8b[:, 0:4], ALU.subtract), inc=None)
                Vr(lambda j=j: nv.tensor_tensor(c8a[:, 0:4], c8a[:, 0:4], EA[:, j, 0:4], ALU.add), inc=None)
                Vr(lambda Cr=Cr, p2i_l=p2i_l: nv.tensor_tensor(c8b[:, 0:4], Cr, p2i_l, ALU.mult), inc=None)
                Vr(lambda Ci=Ci, p2r_l=p2r_l: nv.tensor_tensor(c8b[:, 4:8], Ci, p2r_l, ALU.mult), inc=None)
                Vr(lambda: nv.tensor_tensor(c8a[:, 4:8], c8b[:, 0:4], c8b[:, 4:8], ALU.add), inc=None)
                Vr(lambda j=j: nv.tensor_tensor(c8a[:, 4:8], c8a[:, 4:8], EA[:, j, 4:8], ALU.add), inc=None)
                Vr(lambda: nv.tensor_tensor(c8b[:, 0:8], c8a[:, 0:8], CCs[:, 0:8], ALU.subtract), inc=None)
                Vr(lambda j=j: nv.scalar_tensor_tensor(
                    CCs[:, 0:8], c8b[:, 0:8], selS[:, j:j + 1], CCs[:, 0:8], ALU.mult, ALU.add), inc=None)
            eRl, eIl = pp["eR"][:, lay, :], pp["eI"][:, lay, :]
            rhol = pp["rho"][:, lay, :]
            nsgnl = pp["nsgn"][:, lay, :]
            Vr(lambda eRl=eRl: nv.tensor_tensor(WIs[:, 0:4], CCs[:, 0:4], eRl, ALU.mult), inc=None)
            Vr(lambda eIl=eIl: nv.tensor_tensor(c8a[:, 0:4], CCs[:, 4:8], eIl, ALU.mult), inc=None)
            Vr(lambda: nv.tensor_tensor(WIs[:, 0:4], WIs[:, 0:4], c8a[:, 0:4], ALU.subtract), inc=None)
            Vr(lambda rhol=rhol: nv.tensor_tensor(WIs[:, 0:4], WIs[:, 0:4], rhol, ALU.mult), inc=None)
            Vr(lambda eIl=eIl: nv.tensor_tensor(WIs[:, 4:8], CCs[:, 0:4], eIl, ALU.mult), inc=None)
            Vr(lambda eRl=eRl: nv.tensor_tensor(c8a[:, 4:8], CCs[:, 4:8], eRl, ALU.mult), inc=None)
            Vr(lambda: nv.tensor_tensor(WIs[:, 4:8], WIs[:, 4:8], c8a[:, 4:8], ALU.add), inc=None)
            Vr(lambda rhol=rhol: nv.tensor_tensor(WIs[:, 4:8], WIs[:, 4:8], rhol, ALU.mult), inc=None)
            Vr(lambda nsgnl=nsgnl: nv.tensor_tensor(WIs[:, 4:8], WIs[:, 4:8], nsgnl, ALU.mult))

            for g in range(GD):
                if g > 0:
                    build_ptab(lay, g, with_rho=True)
                Vr(lambda g=g: nv.scalar_tensor_tensor(
                    y[:, g, :], tabs[:, 0, :], WIs[:, g:g + 1], y[:, g, :],
                    ALU.mult, ALU.add), inc=None)
                Vr(lambda g=g: nv.scalar_tensor_tensor(
                    y[:, g, :], tabs[:, 1, :], WIs[:, 4 + g:5 + g], y[:, g, :],
                    ALU.mult, ALU.add))
            if lay == 0 and "conv0" in taps:
                p.wait("gpsimd", "V")
                tap_out("conv0", yf)

            for g in range(GD):
                Vr(lambda g=g, lay=lay: nv.scalar_tensor_tensor(
                    h[:, g, :], y[:, g, :], prm["scl"][:, lay, g:g + 1], h[:, g, :],
                    ALU.mult, ALU.add), inc=None)
            Vr(lambda: nv.tensor_copy(c8a[:, 0:1], CCs[:, 0:1]))

            mark(f"L{lay}_scan")
            s_ln2 = inner_ln(lay, "ln2w", "ln2b", AF.Copy)
            mark(f"L{lay}_ln2")

            evac_s = [0, 0, 0]
            mm2_t = 0
            hupd_v = p.cnt["V"]
            sring.wait("tensor", w1_ld)
            p.wait("tensor", s_ln2[0], s_ln2[1])
            for tok in range(NTOK):
                p.wait("tensor", "V", hupd_v)

                def emit_mm2(fk, sl, pb, wl):
                    sring.wait("tensor", wl)
                    p.wait("tensor", "S", evac_s[pb])
                    for dm in range(GD):
                        Tr(lambda fk=fk, sl=sl, dm=dm, pb=pb: ntn.matmul(
                            ps_a[:, dm, :], w2b[:, sl, dm * 128:(dm + 1) * 128],
                            y1[:, pb, :],
                            start=(fk == 0), stop=(fk == KF - 1)),
                           inc=("T" if dm == GD - 1 else None))
                    w2marks[:] = [w2marks[1], p.cnt["T"]]

                pend = None
                for fk in range(KF):
                    sl = w2i % 3
                    w2i += 1
                    p.wait("sync", "T", w2marks[0])
                    wl = sring.load(w2b[:, sl, :],
                                    dd["w2T"][lay][fk * 128:(fk + 1) * 128, :])
                    pb = fk % 2
                    p.wait("tensor", "S", evac_s[pb])
                    for kt in range(GD):
                        Tr(lambda lay=lay, fk=fk, kt=kt, pb=pb, tok=tok: ntn.matmul(
                            ps_b[:, pb, :], w1t[:, kt, fk * 128:(fk + 1) * 128],
                            y[:, kt, tok * TT:(tok + 1) * TT],
                            start=(kt == 0), stop=(kt == GD - 1)), inc=None)
                    mm1_t = mm_fence(ps_b[0:1, pb, 0:1])
                    p.wait("scalar", "T", mm1_t)
                    evac_s[pb] = Sc(lambda lay=lay, fk=fk, pb=pb: ns.activation(
                        y1[:, pb, :], ps_b[:, pb, :], AF.Silu,
                        bias=b1S[:, lay, fk:fk + 1], scale=1.0))
                    if pend is not None:
                        emit_mm2(*pend)
                    pend = (fk, sl, pb, wl)
                emit_mm2(*pend)
                mm2_t = mm_fence(ps_a[0:1, 0, 0:1])
                p.wait("vector", "T", mm2_t)
                for dm in range(GD):
                    hsl = h[:, dm, tok * TT:(tok + 1) * TT]
                    Vr(lambda dm=dm, lay=lay, hsl=hsl: nv.scalar_tensor_tensor(
                        hsl, ps_a[:, dm, :], b2S[:, lay, dm:dm + 1], hsl,
                        ALU.add, ALU.add), inc=None)
                Vr(lambda: nv.tensor_copy(c8a[:, 0:1], CCs[:, 0:1]))
                hupd_v = p.cnt["V"]
            prev_mm1_t = p.cnt["T"]
            mark(f"L{lay}_ffn")

            if lay == 0 and "h1" in taps:
                p.wait("gpsimd", "V", hupd_v)
                tap_out("h1", hf)

        if stage >= 2 and "hfin" in taps:
            p.wait("gpsimd", "V")
            tap_out("hfin", hf)

        if stage >= 3:
            def final_ln():
                nonlocal stfree_v
                p.wait("scalar", "V")
                p.wait("scalar", "T", prev_mm1_t if stage >= 2 else 0)
                sq_s = Sc(lambda: ns.activation(yf, hf, AF.Square))
                for tok in range(NTOK):
                    p.wait("tensor", "S", sq_s)
                    p.wait("tensor", "V", stfree_v)
                    for kt in range(GD):
                        Tr(lambda kt=kt, tok=tok: ntn.matmul(
                            ps_st[:, 0, :], onec[:], h[:, kt, tok * TT:(tok + 1) * TT],
                            start=(kt == 0), stop=(kt == GD - 1)), inc=None)
                    for kt in range(GD):
                        Tr(lambda kt=kt, tok=tok: ntn.matmul(
                            ps_st[:, 1, :], onec[:], y[:, kt, tok * TT:(tok + 1) * TT],
                            start=(kt == 0), stop=(kt == GD - 1)),
                           inc=("T" if kt == GD - 1 else None))
                    bc_v, stfree_v = ln_stats_and_bcast(tok, float(D))
                rb = bcst[:, 0:1, :].broadcast_to([128, GD, TC])
                mbb = bcst[:, 1:2, :].broadcast_to([128, GD, TC])
                Vr(lambda: nv.tensor_tensor(yf, hf, rb, ALU.mult), inc=None)
                Vr(lambda: nv.tensor_tensor(yf, yf, mbb, ALU.subtract))
                for g in range(GD):
                    bb = lnlbS[:, g:g + 1].broadcast_to([128, TC])
                    Vr(lambda g=g, bb=bb: nv.scalar_tensor_tensor(
                        y[:, g, :], y[:, g, :], lnlwS[:, g:g + 1], bb,
                        ALU.mult, ALU.add))

            final_ln()
            mark("P4_ln")
            if "hln" in taps:
                p.wait("gpsimd", "V")
                tap_out("hln", yf)

            def x1sb(k):
                buf = h if k < 16 else w1t
                kk = k % 16
                return buf[:, kk // 4, (kk % 4) * TT:(kk % 4 + 1) * TT]

            hln_v = p.cnt["V"]
            p.wait("sync", "T", p.cnt["T"])
            wout_mark = [hln_v] * 4
            wt_marks = [0, 0]
            xe_mark = [0, 0, 0]
            sq2_marks = [0, 0, 0]
            stfree2 = 0
            ab_marks = [0, 0]
            rs_evac = 0
            a_epi_v = 0
            a_epi_mt = [0, 0, 0, 0]
            rsA_gd = 0
            a_out_mark = [None, None]
            ai = 0
            for tok in range(NTOK):
                sl_t = slice(tok * TT, (tok + 1) * TT)
                p.wait("tensor", "V", stfree_v)
                for vm in range(KV):
                    sl = vm % 3
                    pa = vm % 4
                    p.wait("sync", "T", wt_marks[0])
                    wl = sring.load(wtb[:, sl, :],
                                    dd["woutT"][:, vm * 128:(vm + 1) * 128].rearrange(
                                        "(kt q) m -> q kt m", q=128))
                    p.wait("gpsimd", "V", xe_mark[sl])
                    xl = ering.load(xeb[:, sl, :],
                                    dd["xT"][vm * 128:(vm + 1) * 128, sl_t])
                    sring.wait("tensor", wl)
                    p.wait("tensor", "V", wout_mark[pa])
                    for kt in range(GD):
                        Tr(lambda vm=vm, sl=sl, kt=kt, pa=pa, sl_t=sl_t: ntn.matmul(
                            ps_a[:, pa, :], wtb[:, sl, kt * 128:(kt + 1) * 128],
                            y[:, kt, sl_t],
                            start=(kt == 0), stop=(kt == GD - 1)), inc=None)
                    wt_marks = [wt_marks[1], mm_fence(ps_a[0:1, pa, 0:1])]
                    x1t = x1sb(vm)
                    p.wait("vector", "T")
                    ering.wait("vector", xl)
                    Vr(lambda vm=vm, sl=sl, pa=pa, x1t=x1t: nv.scalar_tensor_tensor(
                        x1t, ps_a[:, pa, :], boutS[:, vm:vm + 1], xeb[:, sl, :],
                        ALU.add, ALU.add))
                    epi_v = p.cnt["V"]
                    wout_mark[pa] = epi_v
                    xe_mark[sl] = epi_v
                    su = vm % 3
                    p.wait("scalar", "V", epi_v)
                    p.wait("scalar", "T", sq2_marks[su])
                    sqs = Sc(lambda su=su, x1t=x1t: ns.activation(
                        u[:, 0, su * TT:(su + 1) * TT], x1t, AF.Square))
                    p.wait("tensor", "S", sqs)
                    if vm == 0:
                        p.wait("tensor", "V", stfree2)
                    Tr(lambda vm=vm, x1t=x1t: ntn.matmul(
                        ps_st[:, 0, :], onec[:], x1t,
                        start=(vm == 0), stop=(vm == KV - 1)), inc=None)
                    sq2_marks[su] = Tr(lambda vm=vm, su=su: ntn.matmul(
                        ps_st[:, 1, :], onec[:], u[:, 0, su * TT:(su + 1) * TT],
                        start=(vm == 0), stop=(vm == KV - 1)), inc="T")
                x1done_v = p.cnt["V"]
                mm_fence(ps_st[0:1, 0, 0:1])
                bc_v, stfree2 = ln_stats_and_bcast(tok, float(V))

                if "x1" in taps:
                    p.wait("gpsimd", "V", bc_v)
                    for vm in range(KV):
                        gp_load(d_tap["x1"][vm * 128:(vm + 1) * 128, sl_t], x1sb(vm))
                    p.wait("vector", "GDm")

                if stage < 4:
                    continue
                for mb in range(8):
                    if mb == 0:
                        p.wait("tensor", "V", bc_v if tok == 0 else x1done_v)
                    else:
                        for mt in range(4):
                            p.wait("tensor", "V", a_epi_mt[mt])
                    for kb in range(KV // 4):
                        j = ai % 2
                        ai += 1
                        p.wait("sync", "T", ab_marks[0])
                        al = sring.load(
                            ab4[:, j],
                            dd["aT"][kb * 512:(kb + 1) * 512,
                                     mb * TT:(mb + 1) * TT].rearrange(
                                         "(c q) m -> q c m", q=128))
                        sring.wait("tensor", al)
                        if tok == 0 and kb == 0:
                            p.wait("tensor", "V", rs_evac)
                        for c in range(4):
                            k = kb * 4 + c
                            for mt in range(4):
                                Tr(lambda j=j, c=c, mt=mt, k=k: ntn.matmul(
                                    ps_a[:, mt, :], ab4[:, j, c, mt * 128:(mt + 1) * 128],
                                    x1sb(k),
                                    start=(k == 0), stop=(k == KV - 1)),
                                   inc=(None if (mt < 3 or tok == 0 or c < 3) else "T"))
                            if tok == 0:
                                Tr(lambda j=j, c=c, k=k: ntn.matmul(
                                    ps_b[0:1, 0, :], nonec[:], ab4[:, j, c, :],
                                    start=(k == 0), stop=(k == KV - 1)),
                                   inc=("T" if c == 3 else None))
                        ab_marks = ab_marks[1:] + [p.cnt["T"]]
                    mm_t = mm_fence(ps_a[0:1, 0, 0:1])
                    ab_marks = ab_marks[1:] + [mm_t]
                    if tok == 0:
                        p.wait("vector", "T", mm_t)
                        p.wait("vector", "GDm")
                        rs_evac = Vr(lambda: nv.tensor_copy(rowb[:, 1, :], ps_b[0:1, 0, :]))
                        p.wait("gpsimd", "V", rs_evac)
                        gb1 = gp_load(bounce.ap()[0:1, mb * TT:(mb + 1) * TT], rowb[0:1, 1, :])
                        p.wait("gpsimd", "GDm", gb1)
                        rsA_gd = gp_load(
                            rsAS[:, mb * 4:(mb + 1) * 4],
                            bounce.ap()[0, mb * TT:(mb + 1) * TT].rearrange("(g q) -> q g", q=128),
                            slow=True)
                    p.wait("vector", "T", mm_t)
                    if tok == 0:
                        p.wait("vector", "GDm", rsA_gd)
                    for mt in range(4):
                        vm = mb * 4 + mt
                        sl2 = mt % 2
                        esl = u[:, 1, sl2 * TT:(sl2 + 1) * TT]
                        if a_out_mark[sl2] is not None:
                            p.wait("vector", a_out_mark[sl2][0], a_out_mark[sl2][1])
                        Vr(lambda mt=mt, esl=esl, sl_t=sl_t: nv.tensor_tensor(
                            esl, ps_a[:, mt, :], bcst[:, 0, sl_t], ALU.mult), inc=None)
                        a_epi_mt[mt] = p.cnt["V"]
                        Vr(lambda vm=vm, esl=esl, sl_t=sl_t: nv.scalar_tensor_tensor(
                            esl, bcst[:, 1, sl_t], rsAS[:, vm:vm + 1], esl,
                            ALU.mult, ALU.add), inc=None)
                        Vr(lambda vm=vm, esl=esl: nv.scalar_tensor_tensor(
                            esl, esl, bAS[:, vm:vm + 1], x1sb(vm),
                            ALU.add, ALU.add))
                        ep_v = p.cnt["V"]
                        p.wait("scalar", "V", ep_v)
                        onm = "O0" if sl2 % 2 == 0 else "O1"
                        p.wait("scalar", onm, p.cnt[onm])
                        p.run("scalar", lambda vm=vm, esl=esl, sl_t=sl_t:
                              nc.scalar.dma_start(out=outT[vm * 128:(vm + 1) * 128, sl_t], in_=esl),
                              onm, 16)
                        a_out_mark[sl2] = (onm, p.cnt[onm])
                    a_epi_v = p.cnt["V"]
                wout_mark = [a_epi_v] * 4
            mark("P4_wout")

        if stage >= 4:
            mark("P5_A")
        p.wait("sync", "O0", p.cnt["O0"])
        p.wait("sync", "O1", p.cnt["O1"])
        p.wait("sync", "GDm", p.cnt["GDm"])
        p.emit()
    return nc, d_tap


def kernel(**inputs):
    in_maps = stage_inputs(inputs)
    nc, _ = build_program(taps=DEBUG_TAPS, stage=STAGE)
    res = run_bass_kernel_spmd(nc, in_maps, core_ids=list(range(NCORES)))
    out = np.empty((B, L, V), dtype=np.float32)
    for c in range(NCORES):
        b, q = c // QPER, c % QPER
        out[b, q * TC:(q + 1) * TC, :] = res.results[c]["outT"].T
    return out


# version 11
# speedup vs baseline: 1.1084x; 1.0219x over previous
from contextlib import ExitStack

import numpy as np
import concourse.bass as bass
import concourse.mybir as mybir
from concourse.bass_utils import run_bass_kernel_spmd

F32 = mybir.dt.float32
I32 = mybir.dt.int32
AF = mybir.ActivationFunctionType
ALU = mybir.AluOpType

B, L, V, D, FF, DEPTH = 2, 8192, 4096, 512, 2048, 4
NCORES, QPER = 8, 4
TC = L // QPER
TT = 512
NTOK = TC // TT
KV = V // 128
GD = D // 128
KF = FF // 128
PI = float(np.pi)
TWO_PI = float(2 * np.pi)
INV_2PI = float(1.0 / (2 * np.pi))
PI_LO = float(np.float32(3.1415925))
LN_EPS = 1e-5

ENGINES = ("tensor", "vector", "scalar", "gpsimd", "sync")

DEBUG_TAPS = ()
STAGE = 4
PHASE_MARKS = []


class Prog:

    SELF = {"vector": "V", "scalar": "S", "gpsimd": "G"}

    def __init__(self, nc, stack, sem_names):
        self.nc = nc
        self.q = {e: [] for e in ENGINES}
        self.sem = {n: stack.enter_context(nc.semaphore(n)) for n in sem_names}
        self.cnt = {n: 0 for n in sem_names}

    def run(self, eng, fn, inc=None, n=1):
        selfsem = self.SELF.get(eng)
        if selfsem is not None and inc is None:
            inc, n = selfsem, 1
        if inc is not None:
            if selfsem is not None and inc == selfsem:
                prev = self.cnt[inc]
                if prev:
                    self.q[eng].append(("w", self.sem[inc], prev))
            self.cnt[inc] += n
            sem = self.sem[inc]
            self.q[eng].append(lambda fn=fn, sem=sem, n=n: fn().then_inc(sem, n))
            return self.cnt[inc]
        self.q[eng].append(fn)
        return None

    def wait(self, eng, name, val=None):
        v = self.cnt[name] if val is None else val
        if v <= 0:
            return
        self.q[eng].append(("w", self.sem[name], v))

    def emit(self):
        with self.nc.Block() as blk:
            for e in ENGINES:
                def body(engobj, e=e):
                    for item in self.q[e]:
                        if isinstance(item, tuple):
                            engobj.wait_ge(item[1], item[2])
                        else:
                            item()
                getattr(blk, e)(body)


def stage_inputs(inp):
    f = np.ascontiguousarray
    x = np.asarray(inp["L0_attn_x"], dtype=np.float32)

    def ch(v):
        return f(np.asarray(v, np.float32).reshape(GD, 128).T)

    def chl(v):
        return f(np.asarray(v, np.float32).reshape(DEPTH, GD, 128).transpose(2, 0, 1))

    def vw(v, g):
        return f(np.asarray(v, np.float32).reshape(g, 128).T)

    common = {
        "winT": f(np.asarray(inp["L0_attn_Win"], np.float32).T),
        "w1T": f(np.asarray(inp["L0_attn_W1"], np.float32).transpose(0, 2, 1)),
        "w2T": f(np.asarray(inp["L0_attn_W2"], np.float32).transpose(0, 2, 1)),
        "woutT": f(np.asarray(inp["L0_attn_Wout"], np.float32).T),
        "aT": f(np.asarray(inp["L0_ffn_A"], np.float32).T),
        "bin": ch(inp["L0_attn_bin"]),
        "b1": f(np.asarray(inp["L0_attn_b1"], np.float32).reshape(DEPTH, KF, 128).transpose(2, 0, 1)),
        "b2": chl(inp["L0_attn_b2"]),
        "ln1w": chl(inp["L0_attn_ln1_w"]), "ln1b": chl(inp["L0_attn_ln1_b"]),
        "ln2w": chl(inp["L0_attn_ln2_w"]), "ln2b": chl(inp["L0_attn_ln2_b"]),
        "phzr": chl(inp["L0_attn_phz_re"]), "phzi": chl(inp["L0_attn_phz_im"]),
        "lcr": chl(inp["L0_attn_lc_re"]), "lci": chl(inp["L0_attn_lc_im"]),
        "scl": chl(inp["L0_attn_sc_lin"]),
        "lnlw": ch(inp["L0_attn_lnl_w"]), "lnlb": ch(inp["L0_attn_lnl_b"]),
        "bout": vw(inp["L0_attn_bout"], KV),
        "bA": vw(inp["L0_ffn_b"], KV),
        "iota": f(np.broadcast_to(np.arange(TC, dtype=np.float32), (128, TC)).copy()),
    }
    in_maps = []
    for c in range(NCORES):
        b, q = c // QPER, c % QPER
        xT = f(x[b, q * TC:(q + 1) * TC, :].T)
        sel = np.zeros((128, 7), np.float32)
        for j in range(7):
            if QPER * b <= j < QPER * b + q:
                sel[:, j] = 1.0
        in_maps.append(dict(common, xT=xT, sel=sel))
    return in_maps


def build_program(taps=(), stage=4):
    PHASE_MARKS.clear()
    nc = bass.Bass("TRN2", target_bir_lowering=False)
    taps = set(taps)
    nv, ns, ntn, ng = nc.vector, nc.scalar, nc.tensor, nc.gpsimd

    dd = {}
    for name, shape in [
        ("xT", [V, TC]), ("winT", [V, D]), ("w1T", [DEPTH, D, FF]), ("w2T", [DEPTH, FF, D]),
        ("woutT", [D, V]), ("aT", [V, V]), ("bin", [128, GD]),
        ("b1", [128, DEPTH, KF]), ("b2", [128, DEPTH, GD]),
        ("ln1w", [128, DEPTH, GD]), ("ln1b", [128, DEPTH, GD]),
        ("ln2w", [128, DEPTH, GD]), ("ln2b", [128, DEPTH, GD]),
        ("phzr", [128, DEPTH, GD]), ("phzi", [128, DEPTH, GD]),
        ("lcr", [128, DEPTH, GD]), ("lci", [128, DEPTH, GD]),
        ("scl", [128, DEPTH, GD]), ("lnlw", [128, GD]), ("lnlb", [128, GD]),
        ("bout", [128, KV]), ("bA", [128, KV]), ("iota", [128, TC]), ("sel", [128, 7]),
    ]:
        dd[name] = nc.dram_tensor(name, shape, F32, kind="ExternalInput").ap()

    outT = nc.dram_tensor("outT", [V, TC], F32, kind="ExternalOutput").ap() if stage >= 4 else None
    d_tap = {}
    for t in taps:
        shp = {"h0": [128, GD * TC], "y0": [128, GD * TC], "conv0": [128, GD * TC],
               "h1": [128, GD * TC], "hfin": [128, GD * TC], "hln": [128, GD * TC],
               "x1": [V, TC]}[t]
        d_tap[t] = nc.dram_tensor("tap_" + t, shp, F32, kind="ExternalOutput").ap()

    ag_in = nc.dram_tensor("ag_in", [1, 1024], F32)
    ag_out = nc.dram_tensor("ag_out", [NCORES, 1024], F32)
    bounce = nc.dram_tensor("bounce", [1, V], F32)

    st = ExitStack()
    with st:
        def sbt(name, shape, dt=F32):
            return st.enter_context(nc.sbuf_tensor(name, shape, dt))

        h = sbt("h", [128, GD, TC])
        y = sbt("y", [128, GD, TC])
        tabs = sbt("tabs", [128, 2, TC])
        u = sbt("u", [128, 2, TC])
        bcst = sbt("bcst", [128, 2, TC])
        y1 = sbt("y1", [128, 2, TT])
        w1t = sbt("w1t", [128, GD, FF])
        w2b = sbt("w2b", [128, 3, TT])
        xb = sbt("xb", [128, 3, TT])
        wtb = sbt("wtb", [128, 3, TT])
        ab4 = sbt("ab4", [128, 2, 4, TT])
        xeb = sbt("xeb", [128, 3, TT])
        rowb = sbt("rowb", [1, 2, TT])
        iot = sbt("iot", [128, TC])
        onec = sbt("onec", [128, 1])
        zerc = sbt("zerc", [128, 1])
        nonec = sbt("nonec", [128, 1])
        oner = sbt("oner", [1, 128])
        binS = sbt("binS", [128, GD])
        b1S = sbt("b1S", [128, DEPTH, KF])
        b2S = sbt("b2S", [128, DEPTH, GD])
        prm = {k: sbt(k + "S", [128, DEPTH, GD]) for k in
               ["ln1w", "ln1b", "ln2w", "ln2b", "phzr", "phzi", "lcr", "lci", "scl"]}
        lnlwS = sbt("lnlwS", [128, GD])
        lnlbS = sbt("lnlbS", [128, GD])
        boutS = sbt("boutS", [128, KV])
        bAS = sbt("bAS", [128, KV])
        selS = sbt("selS", [128, 7])
        rsS = sbt("rsS", [128, GD])
        rsAS = sbt("rsAS", [128, KV])
        pp = {k: sbt("pp_" + k, [128, DEPTH, GD]) for k in
              ["rho", "lnrho", "absth", "sgn", "eR", "eI", "p2r", "p2i",
               "ln2047", "th2047", "c47", "s47", "nsgn",
               "c256", "s256", "c512", "s512", "c1024", "s1024",
               "kr256", "ki256", "kr512", "ki512", "kr1024", "ki1024"]}
        t16a = sbt("t16a", [128, DEPTH * GD])
        t16b = sbt("t16b", [128, DEPTH * GD])
        t16c = sbt("t16c", [128, DEPTH * GD])
        t16d = sbt("t16d", [128, DEPTH * GD])
        fcell = sbt("fcell", [1, 8])
        ES = sbt("ES", [128, 8])
        EA = sbt("EA", [128, 7, 8])
        CCs = sbt("CCs", [128, 8])
        WIs = sbt("WIs", [128, 8])
        c8a = sbt("c8a", [128, 8])
        c8b = sbt("c8b", [128, 8])

        ps_a = st.enter_context(nc.psum_tensor("ps_a", [128, 4, TT], F32))
        ps_b = st.enter_context(nc.psum_tensor("ps_b", [128, 2, TT], F32))
        ps_st = st.enter_context(nc.psum_tensor("ps_st", [1, 2, TT], F32))

        p = Prog(nc, st, ["T", "V", "S", "G", "GDm",
                          "L0", "L1", "L2", "L3", "L4", "L5",
                          "E0", "E1", "E2", "O0", "O1"])

        def Vr(fn, inc="V", n=1):
            return p.run("vector", fn, None, 1)

        def Sc(fn, inc="S", n=1):
            return p.run("scalar", fn, None, 1)

        def Tr(fn, inc=None, n=1):
            return p.run("tensor", fn, inc, n)

        def Gp(fn, inc=None, n=1):
            if inc is None:
                return p.run("gpsimd", fn, None, 1)
            return p.run("gpsimd", fn, inc, n)

        def Sy(fn, inc=None, n=1):
            return p.run("sync", fn, inc, n)

        def mark(name):
            PHASE_MARKS.append((name, dict(p.cnt)))

        class Ring:

            def __init__(self, eng, names):
                self.eng, self.names = eng, names
                self.i = 0

            def load_group(self, pairs):
                nm = self.names[self.i % len(self.names)]
                self.i += 1
                p.wait(self.eng, nm, p.cnt[nm])
                for dst_ap, src_ap in pairs:
                    if self.eng == "sync":
                        p.run("sync", lambda dst_ap=dst_ap, src_ap=src_ap:
                              nc.sync.dma_start(out=dst_ap, in_=src_ap), nm, 16)
                    else:
                        p.run("gpsimd", lambda dst_ap=dst_ap, src_ap=src_ap:
                              ng.dma_start(out=dst_ap, in_=src_ap), nm, 16)
                return (nm, p.cnt[nm])

            def load(self, dst_ap, src_ap, slow=False):
                nm = self.names[self.i % len(self.names)]
                self.i += 1
                p.wait(self.eng, nm, p.cnt[nm])
                if self.eng == "sync":
                    p.run("sync", lambda dst_ap=dst_ap, src_ap=src_ap:
                          nc.sync.dma_start(out=dst_ap, in_=src_ap), nm, 16)
                else:
                    p.run("gpsimd", lambda dst_ap=dst_ap, src_ap=src_ap, slow=slow:
                          ng.dma_start(out=dst_ap, in_=src_ap,
                                       allow_slow_non_contiguous=slow), nm, 16)
                return (nm, p.cnt[nm])

            def wait(self, eng, mark):
                p.wait(eng, mark[0], mark[1])

        sring = Ring("sync", ["L0", "L1", "L2", "L3", "L4", "L5"])
        ering = Ring("gpsimd", ["E0", "E1", "E2"])

        def gp_load(dst_ap, src_ap, slow=False):
            p.wait("gpsimd", "GDm", p.cnt["GDm"])
            return Gp(lambda dst_ap=dst_ap, src_ap=src_ap, slow=slow:
                      ng.dma_start(out=dst_ap, in_=src_ap,
                                   allow_slow_non_contiguous=slow), inc="GDm", n=16)

        def tap_out(name, src):
            m = gp_load(d_tap[name], src)
            p.wait("vector", "GDm", m)
            p.wait("scalar", "GDm", m)
            p.wait("tensor", "GDm", m)

        def mm_fence(bank_ap):
            Tr(lambda bank_ap=bank_ap: ntn.matmul(
                bank_ap, zerc[:, 0:1], onec[:, 0:1],
                start=False, stop=True, skip_group_check=True), inc=None)
            return Tr(lambda bank_ap=bank_ap: ntn.matmul(
                bank_ap, zerc[:, 0:1], onec[:, 0:1],
                start=False, stop=True, skip_group_check=True), inc="T")

        def range_reduce(dst, src, iscr):
            ii = iscr.bitcast(I32)
            Vr(lambda: nv.tensor_single_scalar(dst, src, INV_2PI, ALU.mult), inc=None)
            Vr(lambda: nv.tensor_copy(ii, dst), inc=None)
            Vr(lambda: nv.tensor_copy(iscr, ii), inc=None)
            Vr(lambda: nv.scalar_tensor_tensor(dst, iscr, -TWO_PI, src, ALU.mult, ALU.add), inc=None)
            Vr(lambda: nv.tensor_single_scalar(iscr, dst, PI, ALU.is_gt), inc=None)
            Vr(lambda: nv.scalar_tensor_tensor(dst, iscr, -TWO_PI, dst, ALU.mult, ALU.add), inc=None)
            Vr(lambda: nv.tensor_single_scalar(iscr, dst, -PI, ALU.is_lt), inc=None)
            Vr(lambda: nv.scalar_tensor_tensor(dst, iscr, TWO_PI, dst, ALU.mult, ALU.add), inc=None)
            Vr(lambda: nv.tensor_single_scalar(dst, dst, PI_LO, ALU.min), inc=None)
            Vr(lambda: nv.tensor_single_scalar(dst, dst, -PI_LO, ALU.max), inc=None)

        def cos_from_red(dst, red, mscr):
            Vr(lambda: nv.tensor_scalar(mscr, red, -1.0, PI / 2, ALU.mult, ALU.add), inc=None)
            Vr(lambda: nv.tensor_single_scalar(dst, mscr, PI, ALU.is_gt), inc=None)
            Vr(lambda: nv.scalar_tensor_tensor(mscr, dst, -TWO_PI, mscr, ALU.mult, ALU.add), inc=None)
            Vr(lambda: nv.tensor_single_scalar(mscr, mscr, PI_LO, ALU.min), inc=None)
            Vr(lambda: nv.tensor_single_scalar(mscr, mscr, -PI_LO, ALU.max))
            p.wait("scalar", "V")
            Sc(lambda: ns.activation(dst, mscr, AF.Sin))

        DBL = [(256, "c256", "s256", "kr256", "ki256"),
               (512, "c512", "s512", "kr512", "ki512"),
               (1024, "c1024", "s1024", "kr1024", "ki1024")]

        def build_ptab(lay, g, with_rho):
            ath = pp["absth"][:, lay, g:g + 1].broadcast_to([128, 256])
            Vr(lambda ath=ath: nv.tensor_tensor(
                u[:, 0, 0:256], iot[:, 0:256], ath, ALU.mult))
            range_reduce(u[:, 1, 0:256], u[:, 0, 0:256], tabs[:, 0, 0:256])
            cos_from_red(tabs[:, 0, 0:256], u[:, 1, 0:256], tabs[:, 1, 0:256])
            Sc(lambda: ns.activation(tabs[:, 1, 0:256], u[:, 1, 0:256], AF.Sin))
            if with_rho:
                lnr = pp["lnrho"][:, lay, g:g + 1].broadcast_to([128, 256])
                Vr(lambda lnr=lnr: nv.tensor_tensor(
                    u[:, 0, 0:256], iot[:, 0:256], lnr, ALU.mult))
                p.wait("scalar", "V")
                Sc(lambda: ns.activation(u[:, 0, 0:256], u[:, 0, 0:256], AF.Exp))
                p.wait("vector", "S")
                Vr(lambda: nv.tensor_tensor(
                    tabs[:, 0, 0:256], tabs[:, 0, 0:256], u[:, 0, 0:256], ALU.mult), inc=None)
                Vr(lambda: nv.tensor_tensor(
                    tabs[:, 1, 0:256], tabs[:, 1, 0:256], u[:, 0, 0:256], ALU.mult), inc=None)
            else:
                p.wait("vector", "S")
            for L, ck, sk, krk, kik in DBL:
                cc = pp[krk if with_rho else ck][:, lay, g:g + 1]
                ss = pp[kik if with_rho else sk][:, lay, g:g + 1]
                ssb = ss.broadcast_to([128, L])
                Vr(lambda L=L, ssb=ssb: nv.tensor_tensor(
                    u[:, 0, 0:L], tabs[:, 1, 0:L], ssb, ALU.mult), inc=None)
                Vr(lambda L=L, ssb=ssb: nv.tensor_tensor(
                    u[:, 1, 0:L], tabs[:, 0, 0:L], ssb, ALU.mult), inc=None)
                Vr(lambda L=L, cc=cc: nv.scalar_tensor_tensor(
                    tabs[:, 0, L:2 * L], tabs[:, 0, 0:L], cc, u[:, 0, 0:L],
                    ALU.mult, ALU.subtract), inc=None)
                Vr(lambda L=L, cc=cc: nv.scalar_tensor_tensor(
                    tabs[:, 1, L:2 * L], tabs[:, 1, 0:L], cc, u[:, 1, 0:L],
                    ALU.mult, ALU.add), inc=None)

        def ln_stats_and_bcast(tok, denom, extra_wait_v=None):
            mu, rr = rowb[:, 0, :], rowb[:, 1, :]
            p.wait("vector", "T")
            Vr(lambda: nv.tensor_single_scalar(mu, ps_st[:, 0, :], 1.0 / denom, ALU.mult), inc=None)
            Vr(lambda: nv.tensor_tensor(rr, mu, mu, ALU.mult), inc=None)
            Vr(lambda: nv.scalar_tensor_tensor(rr, ps_st[:, 1, :], 1.0 / denom, rr,
                                               ALU.mult, ALU.subtract))
            st_free = p.cnt["V"]
            Vr(lambda: nv.tensor_single_scalar(rr, rr, LN_EPS, ALU.add))
            p.wait("scalar", "V")
            Sc(lambda: ns.activation(rr, rr, AF.Sqrt))
            p.wait("vector", "S")
            Vr(lambda: nv.reciprocal(rr, rr), inc=None)
            Vr(lambda: nv.tensor_tensor(mu, rr, mu, ALU.mult))
            rows_done = p.cnt["V"]
            p.wait("tensor", "V", rows_done)
            if extra_wait_v is not None:
                p.wait("tensor", "V", extra_wait_v)
            p.wait("tensor", "S")
            Tr(lambda: ntn.matmul(ps_b[:, 0, :], oner[:], rr, start=True, stop=True), inc=None)
            Tr(lambda: ntn.matmul(ps_b[:, 1, :], oner[:], mu, start=True, stop=True), inc=None)
            mm_fence(ps_b[0:1, 0, 0:1])
            p.wait("vector", "T")
            sl = slice(tok * TT, (tok + 1) * TT)
            Vr(lambda sl=sl: nv.tensor_copy(bcst[:, 0, sl], ps_b[:, 0, :]), inc=None)
            Vr(lambda sl=sl: nv.tensor_copy(bcst[:, 1, sl], ps_b[:, 1, :]))
            return p.cnt["V"], st_free

        Vr(lambda: nv.memset(onec[:], 1.0), inc=None)
        Vr(lambda: nv.memset(zerc[:], 0.0), inc=None)
        Vr(lambda: nv.memset(fcell[:], 1.0), inc=None)
        Vr(lambda: nv.memset(nonec[:], -1.0), inc=None)
        Vr(lambda: nv.memset(oner[:], 1.0))
        const_v = p.cnt["V"]
        p.wait("tensor", "V", const_v)
        p.wait("scalar", "V", const_v)
        p.wait("gpsimd", "V", const_v)
        for dst, src in [(iot, dd["iota"]), (binS, dd["bin"]), (b1S, dd["b1"]),
                         (b2S, dd["b2"]), (lnlwS, dd["lnlw"]), (lnlbS, dd["lnlb"]),
                         (boutS, dd["bout"]), (bAS, dd["bA"]), (selS, dd["sel"])] + \
                        [(prm[k], dd[k]) for k in prm]:
            gp_load(dst[:], src)
        ld_params = p.cnt["GDm"]

        phr, phi = prm["phzr"][:, :, :], prm["phzi"][:, :, :]
        rho16, lnrho16 = pp["rho"][:, :, :], pp["lnrho"][:, :, :]
        absth16, sgn16 = pp["absth"][:, :, :], pp["sgn"][:, :, :]
        eR16, eI16 = pp["eR"][:, :, :], pp["eI"][:, :, :]
        p2r16, p2i16 = pp["p2r"][:, :, :], pp["p2i"][:, :, :]
        ln2047_16, th2047_16 = pp["ln2047"][:, :, :], pp["th2047"][:, :, :]
        A16, B16, C16, D16 = t16a[:], t16b[:], t16c[:], t16d[:]

        p.wait("vector", "GDm", ld_params)
        Vr(lambda: nv.tensor_tensor(A16, phr, phr, ALU.mult), inc=None)
        Vr(lambda: nv.tensor_tensor(B16, phi, phi, ALU.mult), inc=None)
        Vr(lambda: nv.tensor_tensor(A16, A16, B16, ALU.add), inc=None)
        Vr(lambda: nv.tensor_single_scalar(A16, A16, 1e-30, ALU.max))
        p.wait("scalar", "V")
        Sc(lambda: ns.activation(B16, A16, AF.Sqrt))
        p.wait("vector", "S")
        Vr(lambda: nv.reciprocal(C16, B16), inc=None)
        Vr(lambda: nv.tensor_tensor(eR16, phr, C16, ALU.mult), inc=None)
        Vr(lambda: nv.tensor_tensor(eI16, phi, C16, ALU.mult))
        p.wait("scalar", "V")
        Sc(lambda: ns.activation(rho16, B16, AF.Tanh), inc=None)
        Sc(lambda: ns.activation(lnrho16, rho16, AF.Ln), inc=None)
        Sc(lambda: ns.activation(A16, eR16, AF.Abs), inc=None)
        Sc(lambda: ns.activation(B16, eI16, AF.Abs))
        p.wait("vector", "S")
        Vr(lambda: nv.tensor_single_scalar(ln2047_16, lnrho16, 2047.0, ALU.mult), inc=None)
        Vr(lambda: nv.tensor_tensor(C16, A16, B16, ALU.min), inc=None)
        Vr(lambda: nv.tensor_tensor(D16, A16, B16, ALU.max), inc=None)
        Vr(lambda: nv.reciprocal(D16, D16), inc=None)
        Vr(lambda: nv.tensor_tensor(C16, C16, D16, ALU.mult))
        p.wait("scalar", "V")
        Sc(lambda: ns.activation(D16, C16, AF.Arctan))
        p.wait("scalar", "S")
        Sc(lambda: ns.activation(A16, D16, AF.Sin))
        p.wait("vector", "S")
        Vr(lambda: nv.tensor_tensor(B16, A16, A16, ALU.mult), inc=None)
        Vr(lambda: nv.tensor_scalar(B16, B16, -1.0, 1.0, ALU.mult, ALU.add))
        p.wait("scalar", "V")
        Sc(lambda: ns.activation(B16, B16, AF.Sqrt))
        p.wait("vector", "S")
        Vr(lambda: nv.tensor_tensor(C16, C16, B16, ALU.mult), inc=None)
        Vr(lambda: nv.tensor_tensor(C16, C16, A16, ALU.subtract), inc=None)
        Vr(lambda: nv.tensor_tensor(C16, C16, B16, ALU.mult), inc=None)
        Vr(lambda: nv.tensor_tensor(D16, D16, C16, ALU.add), inc=None)
        Vr(lambda: nv.tensor_tensor(A16, eI16, eI16, ALU.mult), inc=None)
        Vr(lambda: nv.tensor_tensor(B16, eR16, eR16, ALU.mult), inc=None)
        Vr(lambda: nv.tensor_tensor(A16, A16, B16, ALU.is_gt), inc=None)
        Vr(lambda: nv.tensor_scalar(B16, D16, -2.0, PI / 2, ALU.mult, ALU.add), inc=None)
        Vr(lambda: nv.tensor_tensor(A16, A16, B16, ALU.mult), inc=None)
        Vr(lambda: nv.tensor_tensor(D16, D16, A16, ALU.add), inc=None)
        Vr(lambda: nv.tensor_single_scalar(A16, eR16, 0.0, ALU.is_lt), inc=None)
        Vr(lambda: nv.tensor_scalar(B16, D16, -2.0, PI, ALU.mult, ALU.add), inc=None)
        Vr(lambda: nv.tensor_tensor(A16, A16, B16, ALU.mult), inc=None)
        Vr(lambda: nv.tensor_tensor(absth16, D16, A16, ALU.add), inc=None)
        Vr(lambda: nv.tensor_single_scalar(A16, eI16, 0.0, ALU.is_ge), inc=None)
        Vr(lambda: nv.tensor_scalar(sgn16, A16, 2.0, -1.0, ALU.mult, ALU.add), inc=None)
        Vr(lambda: nv.tensor_single_scalar(th2047_16, absth16, 2047.0, ALU.mult), inc=None)
        Vr(lambda: nv.tensor_single_scalar(A16, lnrho16, 2048.0, ALU.mult))
        p.wait("scalar", "V")
        Sc(lambda: ns.activation(A16, A16, AF.Exp))
        p.wait("vector", "S")
        Vr(lambda: nv.tensor_single_scalar(B16, absth16, 2048.0, ALU.mult), inc=None)
        range_reduce(C16, B16, D16)
        Vr(lambda: nv.tensor_copy(D16, C16))
        p.wait("scalar", "V")
        Sc(lambda: ns.activation(B16, C16, AF.Sin))
        p.wait("vector", "S")
        Vr(lambda: nv.tensor_tensor(p2i16, A16, B16, ALU.mult), inc=None)
        Vr(lambda: nv.tensor_tensor(p2i16, p2i16, sgn16, ALU.mult), inc=None)
        cos_from_red(B16, D16, C16)
        p.wait("vector", "S")
        Vr(lambda: nv.tensor_tensor(p2r16, A16, B16, ALU.mult), inc=None)
        Vr(lambda: nv.tensor_single_scalar(pp["nsgn"][:, :, :], sgn16, -1.0, ALU.mult), inc=None)
        range_reduce(C16, th2047_16, D16)
        Vr(lambda: nv.tensor_copy(D16, C16))
        cos_from_red(pp["c47"][:, :, :], C16, B16)
        Sc(lambda: ns.activation(pp["s47"][:, :, :], D16, AF.Sin))
        p.wait("vector", "S")
        for Lf, ck, sk, krk, kik in [(256.0, "c256", "s256", "kr256", "ki256"),
                                     (512.0, "c512", "s512", "kr512", "ki512"),
                                     (1024.0, "c1024", "s1024", "kr1024", "ki1024")]:
            Vr(lambda Lf=Lf: nv.tensor_single_scalar(A16, absth16, Lf, ALU.mult), inc=None)
            range_reduce(C16, A16, D16)
            Vr(lambda: nv.tensor_copy(D16, C16), inc=None)
            Vr(lambda Lf=Lf: nv.tensor_single_scalar(A16, lnrho16, Lf, ALU.mult))
            cos_from_red(pp[ck][:, :, :], C16, B16)
            Sc(lambda sk=sk: ns.activation(pp[sk][:, :, :], D16, AF.Sin), inc=None)
            Sc(lambda: ns.activation(A16, A16, AF.Exp))
            p.wait("vector", "S")
            Vr(lambda krk=krk, ck=ck: nv.tensor_tensor(
                pp[krk][:, :, :], pp[ck][:, :, :], A16, ALU.mult), inc=None)
            Vr(lambda kik=kik, sk=sk: nv.tensor_tensor(
                pp[kik][:, :, :], pp[sk][:, :, :], A16, ALU.mult))
        mark("P1")

        tmark_last = 0
        epi_done_v = 0
        stfree_v = 0
        tmarks = [0, 0]
        smarks = [0, 0]
        for tok in range(NTOK):
            for k in range(KV):
                sl = k % 3
                p.wait("sync", "T", tmarks[0])
                p.wait("sync", "S", smarks[0])
                mk = sring.load_group([
                    (xb[:, sl, :], dd["xT"][k * 128:(k + 1) * 128, tok * TT:(tok + 1) * TT]),
                    (wtb[:, sl, :], dd["winT"][k * 128:(k + 1) * 128, :])])
                sring.wait("scalar", mk)
                p.wait("scalar", "T", tmarks[0])
                s_sq = Sc(lambda sl=sl: ns.activation(xeb[:, sl, :], xb[:, sl, :], AF.Square))
                smarks = [smarks[1], s_sq]
                sring.wait("tensor", mk)
                if k == 0:
                    p.wait("tensor", "V", epi_done_v)
                    p.wait("tensor", "V", stfree_v)
                for dm in range(GD):
                    Tr(lambda k=k, sl=sl, dm=dm: ntn.matmul(
                        ps_a[:, dm, :], wtb[:, sl, dm * 128:(dm + 1) * 128], xb[:, sl, :],
                        start=(k == 0), stop=(k == KV - 1)), inc=None)
                Tr(lambda k=k, sl=sl: ntn.matmul(
                    ps_st[:, 0, :], onec[:], xb[:, sl, :],
                    start=(k == 0), stop=(k == KV - 1)), inc=None)
                p.wait("tensor", "S", s_sq)
                if tok == 0:
                    Tr(lambda k=k, sl=sl: ntn.matmul(
                        ps_b[0:1, 0, 0:D], nonec[:], wtb[:, sl, :],
                        start=(k == 0), stop=(k == KV - 1)), inc=None)
                tm = Tr(lambda k=k, sl=sl: ntn.matmul(
                    ps_st[:, 1, :], onec[:], xeb[:, sl, :],
                    start=(k == 0), stop=(k == KV - 1)), inc="T")
                tmarks = [tmarks[1], tm]
            mm_fence(ps_st[0:1, 0, 0:1])
            if tok == 0:
                p.wait("vector", "T")
                rs_v = Vr(lambda: nv.tensor_copy(rowb[:, 1, 0:TT], ps_b[0:1, 0, 0:D]))
                p.wait("gpsimd", "V", rs_v)
                g1 = gp_load(bounce.ap()[0:1, 0:D], rowb[0:1, 1, 0:D])
                p.wait("gpsimd", "GDm", g1)
                g2 = gp_load(rsS[:], bounce.ap()[0, 0:D].rearrange("(g q) -> q g", q=128), slow=True)
                p.wait("vector", "GDm", g1)
            bc_v, stfree_v = ln_stats_and_bcast(tok, float(V), extra_wait_v=(rs_v if tok == 0 else None))
            if tok == 0:
                p.wait("vector", "GDm", g2)
            sl_t = slice(tok * TT, (tok + 1) * TT)
            for dm in range(GD):
                hsl = h[:, dm, sl_t]
                Vr(lambda dm=dm, hsl=hsl, sl_t=sl_t: nv.tensor_tensor(
                    hsl, ps_a[:, dm, :], bcst[:, 0, sl_t], ALU.mult), inc=None)
                Vr(lambda dm=dm, hsl=hsl, sl_t=sl_t: nv.scalar_tensor_tensor(
                    hsl, bcst[:, 1, sl_t], rsS[:, dm:dm + 1], hsl, ALU.mult, ALU.add), inc=None)
                Vr(lambda dm=dm, hsl=hsl: nv.tensor_scalar_add(hsl, hsl, binS[:, dm:dm + 1]))
            epi_done_v = p.cnt["V"]

        mark("P2_win")
        if "h0" in taps:
            p.wait("gpsimd", "V")
            tap_out("h0", h[:, :, :])

        hf = h[:, :, :]
        yf = y[:, :, :]

        def inner_ln(lay, wkey, bkey, act, y_free_t=0):
            nonlocal stfree_v, epi_done_v
            p.wait("scalar", "V")
            p.wait("scalar", "T", y_free_t)
            sq_marks = []
            for tok in range(NTOK):
                sq_marks.append(Sc(lambda tok=tok: ns.activation(
                    y[:, :, tok * TT:(tok + 1) * TT],
                    h[:, :, tok * TT:(tok + 1) * TT], AF.Square)))
            sq_s = sq_marks[-1]
            p.wait("tensor", "V", p.cnt["V"])
            for tok in range(NTOK):
                p.wait("tensor", "V", stfree_v)
                for kt in range(GD):
                    Tr(lambda kt=kt, tok=tok: ntn.matmul(
                        ps_st[:, 0, :], onec[:], h[:, kt, tok * TT:(tok + 1) * TT],
                        start=(kt == 0), stop=(kt == GD - 1)), inc=None)
                p.wait("tensor", "S", sq_marks[tok])
                for kt in range(GD):
                    Tr(lambda kt=kt, tok=tok: ntn.matmul(
                        ps_st[:, 1, :], onec[:], y[:, kt, tok * TT:(tok + 1) * TT],
                        start=(kt == 0), stop=(kt == GD - 1)), inc=None)
                mm_fence(ps_st[0:1, 0, 0:1])
                bc_v, stfree_v = ln_stats_and_bcast(tok, float(D))
            rb = bcst[:, 0:1, :].broadcast_to([128, GD, TC])
            mb = bcst[:, 1:2, :].broadcast_to([128, GD, TC])
            Vr(lambda: nv.tensor_tensor(yf, hf, rb, ALU.mult), inc=None)
            Vr(lambda: nv.tensor_tensor(yf, yf, mb, ALU.subtract))
            if act == AF.Copy:
                for g in range(GD):
                    wcol = prm[wkey][:, lay, g:g + 1]
                    bb = prm[bkey][:, lay, g:g + 1].broadcast_to([128, TC])
                    Vr(lambda g=g, wcol=wcol, bb=bb: nv.scalar_tensor_tensor(
                        y[:, g, :], y[:, g, :], wcol, bb, ALU.mult, ALU.add))
                return ("V", p.cnt["V"])
            p.wait("scalar", "V")
            for g in range(GD):
                Sc(lambda g=g: ns.activation(
                    y[:, g, :], y[:, g, :], act,
                    bias=prm[bkey][:, lay, g:g + 1], scale=prm[wkey][:, lay, g:g + 1]))
            return ("S", p.cnt["S"])

        prev_mm1_t = 0
        w2marks = [0, 0]
        w2i = 0
        for lay in range(DEPTH if stage >= 2 else 0):
            p.wait("sync", "T", prev_mm1_t)
            w1_ld = sring.load(w1t[:, :, :], dd["w1T"][lay].rearrange("(kt q) f -> q kt f", q=128))

            s_ln1 = inner_ln(lay, "ln1w", "ln1b", AF.Silu, y_free_t=prev_mm1_t)
            mark(f"L{lay}_ln1")
            if lay == 0 and "y0" in taps:
                p.wait("gpsimd", s_ln1[0], s_ln1[1])
                tap_out("y0", yf)

            p.wait("vector", s_ln1[0], s_ln1[1])
            for g in range(GD):
                build_ptab(lay, g, with_rho=False)
                rho_b = pp["rho"][:, lay, g:g + 1].broadcast_to([128, TC])
                Vr(lambda g=g: nv.tensor_tensor(u[:, 0, :], tabs[:, 0, :], y[:, g, :], ALU.mult), inc=None)
                Vr(lambda g=g: nv.tensor_tensor(u[:, 1, :], tabs[:, 1, :], y[:, g, :], ALU.mult), inc=None)
                Vr(lambda rho_b=rho_b: nv.tensor_tensor_scan(
                    u[:, 0, :], rho_b, u[:, 0, :], 0.0, ALU.mult, ALU.add), inc=None)
                Vr(lambda rho_b=rho_b: nv.tensor_tensor_scan(
                    u[:, 1, :], rho_b, u[:, 1, :], 0.0, ALU.mult, ALU.subtract), inc=None)
                u0e, u1e = u[:, 0, TC - 1:TC], u[:, 1, TC - 1:TC]
                c47c = pp["c47"][:, lay, g:g + 1]
                s47c = pp["s47"][:, lay, g:g + 1]
                sgnc = pp["sgn"][:, lay, g:g + 1]
                Vr(lambda u1e=u1e, s47c=s47c: nv.tensor_tensor(
                    c8a[:, 0:1], u1e, s47c, ALU.mult), inc=None)
                Vr(lambda g=g, u0e=u0e, c47c=c47c: nv.scalar_tensor_tensor(
                    ES[:, g:g + 1], u0e, c47c, c8a[:, 0:1], ALU.mult, ALU.subtract), inc=None)
                Vr(lambda u1e=u1e, c47c=c47c: nv.tensor_tensor(
                    c8a[:, 1:2], u1e, c47c, ALU.mult), inc=None)
                Vr(lambda u0e=u0e, s47c=s47c: nv.scalar_tensor_tensor(
                    c8b[:, 0:1], u0e, s47c, c8a[:, 1:2], ALU.mult, ALU.add), inc=None)
                Vr(lambda g=g, sgnc=sgnc: nv.tensor_tensor(
                    ES[:, 4 + g:5 + g], c8b[:, 0:1], sgnc, ALU.mult), inc=None)
                Vr(lambda g=g: nv.tensor_tensor(y[:, g, :], tabs[:, 0, :], u[:, 0, :], ALU.mult), inc=None)
                Vr(lambda g=g: nv.tensor_tensor(u[:, 0, :], tabs[:, 1, :], u[:, 1, :], ALU.mult), inc=None)
                Vr(lambda g=g: nv.tensor_tensor(y[:, g, :], y[:, g, :], u[:, 0, :], ALU.subtract))
            e_v = p.cnt["V"]

            p.wait("gpsimd", "V", e_v)
            gp1 = gp_load(ag_in.ap().rearrange("o (q k) -> (o q) k", q=128), ES[:])
            p.wait("gpsimd", "GDm", gp1)
            cc_m = Gp(lambda: ng.collective_compute(
                "AllGather", ALU.bypass, replica_groups=[list(range(NCORES))],
                ins=[ag_in.ap().opt()], outs=[ag_out.ap().opt()]))
            p.wait("gpsimd", "G", cc_m)
            gp2 = gp_load(EA[:, :, :], ag_out.ap()[0:7, :].rearrange("j (q k) -> q j k", q=128))

            build_ptab(lay, 0, with_rho=True)

            p.wait("vector", "GDm", gp2)
            Vr(lambda lay=lay: nv.tensor_copy(CCs[:, 0:4], prm["lcr"][:, lay, :]), inc=None)
            Vr(lambda lay=lay: nv.tensor_copy(CCs[:, 4:8], prm["lci"][:, lay, :]), inc=None)
            for j in range(7):
                Cr, Ci = CCs[:, 0:4], CCs[:, 4:8]
                p2r_l, p2i_l = pp["p2r"][:, lay, :], pp["p2i"][:, lay, :]
                Vr(lambda Cr=Cr, p2r_l=p2r_l: nv.tensor_tensor(c8a[:, 0:4], Cr, p2r_l, ALU.mult), inc=None)
                Vr(lambda Ci=Ci, p2i_l=p2i_l: nv.tensor_tensor(c8b[:, 0:4], Ci, p2i_l, ALU.mult), inc=None)
                Vr(lambda: nv.tensor_tensor(c8a[:, 0:4], c8a[:, 0:4], c8b[:, 0:4], ALU.subtract), inc=None)
                Vr(lambda j=j: nv.tensor_tensor(c8a[:, 0:4], c8a[:, 0:4], EA[:, j, 0:4], ALU.add), inc=None)
                Vr(lambda Cr=Cr, p2i_l=p2i_l: nv.tensor_tensor(c8b[:, 0:4], Cr, p2i_l, ALU.mult), inc=None)
                Vr(lambda Ci=Ci, p2r_l=p2r_l: nv.tensor_tensor(c8b[:, 4:8], Ci, p2r_l, ALU.mult), inc=None)
                Vr(lambda: nv.tensor_tensor(c8a[:, 4:8], c8b[:, 0:4], c8b[:, 4:8], ALU.add), inc=None)
                Vr(lambda j=j: nv.tensor_tensor(c8a[:, 4:8], c8a[:, 4:8], EA[:, j, 4:8], ALU.add), inc=None)
                Vr(lambda: nv.tensor_tensor(c8b[:, 0:8], c8a[:, 0:8], CCs[:, 0:8], ALU.subtract), inc=None)
                Vr(lambda j=j: nv.scalar_tensor_tensor(
                    CCs[:, 0:8], c8b[:, 0:8], selS[:, j:j + 1], CCs[:, 0:8], ALU.mult, ALU.add), inc=None)
            eRl, eIl = pp["eR"][:, lay, :], pp["eI"][:, lay, :]
            rhol = pp["rho"][:, lay, :]
            nsgnl = pp["nsgn"][:, lay, :]
            Vr(lambda eRl=eRl: nv.tensor_tensor(WIs[:, 0:4], CCs[:, 0:4], eRl, ALU.mult), inc=None)
            Vr(lambda eIl=eIl: nv.tensor_tensor(c8a[:, 0:4], CCs[:, 4:8], eIl, ALU.mult), inc=None)
            Vr(lambda: nv.tensor_tensor(WIs[:, 0:4], WIs[:, 0:4], c8a[:, 0:4], ALU.subtract), inc=None)
            Vr(lambda rhol=rhol: nv.tensor_tensor(WIs[:, 0:4], WIs[:, 0:4], rhol, ALU.mult), inc=None)
            Vr(lambda eIl=eIl: nv.tensor_tensor(WIs[:, 4:8], CCs[:, 0:4], eIl, ALU.mult), inc=None)
            Vr(lambda eRl=eRl: nv.tensor_tensor(c8a[:, 4:8], CCs[:, 4:8], eRl, ALU.mult), inc=None)
            Vr(lambda: nv.tensor_tensor(WIs[:, 4:8], WIs[:, 4:8], c8a[:, 4:8], ALU.add), inc=None)
            Vr(lambda rhol=rhol: nv.tensor_tensor(WIs[:, 4:8], WIs[:, 4:8], rhol, ALU.mult), inc=None)
            Vr(lambda nsgnl=nsgnl: nv.tensor_tensor(WIs[:, 4:8], WIs[:, 4:8], nsgnl, ALU.mult))

            for g in range(GD):
                if g > 0:
                    build_ptab(lay, g, with_rho=True)
                Vr(lambda g=g: nv.scalar_tensor_tensor(
                    y[:, g, :], tabs[:, 0, :], WIs[:, g:g + 1], y[:, g, :],
                    ALU.mult, ALU.add), inc=None)
                Vr(lambda g=g: nv.scalar_tensor_tensor(
                    y[:, g, :], tabs[:, 1, :], WIs[:, 4 + g:5 + g], y[:, g, :],
                    ALU.mult, ALU.add))
            if lay == 0 and "conv0" in taps:
                p.wait("gpsimd", "V")
                tap_out("conv0", yf)

            for g in range(GD):
                Vr(lambda g=g, lay=lay: nv.scalar_tensor_tensor(
                    h[:, g, :], y[:, g, :], prm["scl"][:, lay, g:g + 1], h[:, g, :],
                    ALU.mult, ALU.add), inc=None)
            Vr(lambda: nv.tensor_copy(c8a[:, 0:1], CCs[:, 0:1]))

            mark(f"L{lay}_scan")
            s_ln2 = inner_ln(lay, "ln2w", "ln2b", AF.Copy)
            mark(f"L{lay}_ln2")

            evac_s = [0, 0, 0]
            mm2_t = 0
            hupd_v = p.cnt["V"]
            sring.wait("tensor", w1_ld)
            p.wait("tensor", s_ln2[0], s_ln2[1])
            for tok in range(NTOK):
                p.wait("tensor", "V", hupd_v)

                def emit_mm2(fk, sl, pb, wl):
                    sring.wait("tensor", wl)
                    p.wait("tensor", "S", evac_s[pb])
                    for dm in range(GD):
                        Tr(lambda fk=fk, sl=sl, dm=dm, pb=pb: ntn.matmul(
                            ps_a[:, dm, :], w2b[:, sl, dm * 128:(dm + 1) * 128],
                            y1[:, pb, :],
                            start=(fk == 0), stop=(fk == KF - 1)),
                           inc=("T" if dm == GD - 1 else None))
                    w2marks[:] = [w2marks[1], p.cnt["T"]]

                pend = None
                for fk in range(KF):
                    sl = w2i % 3
                    w2i += 1
                    p.wait("sync", "T", w2marks[0])
                    wl = sring.load(w2b[:, sl, :],
                                    dd["w2T"][lay][fk * 128:(fk + 1) * 128, :])
                    pb = fk % 2
                    p.wait("tensor", "S", evac_s[pb])
                    for kt in range(GD):
                        Tr(lambda lay=lay, fk=fk, kt=kt, pb=pb, tok=tok: ntn.matmul(
                            ps_b[:, pb, :], w1t[:, kt, fk * 128:(fk + 1) * 128],
                            y[:, kt, tok * TT:(tok + 1) * TT],
                            start=(kt == 0), stop=(kt == GD - 1)), inc=None)
                    mm1_t = mm_fence(ps_b[0:1, pb, 0:1])
                    p.wait("scalar", "T", mm1_t)
                    evac_s[pb] = Sc(lambda lay=lay, fk=fk, pb=pb: ns.activation(
                        y1[:, pb, :], ps_b[:, pb, :], AF.Silu,
                        bias=b1S[:, lay, fk:fk + 1], scale=1.0))
                    if pend is not None:
                        emit_mm2(*pend)
                    pend = (fk, sl, pb, wl)
                emit_mm2(*pend)
                mm2_t = mm_fence(ps_a[0:1, 0, 0:1])
                p.wait("vector", "T", mm2_t)
                for dm in range(GD):
                    hsl = h[:, dm, tok * TT:(tok + 1) * TT]
                    Vr(lambda dm=dm, lay=lay, hsl=hsl: nv.scalar_tensor_tensor(
                        hsl, ps_a[:, dm, :], b2S[:, lay, dm:dm + 1], hsl,
                        ALU.add, ALU.add), inc=None)
                Vr(lambda: nv.tensor_copy(c8a[:, 0:1], CCs[:, 0:1]))
                hupd_v = p.cnt["V"]
            prev_mm1_t = p.cnt["T"]
            mark(f"L{lay}_ffn")

            if lay == 0 and "h1" in taps:
                p.wait("gpsimd", "V", hupd_v)
                tap_out("h1", hf)

        if stage >= 2 and "hfin" in taps:
            p.wait("gpsimd", "V")
            tap_out("hfin", hf)

        if stage >= 3:
            def final_ln():
                nonlocal stfree_v
                p.wait("scalar", "V")
                p.wait("scalar", "T", prev_mm1_t if stage >= 2 else 0)
                sq_marks = []
                for tok in range(NTOK):
                    sq_marks.append(Sc(lambda tok=tok: ns.activation(
                        y[:, :, tok * TT:(tok + 1) * TT],
                        h[:, :, tok * TT:(tok + 1) * TT], AF.Square)))
                p.wait("tensor", "V", p.cnt["V"])
                for tok in range(NTOK):
                    p.wait("tensor", "V", stfree_v)
                    for kt in range(GD):
                        Tr(lambda kt=kt, tok=tok: ntn.matmul(
                            ps_st[:, 0, :], onec[:], h[:, kt, tok * TT:(tok + 1) * TT],
                            start=(kt == 0), stop=(kt == GD - 1)), inc=None)
                    p.wait("tensor", "S", sq_marks[tok])
                    for kt in range(GD):
                        Tr(lambda kt=kt, tok=tok: ntn.matmul(
                            ps_st[:, 1, :], onec[:], y[:, kt, tok * TT:(tok + 1) * TT],
                            start=(kt == 0), stop=(kt == GD - 1)),
                           inc=("T" if kt == GD - 1 else None))
                    bc_v, stfree_v = ln_stats_and_bcast(tok, float(D))
                rb = bcst[:, 0:1, :].broadcast_to([128, GD, TC])
                mbb = bcst[:, 1:2, :].broadcast_to([128, GD, TC])
                Vr(lambda: nv.tensor_tensor(yf, hf, rb, ALU.mult), inc=None)
                Vr(lambda: nv.tensor_tensor(yf, yf, mbb, ALU.subtract))
                for g in range(GD):
                    bb = lnlbS[:, g:g + 1].broadcast_to([128, TC])
                    Vr(lambda g=g, bb=bb: nv.scalar_tensor_tensor(
                        y[:, g, :], y[:, g, :], lnlwS[:, g:g + 1], bb,
                        ALU.mult, ALU.add))

            final_ln()
            mark("P4_ln")
            if "hln" in taps:
                p.wait("gpsimd", "V")
                tap_out("hln", yf)

            def x1sb(k):
                buf = h if k < 16 else w1t
                kk = k % 16
                return buf[:, kk // 4, (kk % 4) * TT:(kk % 4 + 1) * TT]

            hln_v = p.cnt["V"]
            p.wait("sync", "T", p.cnt["T"])
            wout_mark = [hln_v] * 4
            wt_marks = [0, 0]
            xe_mark = [0, 0, 0]
            sq2_marks = [0, 0, 0]
            stfree2 = 0
            ab_marks = [0, 0]
            rs_evac = 0
            a_epi_v = 0
            a_epi_mt = [0, 0, 0, 0]
            rsA_gd = 0
            a_out_mark = [None, None]
            ai = 0
            for tok in range(NTOK):
                sl_t = slice(tok * TT, (tok + 1) * TT)
                p.wait("tensor", "V", stfree_v)
                for vm in range(KV):
                    sl = vm % 3
                    pa = vm % 4
                    p.wait("sync", "T", wt_marks[0])
                    wl = sring.load(wtb[:, sl, :],
                                    dd["woutT"][:, vm * 128:(vm + 1) * 128].rearrange(
                                        "(kt q) m -> q kt m", q=128))
                    p.wait("gpsimd", "V", xe_mark[sl])
                    xl = ering.load(xeb[:, sl, :],
                                    dd["xT"][vm * 128:(vm + 1) * 128, sl_t])
                    sring.wait("tensor", wl)
                    p.wait("tensor", "V", wout_mark[pa])
                    for kt in range(GD):
                        Tr(lambda vm=vm, sl=sl, kt=kt, pa=pa, sl_t=sl_t: ntn.matmul(
                            ps_a[:, pa, :], wtb[:, sl, kt * 128:(kt + 1) * 128],
                            y[:, kt, sl_t],
                            start=(kt == 0), stop=(kt == GD - 1)), inc=None)
                    wt_marks = [wt_marks[1], mm_fence(ps_a[0:1, pa, 0:1])]
                    x1t = x1sb(vm)
                    p.wait("vector", "T")
                    ering.wait("vector", xl)
                    Vr(lambda vm=vm, sl=sl, pa=pa, x1t=x1t: nv.scalar_tensor_tensor(
                        x1t, ps_a[:, pa, :], boutS[:, vm:vm + 1], xeb[:, sl, :],
                        ALU.add, ALU.add))
                    epi_v = p.cnt["V"]
                    wout_mark[pa] = epi_v
                    xe_mark[sl] = epi_v
                    su = vm % 3
                    p.wait("scalar", "V", epi_v)
                    p.wait("scalar", "T", sq2_marks[su])
                    sqs = Sc(lambda su=su, x1t=x1t: ns.activation(
                        u[:, 0, su * TT:(su + 1) * TT], x1t, AF.Square))
                    p.wait("tensor", "S", sqs)
                    if vm == 0:
                        p.wait("tensor", "V", stfree2)
                    Tr(lambda vm=vm, x1t=x1t: ntn.matmul(
                        ps_st[:, 0, :], onec[:], x1t,
                        start=(vm == 0), stop=(vm == KV - 1)), inc=None)
                    sq2_marks[su] = Tr(lambda vm=vm, su=su: ntn.matmul(
                        ps_st[:, 1, :], onec[:], u[:, 0, su * TT:(su + 1) * TT],
                        start=(vm == 0), stop=(vm == KV - 1)), inc="T")
                x1done_v = p.cnt["V"]
                mm_fence(ps_st[0:1, 0, 0:1])
                bc_v, stfree2 = ln_stats_and_bcast(tok, float(V))

                if "x1" in taps:
                    p.wait("gpsimd", "V", bc_v)
                    for vm in range(KV):
                        gp_load(d_tap["x1"][vm * 128:(vm + 1) * 128, sl_t], x1sb(vm))
                    p.wait("vector", "GDm")

                if stage < 4:
                    continue
                for mb in range(8):
                    if mb == 0:
                        p.wait("tensor", "V", bc_v if tok == 0 else x1done_v)
                    else:
                        for mt in range(4):
                            p.wait("tensor", "V", a_epi_mt[mt])
                    for kb in range(KV // 4):
                        j = ai % 2
                        ai += 1
                        p.wait("sync", "T", ab_marks[0])
                        al = sring.load(
                            ab4[:, j],
                            dd["aT"][kb * 512:(kb + 1) * 512,
                                     mb * TT:(mb + 1) * TT].rearrange(
                                         "(c q) m -> q c m", q=128))
                        sring.wait("tensor", al)
                        if tok == 0 and kb == 0:
                            p.wait("tensor", "V", rs_evac)
                        for c in range(4):
                            k = kb * 4 + c
                            for mt in range(4):
                                Tr(lambda j=j, c=c, mt=mt, k=k: ntn.matmul(
                                    ps_a[:, mt, :], ab4[:, j, c, mt * 128:(mt + 1) * 128],
                                    x1sb(k),
                                    start=(k == 0), stop=(k == KV - 1)),
                                   inc=(None if (mt < 3 or tok == 0 or c < 3) else "T"))
                            if tok == 0:
                                Tr(lambda j=j, c=c, k=k: ntn.matmul(
                                    ps_b[0:1, 0, :], nonec[:], ab4[:, j, c, :],
                                    start=(k == 0), stop=(k == KV - 1)),
                                   inc=("T" if c == 3 else None))
                        ab_marks = ab_marks[1:] + [p.cnt["T"]]
                    mm_t = mm_fence(ps_a[0:1, 0, 0:1])
                    ab_marks = ab_marks[1:] + [mm_t]
                    if tok == 0:
                        p.wait("vector", "T", mm_t)
                        p.wait("vector", "GDm")
                        rs_evac = Vr(lambda: nv.tensor_copy(rowb[:, 1, :], ps_b[0:1, 0, :]))
                        p.wait("gpsimd", "V", rs_evac)
                        gb1 = gp_load(bounce.ap()[0:1, mb * TT:(mb + 1) * TT], rowb[0:1, 1, :])
                        p.wait("gpsimd", "GDm", gb1)
                        rsA_gd = gp_load(
                            rsAS[:, mb * 4:(mb + 1) * 4],
                            bounce.ap()[0, mb * TT:(mb + 1) * TT].rearrange("(g q) -> q g", q=128),
                            slow=True)
                    p.wait("vector", "T", mm_t)
                    if tok == 0:
                        p.wait("vector", "GDm", rsA_gd)
                    for mt in range(4):
                        vm = mb * 4 + mt
                        sl2 = mt % 2
                        esl = u[:, 1, sl2 * TT:(sl2 + 1) * TT]
                        if a_out_mark[sl2] is not None:
                            p.wait("vector", a_out_mark[sl2][0], a_out_mark[sl2][1])
                        Vr(lambda mt=mt, esl=esl, sl_t=sl_t: nv.tensor_tensor(
                            esl, ps_a[:, mt, :], bcst[:, 0, sl_t], ALU.mult), inc=None)
                        a_epi_mt[mt] = p.cnt["V"]
                        Vr(lambda vm=vm, esl=esl, sl_t=sl_t: nv.scalar_tensor_tensor(
                            esl, bcst[:, 1, sl_t], rsAS[:, vm:vm + 1], esl,
                            ALU.mult, ALU.add), inc=None)
                        Vr(lambda vm=vm, esl=esl: nv.scalar_tensor_tensor(
                            esl, esl, bAS[:, vm:vm + 1], x1sb(vm),
                            ALU.add, ALU.add))
                        ep_v = p.cnt["V"]
                        p.wait("scalar", "V", ep_v)
                        onm = "O0" if sl2 % 2 == 0 else "O1"
                        p.wait("scalar", onm, p.cnt[onm])
                        p.run("scalar", lambda vm=vm, esl=esl, sl_t=sl_t:
                              nc.scalar.dma_start(out=outT[vm * 128:(vm + 1) * 128, sl_t], in_=esl),
                              onm, 16)
                        a_out_mark[sl2] = (onm, p.cnt[onm])
                    a_epi_v = p.cnt["V"]
                wout_mark = [a_epi_v] * 4
            mark("P4_wout")

        if stage >= 4:
            mark("P5_A")
        p.wait("sync", "O0", p.cnt["O0"])
        p.wait("sync", "O1", p.cnt["O1"])
        p.wait("sync", "GDm", p.cnt["GDm"])
        p.emit()
    return nc, d_tap


def kernel(**inputs):
    in_maps = stage_inputs(inputs)
    nc, _ = build_program(taps=DEBUG_TAPS, stage=STAGE)
    res = run_bass_kernel_spmd(nc, in_maps, core_ids=list(range(NCORES)))
    out = np.empty((B, L, V), dtype=np.float32)
    for c in range(NCORES):
        b, q = c // QPER, c % QPER
        out[b, q * TC:(q + 1) * TC, :] = res.results[c]["outT"].T
    return out
